# version 04
# baseline (speedup 1.0000x reference)
import numpy as np
import ml_dtypes
import concourse.bass as bass
import concourse.mybir as mybir
from concourse.bass_utils import run_bass_kernel_spmd
from concourse.alu_op_type import AluOpType as AL
from contextlib import ExitStack

F32 = mybir.dt.float32
BF16 = mybir.dt.bfloat16
AF = mybir.ActivationFunctionType
AX = mybir.AxisListType
BF = ml_dtypes.bfloat16

NCORES = 8
TOK = 2048
D = 1024
NCH = D // 128
B, T, H, HD = 4, 4096, 16, 64
NKB = T // 128
EPS = 1e-5
PRUNE_EXP = 93.0


def alibi_slopes(n=16):
    import math
    closest = 2 ** math.floor(math.log2(n))
    base = 2 ** (-2 ** (-(math.log2(closest) - 3)))
    slopes = base ** np.arange(1, closest + 1)
    if closest != n:
        eb = 2 ** (-2 ** (-(math.log2(2 * closest) - 3)))
        extra = eb ** np.arange(1, 2 * (n - closest) + 1, 2)
        slopes = np.concatenate([slopes, extra])
    return slopes.astype(np.float64)



class Prog:
    KEYS = ("t", "v", "s", "g", "S", "G", "A")
    ENG = {"t": "tensor", "v": "vector", "s": "scalar", "g": "gpsimd",
           "S": "sync", "G": "gpsimd", "A": "scalar"}
    DMA = {"S", "G", "A"}

    def __init__(self):
        self.nc = bass.Bass("TRN2", target_bir_lowering=False)
        self.order = {e: [] for e in ("tensor", "vector", "scalar", "gpsimd", "sync")}
        self.cnt = dict.fromkeys(self.KEYS, 0)
        self.engd = dict(self.ENG)
        self.dmaset = set(self.DMA)
        self.waited = {e: dict.fromkeys(self.KEYS, 0) for e in self.order}
        self.floor = {}

    def key(self, queue, name):
        if name in self.cnt:
            return name
        self.engd[name] = self.ENG[queue]
        self.dmaset.add(name)
        self.cnt[name] = 0
        for e in self.waited:
            self.waited[e][name] = 0
        return name

    def barrier(self, keys=("t", "v", "s", "g", "S", "G", "A")):
        for k in keys:
            if self.cnt[k] > self.floor.get(k, 0):
                self.floor[k] = self.cnt[k]

    def op(self, key, fn, waits=(), inc=True):
        eng = self.engd[key]
        w = {}
        for tokn in flat_toks(waits) + list(self.floor.items()):
            if tokn is None:
                continue
            k, v = tokn
            if v > self.waited[eng].get(k, 0) and w.get(k, 0) < v:
                w[k] = v
        for k, v in w.items():
            self.waited[eng][k] = v
        amt = (16 if key in self.dmaset else 1) if inc else 0
        self.cnt[key] += amt
        self.order[eng].append((tuple(w.items()), fn, key, amt))
        return (key, self.cnt[key])

    def finalize(self, es):
        nc = self.nc
        sems = {k: es.enter_context(nc.semaphore("sem_" + k)) for k in self.cnt}
        block = es.enter_context(nc.Block())

        def mk(eng):
            def body(h):
                for waits, fn, key, amt in self.order[eng]:
                    for k, v in waits:
                        h.wait_ge(sems[k], v)
                    inst = fn(h)
                    if amt:
                        inst.then_inc(sems[key], amt)
            return body

        for eng in self.order:
            if self.order[eng]:
                getattr(block, eng)(mk(eng))


def flat_toks(toks):
    out = []
    def add(t):
        if t is None:
            return
        if isinstance(t, tuple) and len(t) == 2 and isinstance(t[0], str):
            out.append(t)
            return
        for x in t:
            add(x)
    add(list(toks))
    return out


def last(*toks):
    return flat_toks(toks)


class Ctx(dict):
    pass


def setup_common(p, es, c, npool=6):
    nc = p.nc
    c["ones_col"] = es.enter_context(nc.sbuf_tensor("cm_ones_col", [128, 1], BF16))
    c["ones_row"] = es.enter_context(nc.sbuf_tensor("cm_ones_row", [1, 128], BF16))
    c["eps1"] = es.enter_context(nc.sbuf_tensor("cm_eps1", [1, 1], F32))
    c["ps"] = [es.enter_context(nc.psum_tensor(f"cm_ps{i}", [128, 512], F32))
               for i in range(npool)]
    c["ps_free"] = {}
    t0 = p.op("g", lambda g: g.memset(c["ones_col"][:], 1.0))
    t1 = p.op("g", lambda g: g.memset(c["ones_row"][:], 1.0))
    t2 = p.op("g", lambda g: g.memset(c["eps1"][:], EPS))
    c["ones_ready"] = [t0, t1, t2]


def load_tile(p, es, name, shape, dtype, src_ap, queue=None):
    t = es.enter_context(p.nc.sbuf_tensor(name, shape, dtype))
    if queue is None:
        queue = "S"
    k = p.key(queue, "setup_" + queue)
    tok = p.op(k, lambda h: h.dma_start(out=t[:], in_=src_ap))
    return t, tok


def setup_bar(p):
    return [(k, p.cnt[k]) for k in ("setup_S", "setup_G", "setup_A") if k in p.cnt]


def all_dma_bar(p):
    return [(k, p.cnt[k]) for k in p.cnt if k in p.dmaset and p.cnt[k] > 0]


def load_x(p, es, c, x_dram, name="x", dtype=F32, nch=NCH, tok=TOK):
    nc = p.nc
    q = "S" if dtype == F32 else "G"
    k = p.key(q, "ldx_" + name)
    tiles = []
    tok_last = None
    for cc in range(nch):
        t = es.enter_context(nc.sbuf_tensor(f"{name}{cc}", [128, tok], dtype))
        tiles.append(t)
        tok_last = p.op(k, lambda h, cc=cc, t=t: h.dma_start(
            out=t[:], in_=x_dram[cc * 128:(cc + 1) * 128, :]))
    return tiles, [tok_last] * nch



def ln_scratch(p, es, name, tok=TOK):
    nc = p.nc
    sc = {}
    for nm, shape, dt in [("xb", [128, 512], BF16), ("xb2", [128, 512], BF16),
                          ("rowA", [1, tok], F32), ("rowB", [1, tok], F32),
                          ("rowC", [1, tok], F32),
                          ("rstdb", [1, tok], BF16), ("mrsb", [1, tok], BF16),
                          ("rbc", [128, tok], BF16), ("mbc", [128, tok], BF16)]:
        sc[nm] = es.enter_context(nc.sbuf_tensor(f"{name}_{nm}", shape, dt))
    sc["busy"] = None
    return sc


def emit_ln(p, es, c, name, x_tiles, x_toks, out_tiles, w_ap=None, b_ap=None,
            tok=TOK, param_toks=(), sc=None, t1_tiles=None, t1_toks=None):
    nc = p.nc
    ntt = tok // 512
    if sc is None:
        sc = ln_scratch(p, es, name, tok)
    if t1_tiles is None:
        t1_tiles = x_tiles
        t1_toks = [None] * NCH
    ps_s = c["ps"][2][0:1, :]
    ps_b = c["ps"][3]
    pf = c["ps_free"]

    s1_toks, s2_toks = [], []
    ev = sc["busy"]
    cast_done = {}
    for which, func, row in ((0, AF.Copy, "rowA"), (1, AF.Square, "rowB")):
        for tt in range(ntt):
            sl = slice(tt * 512, (tt + 1) * 512)
            m = None
            for cc in range(NCH):
                xb = sc["xb"] if (tt * NCH + cc) % 2 == 0 else sc["xb2"]
                ct = p.op("s", lambda s, cc=cc, sl=sl, xb=xb, func=func: s.activation(
                    out=xb[:], in_=x_tiles[cc][:, sl], func=func),
                    waits=last([x_toks[cc]], [cast_done.get(id(xb))]))
                m = p.op("t", lambda t, xb=xb, cc=cc: t.matmul(
                    ps_s, c["ones_col"][:], xb[:],
                    start=(cc == 0), stop=(cc == NCH - 1)),
                    waits=last(c["ones_ready"], [ct], [ev], [pf.get(2)]), inc=True)
                cast_done[id(xb)] = m
            ev = p.op("v", lambda v, sl=sl, row=row: v.tensor_copy(
                sc[row][:, sl], ps_s), waits=last([m], [sc["busy"]]))
            pf[2] = ev
            (s1_toks if which == 0 else s2_toks).append(ev)

    t_mu = p.op("v", lambda v: v.tensor_scalar(out=sc["rowA"][:], in0=sc["rowA"][:],
                scalar1=1.0 / D, scalar2=None, op0=AL.mult), waits=s1_toks)
    t_msq = p.op("v", lambda v: v.tensor_tensor(out=sc["rowC"][:], in0=sc["rowA"][:],
                 in1=sc["rowA"][:], op=AL.mult), waits=last([t_mu], [sc["busy"]]))
    t_var = p.op("v", lambda v: v.scalar_tensor_tensor(out=sc["rowB"][:], in0=sc["rowB"][:],
                 scalar=1.0 / D, in1=sc["rowC"][:], op0=AL.mult, op1=AL.subtract),
                 waits=last(s2_toks, [t_msq]))
    t_std = p.op("s", lambda s: s.activation(out=sc["rowC"][:], in_=sc["rowB"][:],
                 func=AF.Sqrt, bias=c["eps1"][:]),
                 waits=last([t_var], c["ones_ready"]))
    t_rstd = p.op("v", lambda v: v.reciprocal(sc["rowB"][:], sc["rowC"][:]), waits=[t_std])
    t_rb = p.op("v", lambda v: v.tensor_copy(sc["rstdb"][:], sc["rowB"][:]), waits=[t_rstd])
    t_mrs = p.op("v", lambda v: v.tensor_tensor(out=sc["rowA"][:], in0=sc["rowA"][:],
                 in1=sc["rowB"][:], op=AL.mult), waits=[t_rstd])
    t_mb = p.op("v", lambda v: v.tensor_copy(sc["mrsb"][:], sc["rowA"][:]), waits=[t_mrs])

    bc_toks = []
    evp = None
    for tt in range(ntt):
        sl = slice(tt * 512, (tt + 1) * 512)
        m = p.op("t", lambda t, sl=sl: t.matmul(ps_b[:], c["ones_row"][:],
                 sc["rstdb"][:, sl], start=True, stop=True),
                 waits=last([t_rb], [evp], c["ones_ready"], [pf.get(3)]))
        evp = p.op("v", lambda v, sl=sl: v.tensor_copy(sc["rbc"][:, sl], ps_b[:]),
                   waits=last([m], [sc["busy"]]))
        m2 = p.op("t", lambda t, sl=sl: t.matmul(ps_b[:], c["ones_row"][:],
                  sc["mrsb"][:, sl], start=True, stop=True),
                  waits=last([t_mb], [evp]))
        evp = p.op("v", lambda v, sl=sl: v.tensor_copy(sc["mbc"][:, sl], ps_b[:]),
                   waits=[m2])
        pf[3] = evp
        bc_toks.append(evp)

    out_toks = []
    for cc in range(NCH):
        tt1 = p.op("v", lambda v, cc=cc: v.tensor_tensor(out=t1_tiles[cc][:],
                   in0=x_tiles[cc][:], in1=sc["rbc"][:], op=AL.mult),
                   waits=last(bc_toks, [x_toks[cc]], [t1_toks[cc]],
                              out_toks[-1:] if out_toks else []))
        if w_ap is not None:
            tt2 = p.op("v", lambda v, cc=cc: v.scalar_tensor_tensor(out=t1_tiles[cc][:],
                       in0=sc["mbc"][:], scalar=-1.0, in1=t1_tiles[cc][:],
                       op0=AL.mult, op1=AL.add), waits=[tt1])
            tt3 = p.op("v", lambda v, cc=cc: v.tensor_scalar(out=out_tiles[cc][:],
                       in0=t1_tiles[cc][:], scalar1=w_ap(cc), scalar2=b_ap(cc),
                       op0=AL.mult, op1=AL.add), waits=last([tt2], list(param_toks)))
            out_toks.append(tt3)
        else:
            tt2 = p.op("v", lambda v, cc=cc: v.scalar_tensor_tensor(out=out_tiles[cc][:],
                       in0=sc["mbc"][:], scalar=-1.0, in1=t1_tiles[cc][:],
                       op0=AL.mult, op1=AL.add), waits=[tt1])
            out_toks.append(tt2)
    sc["busy"] = out_toks[-1]
    return out_toks



def emit_proj_feat(p, es, c, name, xn_tiles, xn_toks, w_dram, n_out, evac,
                   tok=TOK, nch_in=NCH, extra_mm=None, ps_idx=(0, 1)):
    nc = p.nc
    ntt = tok // 512
    noc = n_out // 128
    wbufs = [es.enter_context(nc.sbuf_tensor(f"{name}_w{i}", [128, nch_in, 128], BF16))
             for i in range(2)]
    wk = [p.key("G", f"{name}_wk{i}") for i in range(2)]
    pss = [c["ps"][ps_idx[0]], c["ps"][ps_idx[1]]]
    wlast = [None, None]
    plast = [c["ps_free"].get(ps_idx[0]), c["ps_free"].get(ps_idx[1])]
    wsrc = w_dram.rearrange("(dc pp) o -> pp dc o", pp=128)
    toks = []
    for oc in range(noc):
        slot = oc % 2
        wt = p.op(wk[slot], lambda g, oc=oc, slot=slot: g.dma_start(
            out=wbufs[slot][:], in_=wsrc[:, :, oc * 128:(oc + 1) * 128]),
            waits=last([wlast[slot]]))
        mm_last = None
        for tt in range(ntt):
            psl = (oc * ntt + tt) % 2
            sl = slice(tt * 512, (tt + 1) * 512)
            m = None
            for dc in range(nch_in):
                stop = (dc == nch_in - 1) and extra_mm is None
                m = p.op("t", lambda t, dc=dc, sl=sl, slot=slot, psl=psl, stop=stop:
                         t.matmul(pss[psl][:], wbufs[slot][:, dc, :],
                                  xn_tiles[dc][:, sl], start=(dc == 0), stop=stop),
                         waits=last([wt], [xn_toks[dc]], [plast[psl]]),
                         inc=(dc == nch_in - 1 and extra_mm is None))
            if extra_mm is not None:
                m = extra_mm(p, oc, tt, pss[psl])
            ev = evac(oc, tt, pss[psl], m)
            plast[psl] = ev
            c["ps_free"][ps_idx[psl]] = ev
            toks.append(ev)
            mm_last = m
        wlast[slot] = mm_last
    return toks


def emit_proj_tok(p, es, c, name, xn_tiles, xn_toks, w_dram, n_out, evac,
                  bias_row=None, bias_tok=None, tok=TOK, nch_in=NCH, ps_idx=(4, 5)):
    nc = p.nc
    ntt = tok // 128
    nvc = n_out // 512
    wbufs = [es.enter_context(nc.sbuf_tensor(f"{name}_w{i}", [128, nch_in, 512], BF16))
             for i in range(2)]
    wk = [p.key("G", f"{name}_wk{i}") for i in range(2)]
    pss = [c["ps"][ps_idx[0]], c["ps"][ps_idx[1]]]
    wlast = [None, None]
    plast = [c["ps_free"].get(ps_idx[0]), c["ps_free"].get(ps_idx[1])]
    wsrc = w_dram.rearrange("(dc pp) o -> pp dc o", pp=128)
    toks = []
    for vc in range(nvc):
        slot = vc % 2
        wt = p.op(wk[slot], lambda g, vc=vc, slot=slot: g.dma_start(
            out=wbufs[slot][:], in_=wsrc[:, :, vc * 512:(vc + 1) * 512]),
            waits=last([wlast[slot]]))
        mm_last = None
        for tt in range(ntt):
            psl = (vc * ntt + tt) % 2
            tsl = slice(tt * 128, (tt + 1) * 128)
            m = None
            for dc in range(nch_in):
                is_last = (dc == nch_in - 1) and bias_row is None
                m = p.op("t", lambda t, dc=dc, tsl=tsl, slot=slot, psl=psl,
                         is_last=is_last:
                         t.matmul(pss[psl][:], xn_tiles[dc][:, tsl],
                                  wbufs[slot][:, dc, :],
                                  start=(dc == 0), stop=is_last),
                         waits=last([wt], [xn_toks[dc]], [plast[psl]]),
                         inc=is_last)
            if bias_row is not None:
                m = p.op("t", lambda t, vc=vc, psl=psl: t.matmul(
                    pss[psl][:], c["ones_row"][:],
                    bias_row[:, vc * 512:(vc + 1) * 512], start=False, stop=True),
                    waits=last([bias_tok], c["ones_ready"]))
            ev = evac(tt, vc, pss[psl], m)
            plast[psl] = ev
            c["ps_free"][ps_idx[psl]] = ev
            toks.append(ev)
            mm_last = m
        wlast[slot] = mm_last
    return toks



def emit_rope(p, es, c, name, strip, ready_toks, cosb, sinb, const_toks, aux, aux_free):
    done = []
    rk = p.key("S", "ropek")
    for rb in (0, 64):
        d1 = p.op(rk, lambda h, rb=rb: h.dma_start(
            out=aux[rb:rb + 16, :], in_=strip[rb + 16:rb + 32, :]),
            waits=last(ready_toks, [aux_free]))
        d2 = p.op(rk, lambda h, rb=rb: h.dma_start(
            out=aux[rb + 16:rb + 32, :], in_=strip[rb:rb + 16, :]),
            waits=last(ready_toks, [aux_free]))
        t1 = p.op("v", lambda v, rb=rb: v.tensor_tensor(out=aux[rb:rb + 32, :],
                  in0=aux[rb:rb + 32, :], in1=sinb[rb:rb + 32, :], op=AL.mult),
                  waits=last([d1], [d2], const_toks))
        t2 = p.op("v", lambda v, rb=rb: v.tensor_tensor(out=strip[rb:rb + 32, :],
                  in0=strip[rb:rb + 32, :], in1=cosb[rb:rb + 32, :], op=AL.mult),
                  waits=last(ready_toks, [d1], [d2]))
        t3 = p.op("v", lambda v, rb=rb: v.tensor_tensor(out=strip[rb:rb + 32, :],
                  in0=strip[rb:rb + 32, :], in1=aux[rb:rb + 32, :], op=AL.subtract),
                  waits=[t1, t2])
        aux_free = t3
        done.append(t3)
    return done, aux_free



def emit_qkv_part(p, es, c, name, xn_tiles, xn_toks, wiT, bqk_sb, bqk_tok,
                  bvb, bv_tok, qT, kT, vTok, rope=None, tok=TOK):
    nc = p.nc
    strips = [es.enter_context(nc.sbuf_tensor(f"{name}_qks{i}", [128, tok], BF16))
              for i in range(2)]
    sk = [p.key("S", f"{name}_sk{i}") for i in range(2)]
    strip_free = [None, None]
    state = {"aux_free": None, "per_oc": {}}

    def evac_qk(oc, tt, ps, mm_tok):
        srot = oc % 2
        sl = slice(tt * 512, (tt + 1) * 512)
        scale = 0.125 if oc < 8 else 1.0
        ev = p.op("s", lambda s, oc=oc, sl=sl, ps=ps, srot=srot, scale=scale:
                  s.activation(out=strips[srot][:, sl], in_=ps[:], func=AF.Identity,
                               bias=bqk_sb[:, oc:oc + 1], scale=scale),
                  waits=last([mm_tok], [bqk_tok], [strip_free[srot]]))
        state["per_oc"].setdefault(oc, []).append(ev)
        if tt == tok // 512 - 1:
            evs = state["per_oc"].pop(oc)
            rtoks = evs
            if rope is not None:
                rt, state["aux_free"] = emit_rope(
                    p, es, c, f"{name}_rope{oc}", strips[srot], evs,
                    rope["cosb"], rope["sinb"], rope["const_toks"],
                    rope["aux"], state["aux_free"])
                rtoks = rt + evs
            dst = qT if oc < 8 else kT
            r0 = (oc % 8) * 128
            dtok = p.op(sk[srot], lambda h, dst=dst, r0=r0, srot=srot: h.dma_start(
                out=dst[r0:r0 + 128, :], in_=strips[srot][:]), waits=rtoks)
            strip_free[srot] = dtok
        return ev

    emit_proj_feat(p, es, c, name + "_qk", xn_tiles, xn_toks, wiT[:, 0:2 * D], 2 * D,
                   evac_qk, tok=tok)

    vtiles = [es.enter_context(nc.sbuf_tensor(f"{name}_vt{i}", [128, 512], BF16))
              for i in range(2)]
    vk = [p.key("S", f"{name}_vk{i}") for i in range(2)]
    vfree = [None, None]

    def evac_v(tt, vc, ps, mm_tok):
        rot = (vc * (tok // 128) + tt) % 2
        ev = p.op("s", lambda s, ps=ps, rot=rot: s.activation(
            out=vtiles[rot][:], in_=ps[:], func=AF.Copy),
            waits=last([mm_tok], [vfree[rot]]))
        dtok = p.op(vk[rot], lambda h, tt=tt, vc=vc, rot=rot: h.dma_start(
            out=vTok[tt * 128:(tt + 1) * 128, vc * 512:(vc + 1) * 512],
            in_=vtiles[rot][:]), waits=[ev])
        vfree[rot] = dtok
        return ev

    emit_proj_tok(p, es, c, name + "_v", xn_tiles, xn_toks, wiT[:, 2 * D:3 * D], D,
                  evac_v, bias_row=bvb, bias_tok=bv_tok, tok=tok)


def load_rope(p, es, c, cosT, sinT, tok=TOK):
    cosb, cos_tok = load_tile(p, es, "sb_cosb", [128, tok], BF16, cosT, queue="G")
    sinb, sin_tok = load_tile(p, es, "sb_sinb", [128, tok], BF16, sinT, queue="G")
    aux = es.enter_context(p.nc.sbuf_tensor("ropeaux", [128, tok], BF16))
    return {"cosb": cosb, "sinb": sinb, "aux": aux, "const_toks": [cos_tok, sin_tok]}



def build_qkv(with_ln, with_rope, tok=TOK):
    p = Prog()
    nc = p.nc
    es = ExitStack()
    c = Ctx()
    xT = nc.dram_tensor("xT", [D, tok], F32, kind="ExternalInput").ap()
    wiT = nc.dram_tensor("wiT", [D, 3 * D], F32, kind="ExternalInput").ap()
    bqk = nc.dram_tensor("bqk", [128, 16], F32, kind="ExternalInput").ap()
    bv_row = nc.dram_tensor("bv_row", [1, D], F32, kind="ExternalInput").ap()
    if with_ln:
        lnp = nc.dram_tensor("lnp", [128, 16], F32, kind="ExternalInput").ap()
    if with_rope:
        cosT = nc.dram_tensor("cosT", [128, tok], F32, kind="ExternalInput").ap()
        sinT = nc.dram_tensor("sinT", [128, tok], F32, kind="ExternalInput").ap()
    qT = nc.dram_tensor("qT", [D, tok], BF16, kind="ExternalOutput").ap()
    kT = nc.dram_tensor("kT", [D, tok], BF16, kind="ExternalOutput").ap()
    vTok = nc.dram_tensor("vTok", [tok, D], BF16, kind="ExternalOutput").ap()

    with es:
        setup_common(p, es, c)
        x_tiles, x_toks = load_x(p, es, c, xT, "x", F32, NCH, tok)
        bqk_sb, _bt = load_tile(p, es, "sb_bqk", [128, 16], F32, bqk)
        bvb, _bv = load_tile(p, es, "sb_bvb", [1, D], BF16, bv_row, queue="G")
        rope = load_rope(p, es, c, cosT, sinT, tok) if with_rope else None
        if with_ln:
            lnp_sb, _lt = load_tile(p, es, "sb_lnp", [128, 16], F32, lnp)
        sbar = setup_bar(p)
        bqk_tok = p.op("v", lambda v: v.tensor_scalar(out=bqk_sb[:, 0:8],
                       in0=bqk_sb[:, 0:8], scalar1=0.125, scalar2=None,
                       op0=AL.mult), waits=sbar)
        bv_tok = sbar
        if rope is not None:
            rope["const_toks"] = sbar

        xn_tiles = [es.enter_context(nc.sbuf_tensor(f"xn{i}", [128, tok], BF16))
                    for i in range(NCH)]
        if with_ln:
            xn_toks = emit_ln(p, es, c, "ln1", x_tiles, x_toks, xn_tiles,
                              w_ap=lambda cc: lnp_sb[:, cc:cc + 1],
                              b_ap=lambda cc: lnp_sb[:, 8 + cc:8 + cc + 1],
                              tok=tok, param_toks=sbar)
        else:
            xn_toks = [p.op("s", lambda s, cc=cc: s.activation(
                out=xn_tiles[cc][:], in_=x_tiles[cc][:], func=AF.Copy),
                waits=[x_toks[cc]]) for cc in range(NCH)]

        emit_qkv_part(p, es, c, "qkv", xn_tiles, xn_toks, wiT, bqk_sb, bqk_tok,
                      bvb, bv_tok, qT, kT, vTok, rope=rope, tok=tok)
        p.op("S", lambda h: h.nop(), waits=all_dma_bar(p), inc=False)
        p.finalize(es)
    return p



def build_attn(tok_q=T, nunits=8):
    p = Prog()
    nc = p.nc
    es = ExitStack()
    c = Ctx()
    nu = nunits
    QA = nc.dram_tensor("QA", [nu, 65, T], BF16, kind="ExternalInput").ap()
    KA = nc.dram_tensor("KA", [nu, 65, T], BF16, kind="ExternalInput").ap()
    VA = nc.dram_tensor("VA", [nu, T, 65], BF16, kind="ExternalInput").ap()
    AB = nc.dram_tensor("AB", [nu, 128, 32], F32, kind="ExternalInput").ap()
    OT = nc.dram_tensor("OT", [nu, 64, T], BF16, kind="ExternalOutput").ap()

    NQT = T // 512

    with es:
        setup_common(p, es, c, npool=0)
        qa = [es.enter_context(nc.sbuf_tensor(f"qa{i}", [65, T], BF16)) for i in range(2)]
        ka = [es.enter_context(nc.sbuf_tensor(f"ka{i}", [65, T], BF16)) for i in range(2)]
        va = [es.enter_context(nc.sbuf_tensor(f"va{i}", [128, NKB, 65], BF16)) for i in range(2)]
        ab = [es.enter_context(nc.sbuf_tensor(f"ab{i}", [128, 32], F32)) for i in range(2)]
        ot = [es.enter_context(nc.sbuf_tensor(f"ot{i}", [64, T], BF16)) for i in range(2)]
        sq = es.enter_context(nc.sbuf_tensor("sq", [64, 512], BF16))
        nrm = es.enter_context(nc.sbuf_tensor("nrm", [1, 2 * T], F32))
        nb2 = es.enter_context(nc.sbuf_tensor("nb2", [1, 2], F32))
        mgsum = es.enter_context(nc.sbuf_tensor("mgsum", [1, 1], F32))
        mgb = es.enter_context(nc.sbuf_tensor("mgb", [1, 1], BF16))
        biasu = [es.enter_context(nc.sbuf_tensor(f"biasu{i}", [128, 32], F32))
                 for i in range(2)]
        psb = [es.enter_context(nc.sbuf_tensor(f"psb{i}", [128, 512], BF16)) for i in range(3)]
        rcp = es.enter_context(nc.sbuf_tensor("rcp", [1, 512], F32))
        rcpb = es.enter_context(nc.sbuf_tensor("rcpb", [1, 512], BF16))
        rbsb = es.enter_context(nc.sbuf_tensor("rbsb", [64, 512], BF16))

        ps_out = [es.enter_context(nc.psum_tensor(f"pso{i}", [65, 512], F32)) for i in range(2)]
        ps_s = [es.enter_context(nc.psum_tensor(f"pss{i}", [128, 512], F32)) for i in range(2)]
        ps_n = es.enter_context(nc.psum_tensor("psn", [1, 512], F32))
        ps_mg = es.enter_context(nc.psum_tensor("psmg", [128, 1], F32))
        ps_rb = es.enter_context(nc.psum_tensor("psrb", [64, 512], F32))

        in_free = [[None, None] for _ in range(4)]
        ot_free = [None, None]
        psb_free = [None] * 3
        pso_free = [None, None]
        pss_free = [None, None]
        bias_free = [None, None]
        last_tbias = None
        last_tmg = None

        otk = [p.key("S", f"otk{i}") for i in range(2)]
        qak = [p.key("S", f"qak{i}") for i in range(2)]
        kak = [p.key("S", f"kak{i}") for i in range(2)]
        vak = [p.key("G", f"vak{i}") for i in range(2)]
        abk = [p.key("G", f"abk{i}") for i in range(2)]
        for u in range(nu):
            rot = u % 2
            tqa = p.op(qak[rot], lambda h, u=u, rot=rot: h.dma_start(
                out=qa[rot][:], in_=QA[u]), waits=last([in_free[0][rot]]))
            tka = p.op(kak[rot], lambda h, u=u, rot=rot: h.dma_start(
                out=ka[rot][:], in_=KA[u]), waits=last([in_free[1][rot]]))
            tva = p.op(vak[rot], lambda g, u=u, rot=rot: g.dma_start(
                out=va[rot][:], in_=VA[u].rearrange("(kb pp) e -> pp kb e", pp=128)),
                waits=last([in_free[2][rot]]))
            tab = p.op(abk[rot], lambda g, u=u, rot=rot: g.dma_start(
                out=ab[rot][:], in_=AB[u]), waits=last([in_free[3][rot]]))

            prev_sqmm = None
            nsegs = T // 512
            for which, src, tsrc in ((0, qa[rot], tqa), (1, ka[rot], tka)):
                ev = None
                for seg in range(nsegs):
                    sl = slice(seg * 512, (seg + 1) * 512)
                    tsq = p.op("s", lambda s, src=src, sl=sl: s.activation(
                        out=sq[:], in_=src[0:64, sl], func=AF.Square),
                        waits=last([tsrc], [prev_sqmm]))
                    m = p.op("t", lambda t: t.matmul(ps_n[:], c["ones_col"][0:64, :],
                             sq[:], start=True, stop=True),
                             waits=last([tsq], c["ones_ready"], [ev]))
                    prev_sqmm = m
                    ev = p.op("v", lambda v, which=which, seg=seg: v.tensor_copy(
                        nrm[:, (which * nsegs + seg) * 512:(which * nsegs + seg + 1) * 512],
                        ps_n[:]), waits=[m])
                p.op("v", lambda v, which=which: v.reduce_max(
                    nb2[:, which:which + 1],
                    nrm[:, which * nsegs * 512:(which + 1) * nsegs * 512],
                    axis=AX.X), waits=[ev])
            tadd = p.op("v", lambda v: v.tensor_tensor(out=mgsum[:], in0=nb2[:, 0:1],
                        in1=nb2[:, 1:2], op=AL.mult), waits=[("v", p.cnt["v"])])
            tln = p.op("s", lambda s: s.activation(out=mgsum[:], in_=mgsum[:],
                       func=AF.Ln), waits=[tadd])
            texp = p.op("s", lambda s: s.activation(out=mgsum[:], in_=mgsum[:],
                        func=AF.Exp, scale=0.5), waits=[tln])
            tmgb = p.op("v", lambda v: v.tensor_copy(mgb[:], mgsum[:]),
                        waits=last([texp], [last_tmg]))
            tmg = p.op("t", lambda t: t.matmul(ps_mg[:], c["ones_row"][:], mgb[:],
                       start=True, stop=True),
                       waits=last([tmgb], c["ones_ready"], [last_tbias]))
            last_tmg = tmg
            tbias = p.op("v", lambda v, rot=rot: v.tensor_scalar(out=biasu[rot][:],
                         in0=ab[rot][:], scalar1=ps_mg[:], scalar2=None,
                         op0=AL.subtract),
                         waits=last([tmg], [tab], [bias_free[rot]]))
            last_tbias = tbias
            in_free[3][rot] = tbias

            strip_toks = []
            for qt in range(NQT):
                orot = qt % 2
                qsl = slice(qt * 512, (qt + 1) * 512)
                nkb = 4 * qt + 4
                av_tok = None
                for kb in range(nkb):
                    srot = kb % 2
                    ksl = slice(kb * 128, (kb + 1) * 128)
                    ms = p.op("t", lambda t, rot=rot, ksl=ksl, qsl=qsl, srot=srot:
                              t.matmul(ps_s[srot][:], ka[rot][:, ksl], qa[rot][:, qsl],
                                       start=True, stop=True),
                              waits=last([tka], [tqa], [pss_free[srot]]))
                    prot = kb % 3
                    te = p.op("s", lambda s, srot=srot, prot=prot, kb=kb, rot=rot:
                              s.activation(out=psb[prot][:], in_=ps_s[srot][:],
                                           func=AF.Exp, bias=biasu[rot][:, kb:kb + 1]),
                              waits=last([ms], [tbias], [psb_free[prot]]))
                    pss_free[srot] = te
                    ptok = te
                    if kb >= 4 * qt:
                        ptok = p.op("g", lambda g, prot=prot, qt=qt, kb=kb:
                                    g.affine_select(out=psb[prot][:], in_=psb[prot][:],
                                                    pattern=[[1, 512]],
                                                    compare_op=AL.is_ge, fill=0.0,
                                                    base=qt * 512 - kb * 128,
                                                    channel_multiplier=-1),
                                    waits=[te])
                    av_tok = p.op("t", lambda t, rot=rot, kb=kb, orot=orot, prot=prot,
                                  nkb=nkb: t.matmul(
                                      ps_out[orot][:],
                                      va[rot][:, kb, :],
                                      psb[prot][:], start=(kb == 0), stop=(kb == nkb - 1)),
                                  waits=last([ptok], [tva], [pso_free[orot]]),
                                  inc=(kb == nkb - 1))
                    psb_free[prot] = av_tok
                trc = p.op("v", lambda v, orot=orot: v.reciprocal(
                    rcp[:], ps_out[orot][64:65, :]), waits=[av_tok])
                trcb = p.op("v", lambda v: v.tensor_copy(rcpb[:], rcp[:]), waits=[trc])
                tmb = p.op("t", lambda t: t.matmul(ps_rb[:], c["ones_row"][:, 0:64],
                           rcpb[:], start=True, stop=True),
                           waits=last([trcb], c["ones_ready"], [("v", p.cnt["v"])]))
                trb = p.op("v", lambda v: v.tensor_copy(rbsb[:], ps_rb[:]), waits=[tmb])
                tst = p.op("v", lambda v, orot=orot, qsl=qsl, rot=rot:
                           v.scalar_tensor_tensor(out=ot[rot][:, qsl],
                                                  in0=ps_out[orot][0:64, :], scalar=1.0,
                                                  in1=rbsb[:], op0=AL.mult, op1=AL.mult),
                           waits=last([trb], [ot_free[rot]]))
                pso_free[orot] = tst
                strip_toks.append(tst)
            dout = p.op(otk[rot], lambda h, u=u, rot=rot: h.dma_start(
                out=OT[u], in_=ot[rot][:]), waits=strip_toks)
            ot_free[rot] = dout
            in_free[0][rot] = av_tok
            in_free[1][rot] = av_tok
            in_free[2][rot] = av_tok
            bias_free[rot] = av_tok

        p.op("S", lambda h: h.nop(), waits=all_dma_bar(p), inc=False)
        p.finalize(es)
    return p



def emit_ffn(p, es, c, name, xn_tiles, xn_toks, f1T, b1_sb, b1_tok, f2T,
             b2_sb, b2_tok, x_tiles, x_toks, tok=TOK):
    nc = p.nc
    NH = 32
    TTH = 512
    h_sb = es.enter_context(nc.sbuf_tensor(name + "_h", [128, NH, TTH], BF16))
    w1b = [es.enter_context(nc.sbuf_tensor(f"{name}_w1{i}", [128, NCH, 128], BF16))
           for i in range(2)]
    w2b = [es.enter_context(nc.sbuf_tensor(f"{name}_w2{i}", [128, NH, 128], BF16))
           for i in range(2)]
    w1k = [p.key("G", f"{name}_w1k{i}") for i in range(2)]
    w2k = [p.key("G", f"{name}_w2k{i}") for i in range(2)]
    pss = [c["ps"][0], c["ps"][1]]
    f1src = f1T.rearrange("(dc pp) o -> pp dc o", pp=128)
    f2src = f2T.rearrange("(hc pp) o -> pp hc o", pp=128)
    w1_free = [None, None]
    w2_free = [None, None]
    ps_free = [c["ps_free"].get(0), c["ps_free"].get(1)]
    h_free = [None] * NH
    new_x_toks = list(x_toks)
    for tp in range(tok // TTH):
        tsl0 = tp * TTH
        for hc in range(NH):
            ws = hc % 2
            wt = p.op(w1k[ws], lambda g, hc=hc, ws=ws: g.dma_start(
                out=w1b[ws][:], in_=f1src[:, :, hc * 128:(hc + 1) * 128]),
                waits=last([w1_free[ws]]))
            for sub in range(TTH // 512):
                psl = (hc * 2 + sub) % 2
                sl = slice(tsl0 + sub * 512, tsl0 + (sub + 1) * 512)
                m = None
                for dc in range(NCH):
                    m = p.op("t", lambda t, dc=dc, sl=sl, ws=ws, psl=psl:
                             t.matmul(pss[psl][:], w1b[ws][:, dc, :],
                                      xn_tiles[dc][:, sl], start=(dc == 0),
                                      stop=(dc == NCH - 1)),
                             waits=last([wt], [xn_toks[dc]], [ps_free[psl]]),
                             inc=(dc == NCH - 1))
                ev = p.op("s", lambda s, hc=hc, sub=sub, psl=psl: s.activation(
                    out=h_sb[:, hc, sub * 512:(sub + 1) * 512], in_=pss[psl][:],
                    func=AF.Gelu, bias=b1_sb[:, hc:hc + 1]),
                    waits=last([m], [b1_tok], [h_free[hc]]))
                ps_free[psl] = ev
                c["ps_free"][psl] = ev
                if sub == TTH // 512 - 1:
                    w1_free[ws] = m
        ev_h_last = None
        for oc in range(NCH):
            ws = oc % 2
            wt = p.op(w2k[ws], lambda g, oc=oc, ws=ws: g.dma_start(
                out=w2b[ws][:], in_=f2src[:, :, oc * 128:(oc + 1) * 128]),
                waits=last([w2_free[ws]]))
            for sub in range(TTH // 512):
                psl = (oc * 2 + sub) % 2
                sl = slice(tsl0 + sub * 512, tsl0 + (sub + 1) * 512)
                ssub = slice(sub * 512, (sub + 1) * 512)
                m = None
                for hc in range(NH):
                    m = p.op("t", lambda t, hc=hc, sl=sl, ssub=ssub, ws=ws, psl=psl:
                             t.matmul(pss[psl][:], w2b[ws][:, hc, :],
                                      h_sb[:, hc, ssub], start=(hc == 0),
                                      stop=(hc == NH - 1)),
                             waits=last([wt], [("s", p.cnt["s"])] if hc == 0 else [],
                                        [ps_free[psl]]),
                             inc=(hc == NH - 1))
                ev = p.op("v", lambda v, oc=oc, sl=sl, psl=psl:
                          v.scalar_tensor_tensor(out=x_tiles[oc][:, sl],
                                                 in0=pss[psl][:],
                                                 scalar=b2_sb[:, oc:oc + 1],
                                                 in1=x_tiles[oc][:, sl],
                                                 op0=AL.add, op1=AL.add),
                          waits=last([m], [b2_tok], [new_x_toks[oc]]))
                ps_free[psl] = ev
                c["ps_free"][psl] = ev
                new_x_toks[oc] = ev
                ev_h_last = m
            w2_free[ws] = ev_h_last
        for hc in range(NH):
            h_free[hc] = ev_h_last
    return new_x_toks


def build_tok(kind, with_rope=False, tok=TOK):
    p = Prog()
    nc = p.nc
    es = ExitStack()
    c = Ctx()
    xT = nc.dram_tensor("xT", [D, tok], F32, kind="ExternalInput").ap()
    attnT = nc.dram_tensor("attnT", [D, tok], BF16, kind="ExternalInput").ap()
    woT = nc.dram_tensor("woT", [D, D], F32, kind="ExternalInput").ap()
    bo = nc.dram_tensor("bo", [128, 8], F32, kind="ExternalInput").ap()
    if kind in ("inner", "final"):
        ln2p = nc.dram_tensor("ln2p", [128, 16], F32, kind="ExternalInput").ap()
        f1T = nc.dram_tensor("f1T", [D, 4 * D], F32, kind="ExternalInput").ap()
        b1 = nc.dram_tensor("b1", [128, 32], F32, kind="ExternalInput").ap()
        f2T = nc.dram_tensor("f2T", [4 * D, D], F32, kind="ExternalInput").ap()
        b2 = nc.dram_tensor("b2", [128, 8], F32, kind="ExternalInput").ap()
    if kind == "inner":
        ln1p = nc.dram_tensor("ln1p", [128, 16], F32, kind="ExternalInput").ap()
        wiT = nc.dram_tensor("wiT", [D, 3 * D], F32, kind="ExternalInput").ap()
        bqk = nc.dram_tensor("bqk", [128, 16], F32, kind="ExternalInput").ap()
        bv_row = nc.dram_tensor("bv_row", [1, D], F32, kind="ExternalInput").ap()
        qTd = nc.dram_tensor("qT", [D, tok], BF16, kind="ExternalOutput").ap()
        kTd = nc.dram_tensor("kT", [D, tok], BF16, kind="ExternalOutput").ap()
        vTokd = nc.dram_tensor("vTok", [tok, D], BF16, kind="ExternalOutput").ap()
    if kind == "final":
        lnfp = nc.dram_tensor("lnfp", [128, 16], F32, kind="ExternalInput").ap()
        xinT = nc.dram_tensor("xinT", [D, tok], F32, kind="ExternalInput").ap()
    xoT = nc.dram_tensor("xoT", [D, tok], F32, kind="ExternalOutput").ap()

    with es:
        setup_common(p, es, c)
        x_tiles, x_toks = load_x(p, es, c, xT, "x", F32, NCH, tok)
        bo_sb, _ = load_tile(p, es, "sb_bo", [128, 8], F32, bo)
        if kind in ("inner", "final"):
            ln2_sb, _ = load_tile(p, es, "sb_ln2", [128, 16], F32, ln2p)
            b1_sb, _ = load_tile(p, es, "sb_b1", [128, 32], F32, b1)
            b2_sb, _ = load_tile(p, es, "sb_b2", [128, 8], F32, b2)
        if kind == "inner":
            ln1_sb, _ = load_tile(p, es, "sb_ln1", [128, 16], F32, ln1p)
            bqk_sb, _ = load_tile(p, es, "sb_bqk", [128, 16], F32, bqk)
            bvb, _ = load_tile(p, es, "sb_bvb", [1, D], BF16, bv_row, queue="G")
        if kind == "final":
            lnf_sb, _ = load_tile(p, es, "sb_lnf", [128, 16], F32, lnfp)
        sbar = setup_bar(p)
        bo_tok = sbar
        ln2_tok = b1_tok = b2_tok = ln1_tok = lnf_tok = bv_tok = sbar
        if kind == "inner":
            bqk_tok = p.op("v", lambda v: v.tensor_scalar(out=bqk_sb[:, 0:8],
                           in0=bqk_sb[:, 0:8], scalar1=0.125, scalar2=None,
                           op0=AL.mult), waits=sbar)
        t1s = es.enter_context(nc.sbuf_tensor("t1s", [128, tok], F32))
        lnsc = ln_scratch(p, es, "lnsc", tok)

        def evac_resid(oc, tt, ps, mm_tok):
            sl = slice(tt * 512, (tt + 1) * 512)
            ev = p.op("v", lambda v, oc=oc, sl=sl, ps=ps: v.scalar_tensor_tensor(
                out=x_tiles[oc][:, sl], in0=ps[:], scalar=bo_sb[:, oc:oc + 1],
                in1=x_tiles[oc][:, sl], op0=AL.add, op1=AL.add),
                waits=last([mm_tok], [bo_tok], [x_toks[oc]]))
            x_toks[oc] = ev
            return ev

        with ExitStack() as es2:
            at_tiles, at_toks = load_x(p, es2, c, attnT, "at", BF16, NCH, tok)
            emit_proj_feat(p, es2, c, "wo", at_tiles, at_toks, woT, D, evac_resid, tok=tok)
        p.barrier()

        xn_tiles = [es.enter_context(nc.sbuf_tensor(f"xn{i}", [128, tok], BF16))
                    for i in range(NCH)]

        if kind in ("inner", "final"):
            xn_toks = emit_ln(p, es, c, "ln2", x_tiles, x_toks, xn_tiles,
                              w_ap=lambda cc: ln2_sb[:, cc:cc + 1],
                              b_ap=lambda cc: ln2_sb[:, 8 + cc:8 + cc + 1],
                              tok=tok, param_toks=ln2_tok, sc=lnsc,
                              t1_tiles=[t1s] * NCH, t1_toks=[None] * NCH)
            with ExitStack() as es3:
                x_toks = emit_ffn(p, es3, c, "ffn", xn_tiles, xn_toks, f1T, b1_sb,
                                  b1_tok, f2T, b2_sb, b2_tok, x_tiles, x_toks, tok=tok)
            p.barrier()

        if kind == "inner":
            xn_toks = emit_ln(p, es, c, "ln1n", x_tiles, x_toks, xn_tiles,
                              w_ap=lambda cc: ln1_sb[:, cc:cc + 1],
                              b_ap=lambda cc: ln1_sb[:, 8 + cc:8 + cc + 1],
                              tok=tok, param_toks=ln1_tok, sc=lnsc,
                              t1_tiles=[t1s] * NCH, t1_toks=[None] * NCH)
            emit_qkv_part(p, es, c, "qkv", xn_tiles, xn_toks, wiT, bqk_sb, bqk_tok,
                          bvb, bv_tok, qTd, kTd, vTokd, rope=None, tok=tok)
        elif kind == "final":
            y_toks = emit_ln(p, es, c, "lnf", x_tiles, x_toks, xn_tiles,
                             w_ap=lambda cc: lnf_sb[:, cc:cc + 1],
                             b_ap=lambda cc: lnf_sb[:, 8 + cc:8 + cc + 1],
                             tok=tok, param_toks=lnf_tok, sc=lnsc,
                             t1_tiles=[t1s] * NCH, t1_toks=[None] * NCH)
            xik = p.key("S", "xik")
            xin_last = None
            for cc in range(NCH):
                xin_last = p.op(xik, lambda h, cc=cc: h.dma_start(
                    out=x_tiles[cc][:], in_=xinT[cc * 128:(cc + 1) * 128, :]),
                    waits=[y_toks[cc], ("v", p.cnt["v"])])
            xin_toks = [xin_last] * NCH
            add_toks = [p.op("v", lambda v, cc=cc: v.tensor_tensor(
                out=x_tiles[cc][:], in0=x_tiles[cc][:], in1=xn_tiles[cc][:],
                op=AL.add), waits=[xin_toks[cc], y_toks[cc]]) for cc in range(NCH)]
            x_toks = emit_ln(p, es, c, "pln", x_tiles, add_toks, x_tiles,
                             tok=tok, sc=lnsc)
        elif kind == "wd":
            x_toks = emit_ln(p, es, c, "pln", x_tiles, x_toks, x_tiles,
                             tok=tok, sc=lnsc)

        xok = p.key("S", "xok")
        for cc in range(NCH):
            p.op(xok, lambda h, cc=cc: h.dma_start(
                out=xoT[cc * 128:(cc + 1) * 128, :], in_=x_tiles[cc][:]),
                waits=[x_toks[cc]])
        p.op("S", lambda h: h.nop(), waits=all_dma_bar(p), inc=False)
        p.finalize(es)
    return p



def emit_router_coef(p, es, c, name, xb_tiles, xb_toks, rw_sb, rw_tok, gb_sb,
                     gb_tok, mode, coefD, tok=TOK):
    nc = p.nc
    NE = 8
    lg = es.enter_context(nc.sbuf_tensor(name + "_lg", [128, NE], F32))
    l2 = es.enter_context(nc.sbuf_tensor(name + "_l2", [128, NE], F32))
    eq1 = es.enter_context(nc.sbuf_tensor(name + "_eq1", [128, NE], F32))
    eq2 = es.enter_context(nc.sbuf_tensor(name + "_eq2", [128, NE], F32))
    cf = es.enter_context(nc.sbuf_tensor(name + "_cf", [128, NE], F32))
    m1 = es.enter_context(nc.sbuf_tensor(name + "_m1", [128, 1], F32))
    m2 = es.enter_context(nc.sbuf_tensor(name + "_m2", [128, 1], F32))
    w1 = es.enter_context(nc.sbuf_tensor(name + "_w1", [128, 1], F32))
    w2 = es.enter_context(nc.sbuf_tensor(name + "_w2", [128, 1], F32))
    sm = es.enter_context(nc.sbuf_tensor(name + "_sm", [128, 1], F32))
    iot = es.enter_context(nc.sbuf_tensor(name + "_iot", [128, NE], F32))
    cfb = es.enter_context(nc.sbuf_tensor(name + "_cfb", [128, NE], BF16))
    wiRow = None
    wi_toks = []
    if mode == "moe1":
        wiRow = es.enter_context(nc.sbuf_tensor(name + "_wiR", [1, 8 * tok], BF16))
        pst = es.enter_context(nc.psum_tensor(name + "_pst", [1, 128], BF16))
        ident = c["ident"]
    onesf = es.enter_context(nc.sbuf_tensor(name + "_onesf", [1, 128], F32))
    of = p.op("g", lambda g: g.memset(onesf[:], 1.0))
    it = p.op("g", lambda g: g.iota(iot[:], pattern=[[1, NE]], base=0,
              channel_multiplier=0, allow_small_or_imprecise_dtypes=True))
    ps = c["ps"][4][:, 0:NE]
    prev = c["ps_free"].get(4)
    coef_toks = []
    for tt in range(tok // 128):
        tsl = slice(tt * 128, (tt + 1) * 128)
        m = None
        for dc in range(NCH):
            m = p.op("t", lambda t, dc=dc, tsl=tsl: t.matmul(
                ps, xb_tiles[dc][:, tsl], rw_sb[:, dc, :],
                start=(dc == 0), stop=False),
                waits=last([rw_tok], [xb_toks[dc]], [prev]), inc=False)
        m = p.op("t", lambda t: t.matmul(ps, onesf[:], gb_sb[:],
                 start=False, stop=True),
                 waits=last([gb_tok], [of]))
        tl = p.op("v", lambda v: v.tensor_copy(lg[:], ps), waits=last([m], [prev]))
        prev = tl
        c["ps_free"][4] = tl
        ncl = 7 if mode == "moe0" else 8
        a1 = p.op("v", lambda v, ncl=ncl: v.reduce_max(m1[:], lg[:, 0:ncl], axis=AX.X),
                  waits=[tl])
        a2 = p.op("v", lambda v: v.tensor_scalar(out=eq1[:], in0=lg[:], scalar1=m1[:],
                  scalar2=None, op0=AL.is_equal), waits=[a1])
        a3 = p.op("v", lambda v: v.scalar_tensor_tensor(out=l2[:], in0=eq1[:],
                  scalar=-1e30, in1=lg[:], op0=AL.mult, op1=AL.add), waits=[a2])
        a4 = p.op("v", lambda v, ncl=ncl: v.reduce_max(m2[:], l2[:, 0:ncl], axis=AX.X),
                  waits=[a3])
        a5 = p.op("v", lambda v: v.tensor_scalar(out=eq2[:], in0=lg[:], scalar1=m2[:],
                  scalar2=None, op0=AL.is_equal), waits=[a4])
        if mode == "moe0":
            b1t = p.op("v", lambda v: v.tensor_scalar(out=w2[:], in0=m2[:], scalar1=m1[:],
                       scalar2=None, op0=AL.subtract), waits=[a4])
            b2t = p.op("s", lambda s: s.activation(out=w1[:], in_=w2[:], func=AF.Sigmoid,
                       scale=-1.0), waits=[b1t])
            b3t = p.op("v", lambda v: v.tensor_scalar(out=w2[:], in0=w1[:], scalar1=-1.0,
                       scalar2=1.0, op0=AL.mult, op1=AL.add), waits=[b2t])
            b4t = p.op("v", lambda v: v.tensor_scalar(out=cf[:], in0=eq1[:], scalar1=w1[:],
                       scalar2=None, op0=AL.mult),
                       waits=last([b2t], [a2], coef_toks[-1:]))
            b5t = p.op("v", lambda v: v.scalar_tensor_tensor(out=cf[:, 0:7], in0=eq2[:, 0:7],
                       scalar=w2[:], in1=cf[:, 0:7], op0=AL.mult, op1=AL.add),
                       waits=[b3t, b4t, a5])
            b6t = p.op("g", lambda g: g.memset(cf[:, 7:8], 1.0), waits=[b5t])
            done = [b6t, b5t]
        else:
            en = p.op("v", lambda v: v.tensor_scalar(out=w2[:], in0=m1[:], scalar1=-1.0,
                      scalar2=None, op0=AL.mult), waits=[a1])
            e1 = p.op("s", lambda s: s.activation(out=l2[:], in_=lg[:], func=AF.Exp,
                      bias=w2[:]), waits=last([a5], [en], coef_toks[-1:]))
            e2 = p.op("v", lambda v: v.reduce_sum(sm[:], l2[:], axis=AX.X), waits=[e1])
            e3 = p.op("v", lambda v: v.reciprocal(w1[:], sm[:]), waits=[e2])
            e4 = p.op("v", lambda v: v.tensor_scalar(out=cf[:], in0=l2[:], scalar1=w1[:],
                      scalar2=None, op0=AL.mult), waits=[e3])
            s1t = p.op("v", lambda v: v.tensor_tensor(out=eq1[:], in0=eq1[:], in1=eq2[:],
                       op=AL.add), waits=[a5, e4])
            s2t = p.op("v", lambda v: v.tensor_tensor(out=cfb[:], in0=cf[:], in1=eq1[:],
                       op=AL.mult), waits=[s1t])
            s3t = p.op("v", lambda v: v.tensor_tensor(out=l2[:], in0=iot[:], in1=eq1[:],
                       op=AL.mult), waits=[s1t, it])
            s4t = p.op("v", lambda v: v.reduce_max(m2[:], l2[:], axis=AX.X), waits=[s3t])
            s5t = p.op("v", lambda v: v.tensor_scalar(out=cf[:], in0=iot[:], scalar1=m2[:],
                       scalar2=None, op0=AL.is_equal), waits=[s4t])
            wv = None
            for e in range(NE):
                tr = p.op("t", lambda t, e=e: t.transpose(pst[:], cfb[:, e:e + 1],
                          ident[:]), waits=last([s2t], [c["ident_tok"]], wi_toks[-1:]))
                wv = p.op("v", lambda v, e=e, tt=tt: v.tensor_copy(
                    wiRow[0:1, e * tok + tt * 128:e * tok + (tt + 1) * 128], pst[:]),
                    waits=[tr])
                wi_toks.append(wv)
            done = [s5t, wv]
        dt = p.op(p.key("S", "cfk"), lambda h, tt=tt: h.dma_start(
            out=coefD[tt * 128:(tt + 1) * 128, :], in_=cf[:]), waits=done)
        coef_toks.append(dt)
        prev = dt
    return coef_toks, wiRow, (wi_toks[-1] if wi_toks else None)


def build_moe_h(mode, tok=TOK):
    p = Prog()
    nc = p.nc
    es = ExitStack()
    c = Ctx()
    NE = 8
    xT = nc.dram_tensor("xT", [D, tok], F32, kind="ExternalInput").ap()
    rwT = nc.dram_tensor("rwT", [D, NE], F32, kind="ExternalInput").ap()
    gbRow = nc.dram_tensor("gbRow", [1, NE], F32, kind="ExternalInput").ap()
    if mode == "moe0":
        wgu = nc.dram_tensor("wgu", [NE, D, 2 * 4096], F32, kind="ExternalInput").ap()
    else:
        w1T = nc.dram_tensor("w1T", [NE, D, 4096], F32, kind="ExternalInput").ap()
        b1t = nc.dram_tensor("b1t", [128, NE, 32], F32, kind="ExternalInput").ap()
        identD = nc.dram_tensor("identD", [128, 128], F32, kind="ExternalInput").ap()
    hD = nc.dram_tensor("hD", [NE, 4096, tok], BF16, kind="ExternalOutput").ap()
    coefD = nc.dram_tensor("coefD", [tok, NE], F32, kind="ExternalOutput").ap()

    with es:
        setup_common(p, es, c)
        x_tiles, x_toks = load_x(p, es, c, xT, "x", F32, NCH, tok)
        xb_tiles = [es.enter_context(nc.sbuf_tensor(f"xb{i}", [128, tok], BF16))
                    for i in range(NCH)]
        xb_toks = [p.op("s", lambda s, cc=cc: s.activation(
            out=xb_tiles[cc][:], in_=x_tiles[cc][:], func=AF.Copy),
            waits=[x_toks[cc]]) for cc in range(NCH)]
        rw_sb, _ = load_tile(p, es, "sb_rw", [128, NCH, NE], F32,
                             rwT.rearrange("(dc pp) e -> pp dc e", pp=128))
        gb_sb, _ = load_tile(p, es, "sb_gb", [1, NE], F32, gbRow)
        if mode == "moe1":
            c["ident"], _ = load_tile(p, es, "sb_id", [128, 128], BF16,
                                      identD, queue="G")
            b1_sb, _ = load_tile(p, es, "sb_b1t", [128, NE, 32], F32, b1t)
        sbar = setup_bar(p)
        rw_tok = gb_tok = b1_tok = sbar
        c["ident_tok"] = sbar

        coef_toks, wiRow, wi_tok = emit_router_coef(
            p, es, c, "rt", x_tiles, x_toks, rw_sb, rw_tok, gb_sb, gb_tok,
            mode, coefD, tok=tok)

        wb = [es.enter_context(nc.sbuf_tensor(f"wgub{i}", [128, NCH, 128], BF16))
              for i in range(4)]
        wbk = [p.key("G", f"wgubk{i}") for i in range(4)]
        hsk = [p.key("S", f"hsk{i}") for i in range(2)]
        sg = [es.enter_context(nc.sbuf_tensor(f"sg{i}", [128, 512], F32))
              for i in range(2)]
        hstrip = [es.enter_context(nc.sbuf_tensor(f"hstrip{i}", [128, tok], BF16))
                  for i in range(2)]
        cbc = es.enter_context(nc.sbuf_tensor("cbc", [128, tok], BF16))
        wfree = [None] * 4
        sgfree = [None, None]
        hfree = [None, None]
        cbc_free = None
        ntt = tok // 512
        for e in range(NE):
            if mode == "moe1":
                cb_toks = []
                for tt in range(ntt):
                    sl = slice(tt * 512, (tt + 1) * 512)
                    m = p.op("t", lambda t, e=e, sl=sl: t.matmul(
                        c["ps"][5][:], c["ones_row"][:],
                        wiRow[0:1, e * tok + sl.start:e * tok + sl.stop],
                        start=True, stop=True),
                        waits=last([wi_tok], c["ones_ready"], [c["ps_free"].get(5)],
                                   [cbc_free]))
                    ev = p.op("v", lambda v, sl=sl: v.tensor_copy(cbc[:, sl],
                              c["ps"][5][:]), waits=[m])
                    c["ps_free"][5] = ev
                    cb_toks.append(ev)
            for hc in range(32):
                if mode == "moe0":
                    ws, ws2 = (hc % 2), 2 + (hc % 2)
                    wt1 = p.op(wbk[ws], lambda g, e=e, hc=hc, ws=ws: g.dma_start(
                        out=wb[ws][:], in_=wgu[e].rearrange("(dc pp) o -> pp dc o", pp=128)
                        [:, :, hc * 128:(hc + 1) * 128]), waits=last([wfree[ws]]))
                    wt2 = p.op(wbk[ws2], lambda g, e=e, hc=hc, ws2=ws2: g.dma_start(
                        out=wb[ws2][:], in_=wgu[e].rearrange("(dc pp) o -> pp dc o", pp=128)
                        [:, :, 4096 + hc * 128:4096 + (hc + 1) * 128]),
                        waits=last([wfree[ws2]]))
                else:
                    ws = hc % 2
                    wt1 = p.op(wbk[ws], lambda g, e=e, hc=hc, ws=ws: g.dma_start(
                        out=wb[ws][:], in_=w1T[e].rearrange("(dc pp) o -> pp dc o", pp=128)
                        [:, :, hc * 128:(hc + 1) * 128]), waits=last([wfree[ws]]))
                hrot = hc % 2
                for tt in range(ntt):
                    sl = slice(tt * 512, (tt + 1) * 512)
                    pa, pb = tt % 2, 2 + tt % 2
                    mg_ = None
                    for dc in range(NCH):
                        mg_ = p.op("t", lambda t, dc=dc, sl=sl, ws=ws, pa=pa:
                                   t.matmul(c["ps"][pa][:], wb[ws][:, dc, :],
                                            xb_tiles[dc][:, sl], start=(dc == 0),
                                            stop=(dc == NCH - 1)),
                                   waits=last([wt1], [xb_toks[dc]],
                                              [c["ps_free"].get(pa)]),
                                   inc=(dc == NCH - 1))
                    if mode == "moe0":
                        mu_ = None
                        for dc in range(NCH):
                            mu_ = p.op("t", lambda t, dc=dc, sl=sl, ws2=ws2, pb=pb:
                                       t.matmul(c["ps"][pb][:], wb[ws2][:, dc, :],
                                                xb_tiles[dc][:, sl], start=(dc == 0),
                                                stop=(dc == NCH - 1)),
                                       waits=last([wt2], [xb_toks[dc]],
                                                  [c["ps_free"].get(pb)]),
                                       inc=(dc == NCH - 1))
                        srot = tt % 2
                        tsg = p.op("s", lambda s, srot=srot, pa=pa: s.activation(
                            out=sg[srot][:], in_=c["ps"][pa][:], func=AF.Silu),
                            waits=last([mg_], [sgfree[srot]]))
                        c["ps_free"][pa] = tsg
                        th = p.op("v", lambda v, srot=srot, pb=pb, sl=sl, hrot=hrot:
                                  v.tensor_tensor(out=hstrip[hrot][:, sl],
                                                  in0=sg[srot][:], in1=c["ps"][pb][:],
                                                  op=AL.mult),
                                  waits=last([tsg], [mu_], [hfree[hrot]]))
                        c["ps_free"][pb] = th
                        sgfree[srot] = th
                    else:
                        srot = tt % 2
                        tm = p.op("v", lambda v, srot=srot, pa=pa, sl=sl:
                                  v.scalar_tensor_tensor(out=sg[srot][:], in0=cbc[:, sl],
                                                         scalar=1.0, in1=c["ps"][pa][:],
                                                         op0=AL.mult, op1=AL.mult),
                                  waits=last([mg_], cb_toks, [sgfree[srot]]))
                        c["ps_free"][pa] = tm
                        th = p.op("s", lambda s, srot=srot, sl=sl, hrot=hrot, e=e, hc=hc:
                                  s.activation(out=hstrip[hrot][:, sl], in_=sg[srot][:],
                                               func=AF.Relu,
                                               bias=b1_sb[:, e, hc:hc + 1]),
                                  waits=last([tm], [b1_tok], [hfree[hrot]]))
                        sgfree[srot] = th
                    if tt == ntt - 1:
                        wfree[ws] = mg_
                        if mode == "moe0":
                            wfree[ws2] = mu_
                        hd = p.op(hsk[hrot], lambda h, e=e, hc=hc, hrot=hrot: h.dma_start(
                            out=hD[e][hc * 128:(hc + 1) * 128, :], in_=hstrip[hrot][:]),
                            waits=[th])
                        hfree[hrot] = hd
            cbc_free = ("t", p.cnt["t"]) if mode == "moe1" else None
        p.op("S", lambda h: h.nop(), waits=all_dma_bar(p), inc=False)
        p.finalize(es)
    return p


def build_moe_down(mode, tok=TOK):
    p = Prog()
    nc = p.nc
    es = ExitStack()
    c = Ctx()
    NE = 8
    TP = 1024
    xT = nc.dram_tensor("xT", [D, tok], F32, kind="ExternalInput").ap()
    hD = nc.dram_tensor("hD", [NE, 4096, tok], BF16, kind="ExternalInput").ap()
    coefD = nc.dram_tensor("coefD", [tok, NE], F32, kind="ExternalInput").ap()
    wd = nc.dram_tensor("wd", [NE, 4096, D], F32, kind="ExternalInput").ap()
    b2Row = nc.dram_tensor("b2Row", [1, NE * D], F32, kind="ExternalInput").ap()
    identD = nc.dram_tensor("identD", [128, 128], F32, kind="ExternalInput").ap()
    xoT = nc.dram_tensor("xoT", [D, tok], F32, kind="ExternalOutput").ap()

    with es:
        setup_common(p, es, c)
        identf, _ = load_tile(p, es, "sb_idf", [128, 128], F32, identD)
        coef_sb, _ = load_tile(p, es, "sb_coef", [128, tok // 128, NE], F32,
                               coefD.rearrange("(tt pp) e -> pp tt e", pp=128))
        b2_sb, _ = load_tile(p, es, "sb_b2r", [1, NE * D], BF16, b2Row, queue="G")
        sbar = setup_bar(p)
        identf_tok = coef_tok = b2_tok = sbar
        acc = es.enter_context(nc.sbuf_tensor("acc", [128, TP // 128, D], F32))
        hbuf = [es.enter_context(nc.sbuf_tensor(f"hbuf{i}", [128, 16, TP], BF16))
                for i in range(2)]
        wdb = es.enter_context(nc.sbuf_tensor("wdb", [128, 32, 512], BF16))
        xs_tiles = [es.enter_context(nc.sbuf_tensor(f"xs{i}", [128, TP], F32))
                    for i in range(NCH)]
        lnsc = ln_scratch(p, es, "lnsc", TP)
        pstr = es.enter_context(nc.psum_tensor("pstr", [128, 128], F32))

        hfree = [None, None]
        wd_free = [None, None]
        acc_free = {}
        accb_free = {}
        pstr_free = None
        NTS = TP // 128

        for tp in range(tok // TP):
            tpsl = slice(tp * TP, (tp + 1) * TP)
            z_toks = {}
            tbar0 = ("t", p.cnt["t"])
            for ts in range(NTS):
                z_toks[ts] = p.op("g", lambda g, ts=ts: g.memset(acc[:, ts, :], 0.0),
                                  waits=[tbar0])
            for e in range(NE):
                ht = []
                hbk = [p.key("G", f"hbk{i}") for i in range(2)]
                for half in range(2):
                    ht.append(p.op(hbk[half], lambda g, e=e, half=half, tpsl=tpsl:
                                   g.dma_start(out=hbuf[half][:],
                                               in_=hD[e].rearrange("(hc pp) t -> pp hc t", pp=128)
                                               [:, half * 16:(half + 1) * 16, tpsl]),
                                   waits=last([hfree[half]])))
                for dc2 in range(2):
                    dsl = slice(dc2 * 512, (dc2 + 1) * 512)
                    wt = []
                    wdk = [p.key("G", f"wdk{i}") for i in range(2)]
                    for half in range(2):
                        wt.append(p.op(wdk[half], lambda g, e=e, half=half, dsl=dsl:
                                       g.dma_start(out=wdb[:, half * 16:(half + 1) * 16, :],
                                                   in_=wd[e].rearrange("(hc pp) o -> pp hc o", pp=128)
                                                   [:, half * 16:(half + 1) * 16, dsl]),
                                       waits=last([wd_free[half]])))
                    last_mm = None
                    for ts in range(NTS):
                        psl = ts % 4
                        m = None
                        for hc in range(32):
                            m = p.op("t", lambda t, hc=hc, ts=ts, psl=psl:
                                     t.matmul(c["ps"][psl][:],
                                              hbuf[hc // 16][:, hc % 16, ts * 128:(ts + 1) * 128],
                                              wdb[:, hc, :], start=(hc == 0), stop=False),
                                     waits=last([ht[hc // 16]], [wt[hc // 16]],
                                                [c["ps_free"].get(psl)]), inc=False)
                        m = p.op("t", lambda t, e=e, dc2=dc2, psl=psl: t.matmul(
                            c["ps"][psl][:], c["ones_row"][:],
                            b2_sb[:, e * D + dc2 * 512:e * D + (dc2 + 1) * 512],
                            start=False, stop=True),
                            waits=last([b2_tok], c["ones_ready"]))
                        ev = p.op("v", lambda v, e=e, ts=ts, tp=tp, dsl=dsl, psl=psl:
                                  v.scalar_tensor_tensor(
                                      out=acc[:, ts, dsl], in0=c["ps"][psl][:],
                                      scalar=coef_sb[:, tp * NTS + ts, e:e + 1],
                                      in1=acc[:, ts, dsl], op0=AL.mult, op1=AL.add),
                                  waits=last([m], [coef_tok], [z_toks[ts]],
                                             [acc_free.get((tp, ts, dsl.start))]))
                        acc_free[(tp, ts, dsl.start)] = ev
                        c["ps_free"][psl] = ev
                        last_mm = m
                    wd_free = [last_mm, last_mm]
                hfree = [last_mm, last_mm]
            xdk = p.key("S", "xdk")
            xok2 = p.key("S", "xok2")
            x_last = None
            for cc in range(NCH):
                x_last = p.op(xdk, lambda h, cc=cc, tpsl=tpsl: h.dma_start(
                    out=xs_tiles[cc][:], in_=xT[cc * 128:(cc + 1) * 128, tpsl]),
                    waits=last([("xok2", p.cnt["xok2"])] if tp > 0 else []))
            x_toks = [x_last] * NCH
            addt = {}
            for cc in range(NCH):
                for ts in range(NTS):
                    tr = p.op("t", lambda t, cc=cc, ts=ts: t.transpose(
                        pstr[:], acc[:, ts, cc * 128:(cc + 1) * 128], identf[:]),
                        waits=last([acc_free.get((tp, ts, 0))],
                                   [acc_free.get((tp, ts, 512))],
                                   [identf_tok], [pstr_free]))
                    ad = p.op("v", lambda v, cc=cc, ts=ts: v.scalar_tensor_tensor(
                        out=xs_tiles[cc][:, ts * 128:(ts + 1) * 128], in0=pstr[:],
                        scalar=1.0, in1=xs_tiles[cc][:, ts * 128:(ts + 1) * 128],
                        op0=AL.mult, op1=AL.add),
                        waits=last([tr], [x_toks[cc]]))
                    pstr_free = ad
                    addt[cc] = ad
            out_toks = emit_ln(p, es, c, f"pln{tp}", xs_tiles,
                               [addt[cc] for cc in range(NCH)], xs_tiles,
                               tok=TP, sc=lnsc)
            for cc in range(NCH):
                p.op(xok2, lambda h, cc=cc, tpsl=tpsl: h.dma_start(
                    out=xoT[cc * 128:(cc + 1) * 128, tpsl], in_=xs_tiles[cc][:]),
                    waits=[out_toks[cc]])
        p.op("S", lambda h: h.nop(), waits=all_dma_bar(p), inc=False)
        p.finalize(es)
    return p



def _tile8(v, n):
    return np.ascontiguousarray(v.reshape(n, 128).T).astype(np.float32)


def _lnp(w, b):
    return np.concatenate([_tile8(w, 8), _tile8(b, 8)], axis=1)


_PROG_CACHE = {}


def _prog(key, builder, *args, **kw):
    if key not in _PROG_CACHE:
        _PROG_CACHE[key] = builder(*args, **kw)
    return _PROG_CACHE[key]


def _run(p, in_maps):
    res = run_bass_kernel_spmd(p.nc, in_maps, core_ids=list(range(NCORES)))
    return res.results


def _attn_assemble(qTs, kTs, vTs, qpos_rows, ab_rows):
    QA = np.zeros((NCORES, 8, 65, T), dtype=BF)
    KA = np.zeros((NCORES, 8, 65, T), dtype=BF)
    VA = np.zeros((NCORES, 8, T, 65), dtype=BF)
    ABc = np.zeros((NCORES, 8, 128, 32), dtype=np.float32)
    for u in range(64):
        b, h = u // 16, u % 16
        cc, j = u // 8, u % 8
        q = np.concatenate([np.asarray(qTs[2 * b])[64 * h:64 * h + 64],
                            np.asarray(qTs[2 * b + 1])[64 * h:64 * h + 64]], axis=1)
        k = np.concatenate([np.asarray(kTs[2 * b])[64 * h:64 * h + 64],
                            np.asarray(kTs[2 * b + 1])[64 * h:64 * h + 64]], axis=1)
        v = np.concatenate([np.asarray(vTs[2 * b])[:, 64 * h:64 * h + 64],
                            np.asarray(vTs[2 * b + 1])[:, 64 * h:64 * h + 64]], axis=0)
        QA[cc, j, 0:64] = q
        QA[cc, j, 64] = qpos_rows[h]
        KA[cc, j, 0:64] = k
        KA[cc, j, 64] = 1.0
        VA[cc, j, :, 0:64] = v
        VA[cc, j, :, 64] = 1.0
        ABc[cc, j] = ab_rows[h]
    return QA, KA, VA, ABc


def _attn_out_slices(OTs):
    full = np.zeros((B, D, T), dtype=BF)
    for u in range(64):
        b, h = u // 16, u % 16
        cc, j = u // 8, u % 8
        full[b, 64 * h:64 * h + 64] = np.asarray(OTs[cc])[j]
    out = []
    for cc in range(NCORES):
        b, half = cc // 2, cc % 2
        out.append(np.ascontiguousarray(full[b][:, half * TOK:(half + 1) * TOK]))
    return out


_DBG_HOOK = None


def _dbg(name, xs):
    if _DBG_HOOK is not None:
        _DBG_HOOK(name, xs)


def kernel(**inp):
    f32 = np.float32
    x0 = np.asarray(inp["L0_attn_x"], dtype=f32)
    xT = np.ascontiguousarray(x0.reshape(B * T, D).T)
    xs = [np.ascontiguousarray(xT[:, c * TOK:(c + 1) * TOK]) for c in range(NCORES)]
    x_in = [a.copy() for a in xs]

    slopes = alibi_slopes(H)
    pos = np.arange(T, dtype=np.float64)
    qpos_rows = [(-slopes[h] * pos).astype(BF) for h in range(H)]
    p_idx = np.arange(128, dtype=np.float64)[:, None]
    kb_idx = np.arange(32, dtype=np.float64)[None, :]
    ab_rows = [(slopes[h] * (128 * kb_idx + p_idx)).astype(f32) for h in range(H)]
    ab_zero = [np.zeros((128, 32), f32)] * H
    qpos_zero = [np.zeros(T, BF)] * H
    ident = np.eye(128, dtype=f32)
    invf = 10000.0 ** (-np.arange(0, 64, 2) / 64.0)
    cosf = np.cos(pos[None, :] * invf[:, None]).astype(f32)
    sinf = np.sin(pos[None, :] * invf[:, None]).astype(f32)
    cosT = np.zeros((128, T), f32)
    sinT = np.zeros((128, T), f32)
    cosT[0:32] = cosf; cosT[64:96] = cosf
    sinT[0:32] = sinf; sinT[64:96] = sinf

    def seqpos_slice(arr2d, cc):
        half = cc % 2
        return np.ascontiguousarray(arr2d[:, half * TOK:(half + 1) * TOK])

    ipw = np.asarray(inp["L0_attn_in_proj_w"], f32)
    ipb = np.asarray(inp["L0_attn_in_proj_b"], f32)
    opw = np.asarray(inp["L0_attn_out_proj_w"], f32)
    opb = np.asarray(inp["L0_attn_out_proj_b"], f32)
    f1w = np.asarray(inp["L0_attn_ffn1_w"], f32)
    f1b = np.asarray(inp["L0_attn_ffn1_b"], f32)
    f2w = np.asarray(inp["L0_attn_ffn2_w"], f32)
    f2b = np.asarray(inp["L0_attn_ffn2_b"], f32)
    ln1w = np.asarray(inp["L0_attn_ln1_w"], f32)
    ln1b = np.asarray(inp["L0_attn_ln1_b"], f32)
    ln2w = np.asarray(inp["L0_attn_ln2_w"], f32)
    ln2b = np.asarray(inp["L0_attn_ln2_b"], f32)

    def l0_qkv_inputs(i):
        wiT = np.ascontiguousarray(ipw[i].T)
        bq = ipb[i]
        return {"wiT": wiT,
                "bqk": np.concatenate([_tile8(bq[:D], 8), _tile8(bq[D:2 * D], 8)], 1),
                "bv_row": np.ascontiguousarray(bq[2 * D:].reshape(1, D)),
                "lnp": _lnp(ln1w[i], ln1b[i])}

    p_qkv = _prog("qkv_ln", build_qkv, True, False)
    p_attn = _prog("attn", build_attn)
    p_inner = _prog("tok_inner", build_tok, "inner")
    p_final = _prog("tok_final", build_tok, "final")

    l0q = l0_qkv_inputs(0)
    r = _run(p_qkv, [dict(xT=xs[c], **l0q) for c in range(NCORES)])
    qTs = [r[c]["qT"] for c in range(NCORES)]
    kTs = [r[c]["kT"] for c in range(NCORES)]
    vTs = [r[c]["vTok"] for c in range(NCORES)]

    for i in range(4):
        QA, KA, VA, ABc = _attn_assemble(qTs, kTs, vTs, qpos_rows, ab_rows)
        r = _run(p_attn, [{"QA": QA[c], "KA": KA[c], "VA": VA[c], "AB": ABc[c]}
                          for c in range(NCORES)])
        at = _attn_out_slices([r[c]["OT"] for c in range(NCORES)])
        common = {"woT": np.ascontiguousarray(opw[i].T), "bo": _tile8(opb[i], 8),
                  "ln2p": _lnp(ln2w[i], ln2b[i]),
                  "f1T": np.ascontiguousarray(f1w[i].T), "b1": _tile8(f1b[i], 32),
                  "f2T": np.ascontiguousarray(f2w[i].T), "b2": _tile8(f2b[i], 8)}
        if i < 3:
            nq = l0_qkv_inputs(i + 1)
            r = _run(p_inner, [dict(xT=xs[c], attnT=at[c], **common, ln1p=nq["lnp"],
                                    wiT=nq["wiT"], bqk=nq["bqk"], bv_row=nq["bv_row"])
                               for c in range(NCORES)])
            xs = [np.asarray(r[c]["xoT"], f32) for c in range(NCORES)]
            _dbg(f"l0_{i}", xs)
            qTs = [r[c]["qT"] for c in range(NCORES)]
            kTs = [r[c]["kT"] for c in range(NCORES)]
            vTs = [r[c]["vTok"] for c in range(NCORES)]
        else:
            fin = {"lnfp": _lnp(np.asarray(inp["L0_attn_final_w"], f32),
                                np.asarray(inp["L0_attn_final_b"], f32))}
            r = _run(p_final, [dict(xT=xs[c], attnT=at[c], **common, **fin,
                                    xinT=x_in[c]) for c in range(NCORES)])
            xs = [np.asarray(r[c]["xoT"], f32) for c in range(NCORES)]
            _dbg("xb0", xs)

    rw0 = np.asarray(inp["L0_ffn_router_w"], f32)
    rwT0 = np.concatenate([rw0.T, np.zeros((D, 1), f32)], 1)
    wg8 = np.concatenate([np.asarray(inp["L0_ffn_w_gate"], f32),
                          np.asarray(inp["L0_ffn_sw_gate"], f32)], 0)
    wu8 = np.concatenate([np.asarray(inp["L0_ffn_w_up"], f32),
                          np.asarray(inp["L0_ffn_sw_up"], f32)], 0)
    wgu = np.ascontiguousarray(np.concatenate([wg8, wu8], axis=2))
    wd8 = np.ascontiguousarray(np.concatenate([np.asarray(inp["L0_ffn_w_down"], f32),
                                               np.asarray(inp["L0_ffn_sw_down"], f32)], 0))
    gb0 = np.zeros((1, 8), f32)
    b2z = np.zeros((1, 8 * D), f32)
    p_h0 = _prog("moe0_h", build_moe_h, "moe0")
    p_d0 = _prog("moe0_d", build_moe_down, "moe0")
    r = _run(p_h0, [{"xT": xs[c], "rwT": rwT0, "gbRow": gb0, "wgu": wgu}
                    for c in range(NCORES)])
    r = _run(p_d0, [{"xT": xs[c], "hD": np.asarray(r[c]["hD"]),
                     "coefD": np.asarray(r[c]["coefD"], f32), "wd": wd8,
                     "b2Row": b2z, "identD": ident} for c in range(NCORES)])
    xs = [np.asarray(r[c]["xoT"], f32) for c in range(NCORES)]
    _dbg("xb1", xs)

    wiT1 = np.ascontiguousarray(np.concatenate(
        [np.asarray(inp["L1_attn_Wq"], f32), np.asarray(inp["L1_attn_Wk"], f32),
         np.asarray(inp["L1_attn_Wv"], f32)], axis=1))
    bq1 = np.concatenate([np.asarray(inp["L1_attn_bq"], f32),
                          np.asarray(inp["L1_attn_bk"], f32)])
    bv1 = np.asarray(inp["L1_attn_bv"], f32)
    p_qkv1 = _prog("qkv_rope", build_qkv, False, True)
    r = _run(p_qkv1, [{"xT": xs[c], "wiT": wiT1,
                       "bqk": np.concatenate([_tile8(bq1[:D], 8), _tile8(bq1[D:], 8)], 1),
                       "bv_row": bv1.reshape(1, D),
                       "cosT": seqpos_slice(cosT, c), "sinT": seqpos_slice(sinT, c)}
                      for c in range(NCORES)])
    QA, KA, VA, ABc = _attn_assemble([r[c]["qT"] for c in range(NCORES)],
                                     [r[c]["kT"] for c in range(NCORES)],
                                     [r[c]["vTok"] for c in range(NCORES)],
                                     qpos_zero, ab_zero)
    r = _run(p_attn, [{"QA": QA[c], "KA": KA[c], "VA": VA[c], "AB": ABc[c]}
                      for c in range(NCORES)])
    at = _attn_out_slices([r[c]["OT"] for c in range(NCORES)])
    p_wd = _prog("tok_wd", build_tok, "wd")
    r = _run(p_wd, [{"xT": xs[c], "attnT": at[c],
                     "woT": np.asarray(inp["L1_attn_Wd"], f32),
                     "bo": _tile8(np.asarray(inp["L1_attn_bd"], f32), 8)}
                    for c in range(NCORES)])
    xs = [np.asarray(r[c]["xoT"], f32) for c in range(NCORES)]
    _dbg("xb2", xs)

    rwT1 = np.ascontiguousarray(np.asarray(inp["L1_ffn_gate_w"], f32).T)
    gb1 = np.asarray(inp["L1_ffn_gate_b"], f32).reshape(1, 8)
    w1T = np.ascontiguousarray(np.asarray(inp["L1_ffn_w1"], f32).transpose(0, 2, 1))
    b1m = np.asarray(inp["L1_ffn_b1"], f32)
    b1t = np.ascontiguousarray(b1m.reshape(8, 32, 128).transpose(2, 0, 1))
    w2T = np.ascontiguousarray(np.asarray(inp["L1_ffn_w2"], f32).transpose(0, 2, 1))
    b2r = np.ascontiguousarray(np.asarray(inp["L1_ffn_b2"], f32).reshape(1, 8 * D))
    p_h1 = _prog("moe1_h", build_moe_h, "moe1")
    p_d1 = _prog("moe1_d", build_moe_down, "moe1")
    r = _run(p_h1, [{"xT": xs[c], "rwT": rwT1, "gbRow": gb1, "w1T": w1T,
                     "b1t": b1t, "identD": ident} for c in range(NCORES)])
    r = _run(p_d1, [{"xT": xs[c], "hD": np.asarray(r[c]["hD"]),
                     "coefD": np.asarray(r[c]["coefD"], f32), "wd": w2T,
                     "b2Row": b2r, "identD": ident} for c in range(NCORES)])
    xs = [np.asarray(r[c]["xoT"], f32) for c in range(NCORES)]

    out = np.concatenate(xs, axis=1)
    return np.ascontiguousarray(out.T).reshape(B, T, D).astype(np.float32)


# version 05
# speedup vs baseline: 1.3476x; 1.3476x over previous
import numpy as np
import ml_dtypes
import concourse.bass as bass
import concourse.mybir as mybir
from concourse.bass_utils import run_bass_kernel_spmd
from concourse.alu_op_type import AluOpType as AL
from contextlib import ExitStack

F32 = mybir.dt.float32
BF16 = mybir.dt.bfloat16
AF = mybir.ActivationFunctionType
AX = mybir.AxisListType
BF = ml_dtypes.bfloat16

NCORES = 8
TOK = 2048
D = 1024
NCH = D // 128
B, T, H, HD = 4, 4096, 16, 64
NKB = T // 128
EPS = 1e-5
PRUNE_EXP = 93.0


def alibi_slopes(n=16):
    import math
    closest = 2 ** math.floor(math.log2(n))
    base = 2 ** (-2 ** (-(math.log2(closest) - 3)))
    slopes = base ** np.arange(1, closest + 1)
    if closest != n:
        eb = 2 ** (-2 ** (-(math.log2(2 * closest) - 3)))
        extra = eb ** np.arange(1, 2 * (n - closest) + 1, 2)
        slopes = np.concatenate([slopes, extra])
    return slopes.astype(np.float64)



class Prog:
    KEYS = ("t", "v", "s", "g", "S", "G", "A")
    ENG = {"t": "tensor", "v": "vector", "s": "scalar", "g": "gpsimd",
           "S": "sync", "G": "gpsimd", "A": "scalar"}
    DMA = {"S", "G", "A"}

    def __init__(self):
        self.nc = bass.Bass("TRN2", target_bir_lowering=False)
        self.order = {e: [] for e in ("tensor", "vector", "scalar", "gpsimd", "sync")}
        self.cnt = dict.fromkeys(self.KEYS, 0)
        self.engd = dict(self.ENG)
        self.dmaset = set(self.DMA)
        self.waited = {e: dict.fromkeys(self.KEYS, 0) for e in self.order}
        self.floor = {}

    def key(self, queue, name):
        if name in self.cnt:
            return name
        self.engd[name] = self.ENG[queue]
        self.dmaset.add(name)
        self.cnt[name] = 0
        for e in self.waited:
            self.waited[e][name] = 0
        return name

    def barrier(self, keys=("t", "v", "s", "g", "S", "G", "A")):
        for k in keys:
            if self.cnt[k] > self.floor.get(k, 0):
                self.floor[k] = self.cnt[k]

    def op(self, key, fn, waits=(), inc=True):
        eng = self.engd[key]
        w = {}
        for tokn in flat_toks(waits) + list(self.floor.items()):
            if tokn is None:
                continue
            k, v = tokn
            if v > self.waited[eng].get(k, 0) and w.get(k, 0) < v:
                w[k] = v
        for k, v in w.items():
            self.waited[eng][k] = v
        amt = (16 if key in self.dmaset else 1) if inc else 0
        self.cnt[key] += amt
        self.order[eng].append((tuple(w.items()), fn, key, amt))
        return (key, self.cnt[key])

    def finalize(self, es):
        nc = self.nc
        sems = {k: es.enter_context(nc.semaphore("sem_" + k)) for k in self.cnt}
        block = es.enter_context(nc.Block())

        def mk(eng):
            def body(h):
                for waits, fn, key, amt in self.order[eng]:
                    for k, v in waits:
                        h.wait_ge(sems[k], v)
                    inst = fn(h)
                    if amt:
                        inst.then_inc(sems[key], amt)
            return body

        for eng in self.order:
            if self.order[eng]:
                getattr(block, eng)(mk(eng))


def flat_toks(toks):
    out = []
    def add(t):
        if t is None:
            return
        if isinstance(t, tuple) and len(t) == 2 and isinstance(t[0], str):
            out.append(t)
            return
        for x in t:
            add(x)
    add(list(toks))
    return out


def last(*toks):
    return flat_toks(toks)


class Ctx(dict):
    pass


def setup_common(p, es, c, npool=6):
    nc = p.nc
    c["ones_col"] = es.enter_context(nc.sbuf_tensor("cm_ones_col", [128, 1], BF16))
    c["ones_row"] = es.enter_context(nc.sbuf_tensor("cm_ones_row", [1, 128], BF16))
    c["eps1"] = es.enter_context(nc.sbuf_tensor("cm_eps1", [1, 1], F32))
    c["ps"] = [es.enter_context(nc.psum_tensor(f"cm_ps{i}", [128, 512], F32))
               for i in range(npool)]
    c["ps_free"] = {}
    t0 = p.op("g", lambda g: g.memset(c["ones_col"][:], 1.0))
    t1 = p.op("g", lambda g: g.memset(c["ones_row"][:], 1.0))
    t2 = p.op("g", lambda g: g.memset(c["eps1"][:], EPS))
    c["ones_ready"] = [t0, t1, t2]


def load_tile(p, es, name, shape, dtype, src_ap, queue=None):
    t = es.enter_context(p.nc.sbuf_tensor(name, shape, dtype))
    if queue is None:
        queue = "S"
    k = p.key(queue, "setup_" + queue)
    tok = p.op(k, lambda h: h.dma_start(out=t[:], in_=src_ap))
    return t, tok


def setup_bar(p):
    return [(k, p.cnt[k]) for k in ("setup_S", "setup_G", "setup_A") if k in p.cnt]


def all_dma_bar(p):
    return [(k, p.cnt[k]) for k in p.cnt if k in p.dmaset and p.cnt[k] > 0]


def load_x(p, es, c, x_dram, name="x", dtype=F32, nch=NCH, tok=TOK):
    nc = p.nc
    q = "S" if dtype == F32 else "G"
    k = p.key(q, "ldx_" + name)
    tiles = []
    tok_last = None
    for cc in range(nch):
        t = es.enter_context(nc.sbuf_tensor(f"{name}{cc}", [128, tok], dtype))
        tiles.append(t)
        tok_last = p.op(k, lambda h, cc=cc, t=t: h.dma_start(
            out=t[:], in_=x_dram[cc * 128:(cc + 1) * 128, :]))
    return tiles, [tok_last] * nch



def ln_scratch(p, es, name, tok=TOK):
    nc = p.nc
    sc = {}
    for nm, shape, dt in [("xb", [128, 512], BF16), ("xb2", [128, 512], BF16),
                          ("rowA", [1, tok], F32), ("rowB", [1, tok], F32),
                          ("rowC", [1, tok], F32),
                          ("rstdb", [1, tok], BF16), ("mrsb", [1, tok], BF16),
                          ("rbc", [128, tok], BF16), ("mbc", [128, tok], BF16)]:
        sc[nm] = es.enter_context(nc.sbuf_tensor(f"{name}_{nm}", shape, dt))
    sc["busy"] = None
    return sc


def emit_ln(p, es, c, name, x_tiles, x_toks, out_tiles, w_ap=None, b_ap=None,
            tok=TOK, param_toks=(), sc=None, t1_tiles=None, t1_toks=None):
    nc = p.nc
    ntt = tok // 512
    if sc is None:
        sc = ln_scratch(p, es, name, tok)
    if t1_tiles is None:
        t1_tiles = x_tiles
        t1_toks = [None] * NCH
    ps_s = c["ps"][2][0:1, :]
    ps_b = c["ps"][3]
    pf = c["ps_free"]

    s1_toks, s2_toks = [], []
    ev = sc["busy"]
    cast_done = {}
    for which, func, row in ((0, AF.Copy, "rowA"), (1, AF.Square, "rowB")):
        for tt in range(ntt):
            sl = slice(tt * 512, (tt + 1) * 512)
            m = None
            for cc in range(NCH):
                xb = sc["xb"] if (tt * NCH + cc) % 2 == 0 else sc["xb2"]
                ct = p.op("s", lambda s, cc=cc, sl=sl, xb=xb, func=func: s.activation(
                    out=xb[:], in_=x_tiles[cc][:, sl], func=func),
                    waits=last([x_toks[cc]], [cast_done.get(id(xb))]))
                m = p.op("t", lambda t, xb=xb, cc=cc: t.matmul(
                    ps_s, c["ones_col"][:], xb[:],
                    start=(cc == 0), stop=(cc == NCH - 1)),
                    waits=last(c["ones_ready"], [ct], [ev], [pf.get(2)]), inc=True)
                cast_done[id(xb)] = m
            ev = p.op("v", lambda v, sl=sl, row=row: v.tensor_copy(
                sc[row][:, sl], ps_s), waits=last([m], [sc["busy"]]))
            pf[2] = ev
            (s1_toks if which == 0 else s2_toks).append(ev)

    t_mu = p.op("v", lambda v: v.tensor_scalar(out=sc["rowA"][:], in0=sc["rowA"][:],
                scalar1=1.0 / D, scalar2=None, op0=AL.mult), waits=s1_toks)
    t_msq = p.op("v", lambda v: v.tensor_tensor(out=sc["rowC"][:], in0=sc["rowA"][:],
                 in1=sc["rowA"][:], op=AL.mult), waits=last([t_mu], [sc["busy"]]))
    t_var = p.op("v", lambda v: v.scalar_tensor_tensor(out=sc["rowB"][:], in0=sc["rowB"][:],
                 scalar=1.0 / D, in1=sc["rowC"][:], op0=AL.mult, op1=AL.subtract),
                 waits=last(s2_toks, [t_msq]))
    t_std = p.op("s", lambda s: s.activation(out=sc["rowC"][:], in_=sc["rowB"][:],
                 func=AF.Sqrt, bias=c["eps1"][:]),
                 waits=last([t_var], c["ones_ready"]))
    t_rstd = p.op("v", lambda v: v.reciprocal(sc["rowB"][:], sc["rowC"][:]), waits=[t_std])
    t_rb = p.op("v", lambda v: v.tensor_copy(sc["rstdb"][:], sc["rowB"][:]), waits=[t_rstd])
    t_mrs = p.op("v", lambda v: v.tensor_tensor(out=sc["rowA"][:], in0=sc["rowA"][:],
                 in1=sc["rowB"][:], op=AL.mult), waits=[t_rstd])
    t_mb = p.op("v", lambda v: v.tensor_copy(sc["mrsb"][:], sc["rowA"][:]), waits=[t_mrs])

    bc_toks = []
    evp = None
    for tt in range(ntt):
        sl = slice(tt * 512, (tt + 1) * 512)
        m = p.op("t", lambda t, sl=sl: t.matmul(ps_b[:], c["ones_row"][:],
                 sc["rstdb"][:, sl], start=True, stop=True),
                 waits=last([t_rb], [evp], c["ones_ready"], [pf.get(3)]))
        evp = p.op("v", lambda v, sl=sl: v.tensor_copy(sc["rbc"][:, sl], ps_b[:]),
                   waits=last([m], [sc["busy"]]))
        m2 = p.op("t", lambda t, sl=sl: t.matmul(ps_b[:], c["ones_row"][:],
                  sc["mrsb"][:, sl], start=True, stop=True),
                  waits=last([t_mb], [evp]))
        evp = p.op("v", lambda v, sl=sl: v.tensor_copy(sc["mbc"][:, sl], ps_b[:]),
                   waits=[m2])
        pf[3] = evp
        bc_toks.append(evp)

    out_toks = []
    for cc in range(NCH):
        tt1 = p.op("v", lambda v, cc=cc: v.tensor_tensor(out=t1_tiles[cc][:],
                   in0=x_tiles[cc][:], in1=sc["rbc"][:], op=AL.mult),
                   waits=last(bc_toks, [x_toks[cc]], [t1_toks[cc]],
                              out_toks[-1:] if out_toks else []))
        if w_ap is not None:
            tt2 = p.op("v", lambda v, cc=cc: v.scalar_tensor_tensor(out=t1_tiles[cc][:],
                       in0=sc["mbc"][:], scalar=-1.0, in1=t1_tiles[cc][:],
                       op0=AL.mult, op1=AL.add), waits=[tt1])
            tt3 = p.op("v", lambda v, cc=cc: v.tensor_scalar(out=out_tiles[cc][:],
                       in0=t1_tiles[cc][:], scalar1=w_ap(cc), scalar2=b_ap(cc),
                       op0=AL.mult, op1=AL.add), waits=last([tt2], list(param_toks)))
            out_toks.append(tt3)
        else:
            tt2 = p.op("v", lambda v, cc=cc: v.scalar_tensor_tensor(out=out_tiles[cc][:],
                       in0=sc["mbc"][:], scalar=-1.0, in1=t1_tiles[cc][:],
                       op0=AL.mult, op1=AL.add), waits=[tt1])
            out_toks.append(tt2)
    sc["busy"] = out_toks[-1]
    return out_toks



def emit_proj_feat(p, es, c, name, xn_tiles, xn_toks, w_dram, n_out, evac,
                   tok=TOK, nch_in=NCH, extra_mm=None, ps_idx=(0, 1)):
    nc = p.nc
    ntt = tok // 512
    noc = n_out // 128
    wbufs = [es.enter_context(nc.sbuf_tensor(f"{name}_w{i}", [128, nch_in, 128], BF16))
             for i in range(2)]
    wk = [p.key("G", f"{name}_wk{i}") for i in range(2)]
    pss = [c["ps"][ps_idx[0]], c["ps"][ps_idx[1]]]
    wlast = [None, None]
    plast = [c["ps_free"].get(ps_idx[0]), c["ps_free"].get(ps_idx[1])]
    wsrc = w_dram.rearrange("(dc pp) o -> pp dc o", pp=128)
    toks = []
    for oc in range(noc):
        slot = oc % 2
        wt = p.op(wk[slot], lambda g, oc=oc, slot=slot: g.dma_start(
            out=wbufs[slot][:], in_=wsrc[:, :, oc * 128:(oc + 1) * 128]),
            waits=last([wlast[slot]]))
        mm_last = None
        for tt in range(ntt):
            psl = (oc * ntt + tt) % 2
            sl = slice(tt * 512, (tt + 1) * 512)
            m = None
            for dc in range(nch_in):
                stop = (dc == nch_in - 1) and extra_mm is None
                m = p.op("t", lambda t, dc=dc, sl=sl, slot=slot, psl=psl, stop=stop:
                         t.matmul(pss[psl][:], wbufs[slot][:, dc, :],
                                  xn_tiles[dc][:, sl], start=(dc == 0), stop=stop),
                         waits=last([wt], [xn_toks[dc]], [plast[psl]]),
                         inc=(dc == nch_in - 1 and extra_mm is None))
            if extra_mm is not None:
                m = extra_mm(p, oc, tt, pss[psl])
            ev = evac(oc, tt, pss[psl], m)
            plast[psl] = ev
            c["ps_free"][ps_idx[psl]] = ev
            toks.append(ev)
            mm_last = m
        wlast[slot] = mm_last
    return toks


def emit_proj_tok(p, es, c, name, xn_tiles, xn_toks, w_dram, n_out, evac,
                  bias_row=None, bias_tok=None, tok=TOK, nch_in=NCH, ps_idx=(4, 5)):
    nc = p.nc
    ntt = tok // 128
    nvc = n_out // 512
    wbufs = [es.enter_context(nc.sbuf_tensor(f"{name}_w{i}", [128, nch_in, 512], BF16))
             for i in range(2)]
    wk = [p.key("G", f"{name}_wk{i}") for i in range(2)]
    pss = [c["ps"][ps_idx[0]], c["ps"][ps_idx[1]]]
    wlast = [None, None]
    plast = [c["ps_free"].get(ps_idx[0]), c["ps_free"].get(ps_idx[1])]
    wsrc = w_dram.rearrange("(dc pp) o -> pp dc o", pp=128)
    toks = []
    for vc in range(nvc):
        slot = vc % 2
        wt = p.op(wk[slot], lambda g, vc=vc, slot=slot: g.dma_start(
            out=wbufs[slot][:], in_=wsrc[:, :, vc * 512:(vc + 1) * 512]),
            waits=last([wlast[slot]]))
        mm_last = None
        for tt in range(ntt):
            psl = (vc * ntt + tt) % 2
            tsl = slice(tt * 128, (tt + 1) * 128)
            m = None
            for dc in range(nch_in):
                is_last = (dc == nch_in - 1) and bias_row is None
                m = p.op("t", lambda t, dc=dc, tsl=tsl, slot=slot, psl=psl,
                         is_last=is_last:
                         t.matmul(pss[psl][:], xn_tiles[dc][:, tsl],
                                  wbufs[slot][:, dc, :],
                                  start=(dc == 0), stop=is_last),
                         waits=last([wt], [xn_toks[dc]], [plast[psl]]),
                         inc=is_last)
            if bias_row is not None:
                m = p.op("t", lambda t, vc=vc, psl=psl: t.matmul(
                    pss[psl][:], c["ones_row"][:],
                    bias_row[:, vc * 512:(vc + 1) * 512], start=False, stop=True),
                    waits=last([bias_tok], c["ones_ready"]))
            ev = evac(tt, vc, pss[psl], m)
            plast[psl] = ev
            c["ps_free"][ps_idx[psl]] = ev
            toks.append(ev)
            mm_last = m
        wlast[slot] = mm_last
    return toks



def emit_rope(p, es, c, name, strip, ready_toks, cosb, sinb, const_toks, aux, aux_free):
    done = []
    rk = p.key("S", "ropek")
    for rb in (0, 64):
        d1 = p.op(rk, lambda h, rb=rb: h.dma_start(
            out=aux[rb:rb + 16, :], in_=strip[rb + 16:rb + 32, :]),
            waits=last(ready_toks, [aux_free]))
        d2 = p.op(rk, lambda h, rb=rb: h.dma_start(
            out=aux[rb + 16:rb + 32, :], in_=strip[rb:rb + 16, :]),
            waits=last(ready_toks, [aux_free]))
        t1 = p.op("v", lambda v, rb=rb: v.tensor_tensor(out=aux[rb:rb + 32, :],
                  in0=aux[rb:rb + 32, :], in1=sinb[rb:rb + 32, :], op=AL.mult),
                  waits=last([d1], [d2], const_toks))
        t2 = p.op("v", lambda v, rb=rb: v.tensor_tensor(out=strip[rb:rb + 32, :],
                  in0=strip[rb:rb + 32, :], in1=cosb[rb:rb + 32, :], op=AL.mult),
                  waits=last(ready_toks, [d1], [d2]))
        t3 = p.op("v", lambda v, rb=rb: v.tensor_tensor(out=strip[rb:rb + 32, :],
                  in0=strip[rb:rb + 32, :], in1=aux[rb:rb + 32, :], op=AL.subtract),
                  waits=[t1, t2])
        aux_free = t3
        done.append(t3)
    return done, aux_free



def emit_qkv_part(p, es, c, name, xn_tiles, xn_toks, wiT, bqk_sb, bqk_tok,
                  bvb, bv_tok, qT, kT, vTok, rope=None, tok=TOK):
    nc = p.nc
    strips = [es.enter_context(nc.sbuf_tensor(f"{name}_qks{i}", [128, tok], BF16))
              for i in range(2)]
    sk = [p.key("S", f"{name}_sk{i}") for i in range(2)]
    strip_free = [None, None]
    state = {"aux_free": None, "per_oc": {}}

    def evac_qk(oc, tt, ps, mm_tok):
        srot = oc % 2
        sl = slice(tt * 512, (tt + 1) * 512)
        scale = 0.125 if oc < 8 else 1.0
        ev = p.op("s", lambda s, oc=oc, sl=sl, ps=ps, srot=srot, scale=scale:
                  s.activation(out=strips[srot][:, sl], in_=ps[:], func=AF.Identity,
                               bias=bqk_sb[:, oc:oc + 1], scale=scale),
                  waits=last([mm_tok], [bqk_tok], [strip_free[srot]]))
        state["per_oc"].setdefault(oc, []).append(ev)
        if tt == tok // 512 - 1:
            evs = state["per_oc"].pop(oc)
            rtoks = evs
            if rope is not None:
                rt, state["aux_free"] = emit_rope(
                    p, es, c, f"{name}_rope{oc}", strips[srot], evs,
                    rope["cosb"], rope["sinb"], rope["const_toks"],
                    rope["aux"], state["aux_free"])
                rtoks = rt + evs
            dst = qT if oc < 8 else kT
            r0 = (oc % 8) * 128
            dtok = p.op(sk[srot], lambda h, dst=dst, r0=r0, srot=srot: h.dma_start(
                out=dst[r0:r0 + 128, :], in_=strips[srot][:]), waits=rtoks)
            strip_free[srot] = dtok
        return ev

    emit_proj_feat(p, es, c, name + "_qk", xn_tiles, xn_toks, wiT[:, 0:2 * D], 2 * D,
                   evac_qk, tok=tok)

    vtiles = [es.enter_context(nc.sbuf_tensor(f"{name}_vt{i}", [128, 512], BF16))
              for i in range(2)]
    vk = [p.key("S", f"{name}_vk{i}") for i in range(2)]
    vfree = [None, None]

    def evac_v(tt, vc, ps, mm_tok):
        rot = (vc * (tok // 128) + tt) % 2
        ev = p.op("s", lambda s, ps=ps, rot=rot: s.activation(
            out=vtiles[rot][:], in_=ps[:], func=AF.Copy),
            waits=last([mm_tok], [vfree[rot]]))
        dtok = p.op(vk[rot], lambda h, tt=tt, vc=vc, rot=rot: h.dma_start(
            out=vTok[tt * 128:(tt + 1) * 128, vc * 512:(vc + 1) * 512],
            in_=vtiles[rot][:]), waits=[ev])
        vfree[rot] = dtok
        return ev

    emit_proj_tok(p, es, c, name + "_v", xn_tiles, xn_toks, wiT[:, 2 * D:3 * D], D,
                  evac_v, bias_row=bvb, bias_tok=bv_tok, tok=tok)


def load_rope(p, es, c, cosT, sinT, tok=TOK):
    cosb, cos_tok = load_tile(p, es, "sb_cosb", [128, tok], BF16, cosT, queue="G")
    sinb, sin_tok = load_tile(p, es, "sb_sinb", [128, tok], BF16, sinT, queue="G")
    aux = es.enter_context(p.nc.sbuf_tensor("ropeaux", [128, tok], BF16))
    return {"cosb": cosb, "sinb": sinb, "aux": aux, "const_toks": [cos_tok, sin_tok]}



def build_qkv(with_ln, with_rope, tok=TOK):
    p = Prog()
    nc = p.nc
    es = ExitStack()
    c = Ctx()
    xT = nc.dram_tensor("xT", [D, tok], F32, kind="ExternalInput").ap()
    wiT = nc.dram_tensor("wiT", [D, 3 * D], F32, kind="ExternalInput").ap()
    bqk = nc.dram_tensor("bqk", [128, 16], F32, kind="ExternalInput").ap()
    bv_row = nc.dram_tensor("bv_row", [1, D], F32, kind="ExternalInput").ap()
    if with_ln:
        lnp = nc.dram_tensor("lnp", [128, 16], F32, kind="ExternalInput").ap()
    if with_rope:
        cosT = nc.dram_tensor("cosT", [128, tok], F32, kind="ExternalInput").ap()
        sinT = nc.dram_tensor("sinT", [128, tok], F32, kind="ExternalInput").ap()
    qT = nc.dram_tensor("qT", [D, tok], BF16, kind="ExternalOutput").ap()
    kT = nc.dram_tensor("kT", [D, tok], BF16, kind="ExternalOutput").ap()
    vTok = nc.dram_tensor("vTok", [tok, D], BF16, kind="ExternalOutput").ap()

    with es:
        setup_common(p, es, c)
        x_tiles, x_toks = load_x(p, es, c, xT, "x", F32, NCH, tok)
        bqk_sb, _bt = load_tile(p, es, "sb_bqk", [128, 16], F32, bqk)
        bvb, _bv = load_tile(p, es, "sb_bvb", [1, D], BF16, bv_row, queue="G")
        rope = load_rope(p, es, c, cosT, sinT, tok) if with_rope else None
        if with_ln:
            lnp_sb, _lt = load_tile(p, es, "sb_lnp", [128, 16], F32, lnp)
        sbar = setup_bar(p)
        bqk_tok = p.op("v", lambda v: v.tensor_scalar(out=bqk_sb[:, 0:8],
                       in0=bqk_sb[:, 0:8], scalar1=0.125, scalar2=None,
                       op0=AL.mult), waits=sbar)
        bv_tok = sbar
        if rope is not None:
            rope["const_toks"] = sbar

        xn_tiles = [es.enter_context(nc.sbuf_tensor(f"xn{i}", [128, tok], BF16))
                    for i in range(NCH)]
        if with_ln:
            xn_toks = emit_ln(p, es, c, "ln1", x_tiles, x_toks, xn_tiles,
                              w_ap=lambda cc: lnp_sb[:, cc:cc + 1],
                              b_ap=lambda cc: lnp_sb[:, 8 + cc:8 + cc + 1],
                              tok=tok, param_toks=sbar)
        else:
            xn_toks = [p.op("s", lambda s, cc=cc: s.activation(
                out=xn_tiles[cc][:], in_=x_tiles[cc][:], func=AF.Copy),
                waits=[x_toks[cc]]) for cc in range(NCH)]

        emit_qkv_part(p, es, c, "qkv", xn_tiles, xn_toks, wiT, bqk_sb, bqk_tok,
                      bvb, bv_tok, qT, kT, vTok, rope=rope, tok=tok)
        p.op("S", lambda h: h.nop(), waits=all_dma_bar(p), inc=False)
        p.finalize(es)
    return p



def build_attn(tok_q=T, nunits=8):
    p = Prog()
    nc = p.nc
    es = ExitStack()
    c = Ctx()
    nu = nunits
    QA = nc.dram_tensor("QA", [nu, 65, T], BF16, kind="ExternalInput").ap()
    KA = nc.dram_tensor("KA", [nu, 65, T], BF16, kind="ExternalInput").ap()
    VA = nc.dram_tensor("VA", [nu, T, 65], BF16, kind="ExternalInput").ap()
    AB = nc.dram_tensor("AB", [nu, 128, 32], F32, kind="ExternalInput").ap()
    OT = nc.dram_tensor("OT", [nu, 64, T], BF16, kind="ExternalOutput").ap()

    NQT = T // 512

    with es:
        setup_common(p, es, c, npool=0)
        qa = [es.enter_context(nc.sbuf_tensor(f"qa{i}", [65, T], BF16)) for i in range(2)]
        ka = [es.enter_context(nc.sbuf_tensor(f"ka{i}", [65, T], BF16)) for i in range(2)]
        va = [es.enter_context(nc.sbuf_tensor(f"va{i}", [128, NKB, 65], BF16)) for i in range(2)]
        ab = [es.enter_context(nc.sbuf_tensor(f"ab{i}", [128, 32], F32)) for i in range(2)]
        ot = [es.enter_context(nc.sbuf_tensor(f"ot{i}", [64, T], BF16)) for i in range(2)]
        sq = es.enter_context(nc.sbuf_tensor("sq", [64, 512], BF16))
        nrm = es.enter_context(nc.sbuf_tensor("nrm", [1, 2 * T], F32))
        nb2 = es.enter_context(nc.sbuf_tensor("nb2", [1, 2], F32))
        mgsum = es.enter_context(nc.sbuf_tensor("mgsum", [1, 1], F32))
        mgb = es.enter_context(nc.sbuf_tensor("mgb", [1, 1], BF16))
        biasu = [es.enter_context(nc.sbuf_tensor(f"biasu{i}", [128, 32], F32))
                 for i in range(2)]
        psb = [es.enter_context(nc.sbuf_tensor(f"psb{i}", [128, 512], BF16)) for i in range(3)]
        rcp = es.enter_context(nc.sbuf_tensor("rcp", [1, 512], F32))
        rcpb = es.enter_context(nc.sbuf_tensor("rcpb", [1, 512], BF16))
        rbsb = es.enter_context(nc.sbuf_tensor("rbsb", [64, 512], BF16))

        ps_out = [es.enter_context(nc.psum_tensor(f"pso{i}", [65, 512], F32)) for i in range(2)]
        ps_s = [es.enter_context(nc.psum_tensor(f"pss{i}", [128, 512], F32)) for i in range(3)]
        ps_n = es.enter_context(nc.psum_tensor("psn", [1, 512], F32))
        ps_mg = es.enter_context(nc.psum_tensor("psmg", [128, 1], F32))
        ps_rb = es.enter_context(nc.psum_tensor("psrb", [64, 512], F32))

        in_free = [[None, None] for _ in range(4)]
        ot_free = [None, None]
        psb_free = [None] * 3
        pso_free = [None, None]
        pss_free = [None, None, None]
        bias_free = [None, None]
        last_tbias = None
        last_tmg = None

        otk = [p.key("S", f"otk{i}") for i in range(2)]
        qak = [p.key("S", f"qak{i}") for i in range(2)]
        kak = [p.key("S", f"kak{i}") for i in range(2)]
        vak = [p.key("G", f"vak{i}") for i in range(2)]
        abk = [p.key("G", f"abk{i}") for i in range(2)]
        for u in range(nu):
            rot = u % 2
            tqa = p.op(qak[rot], lambda h, u=u, rot=rot: h.dma_start(
                out=qa[rot][:], in_=QA[u]), waits=last([in_free[0][rot]]))
            tka = p.op(kak[rot], lambda h, u=u, rot=rot: h.dma_start(
                out=ka[rot][:], in_=KA[u]), waits=last([in_free[1][rot]]))
            tva = p.op(vak[rot], lambda g, u=u, rot=rot: g.dma_start(
                out=va[rot][:], in_=VA[u].rearrange("(kb pp) e -> pp kb e", pp=128)),
                waits=last([in_free[2][rot]]))
            tab = p.op(abk[rot], lambda g, u=u, rot=rot: g.dma_start(
                out=ab[rot][:], in_=AB[u]), waits=last([in_free[3][rot]]))

            prev_sqmm = None
            nsegs = T // 512
            for which, src, tsrc in ((0, qa[rot], tqa), (1, ka[rot], tka)):
                ev = None
                for seg in range(nsegs):
                    sl = slice(seg * 512, (seg + 1) * 512)
                    tsq = p.op("v", lambda v, src=src, sl=sl: v.tensor_tensor(
                        out=sq[:], in0=src[0:64, sl], in1=src[0:64, sl], op=AL.mult),
                        waits=last([tsrc], [prev_sqmm]))
                    m = p.op("t", lambda t: t.matmul(ps_n[:], c["ones_col"][0:64, :],
                             sq[:], start=True, stop=True),
                             waits=last([tsq], c["ones_ready"], [ev]))
                    prev_sqmm = m
                    ev = p.op("v", lambda v, which=which, seg=seg: v.tensor_copy(
                        nrm[:, (which * nsegs + seg) * 512:(which * nsegs + seg + 1) * 512],
                        ps_n[:]), waits=[m])
                p.op("v", lambda v, which=which: v.reduce_max(
                    nb2[:, which:which + 1],
                    nrm[:, which * nsegs * 512:(which + 1) * nsegs * 512],
                    axis=AX.X), waits=[ev])
            tadd = p.op("v", lambda v: v.tensor_tensor(out=mgsum[:], in0=nb2[:, 0:1],
                        in1=nb2[:, 1:2], op=AL.mult), waits=[("v", p.cnt["v"])])
            tln = p.op("s", lambda s: s.activation(out=mgsum[:], in_=mgsum[:],
                       func=AF.Ln), waits=[tadd])
            texp = p.op("s", lambda s: s.activation(out=mgsum[:], in_=mgsum[:],
                        func=AF.Exp, scale=0.5), waits=[tln])
            tmgb = p.op("v", lambda v: v.tensor_copy(mgb[:], mgsum[:]),
                        waits=last([texp], [last_tmg]))
            tmg = p.op("t", lambda t: t.matmul(ps_mg[:], c["ones_row"][:], mgb[:],
                       start=True, stop=True),
                       waits=last([tmgb], c["ones_ready"], [last_tbias]))
            last_tmg = tmg
            tbias = p.op("v", lambda v, rot=rot: v.tensor_scalar(out=biasu[rot][:],
                         in0=ab[rot][:], scalar1=ps_mg[:], scalar2=None,
                         op0=AL.subtract),
                         waits=last([tmg], [tab], [bias_free[rot]]))
            last_tbias = tbias
            in_free[3][rot] = tbias

            strip_toks = []
            for qt in range(NQT):
                orot = qt % 2
                qsl = slice(qt * 512, (qt + 1) * 512)
                nkb = 4 * qt + 4
                av_tok = None
                ms_toks = {}

                def emit_ms(kb, qsl=qsl, rot=rot):
                    srot = kb % 3
                    ksl = slice(kb * 128, (kb + 1) * 128)
                    ms_toks[kb] = p.op("t", lambda t, rot=rot, ksl=ksl, qsl=qsl,
                                       srot=srot:
                                       t.matmul(ps_s[srot][:], ka[rot][:, ksl],
                                                qa[rot][:, qsl], start=True, stop=True),
                                       waits=last([tka], [tqa], [pss_free[srot]]))

                emit_ms(0)
                if nkb > 1:
                    emit_ms(1)
                for kb in range(nkb):
                    if kb + 2 < nkb:
                        emit_ms(kb + 2)
                    srot = kb % 3
                    prot = kb % 3
                    te = p.op("s", lambda s, srot=srot, prot=prot, kb=kb, rot=rot:
                              s.activation(out=psb[prot][:], in_=ps_s[srot][:],
                                           func=AF.Exp, bias=biasu[rot][:, kb:kb + 1]),
                              waits=last([ms_toks[kb]], [tbias], [psb_free[prot]]))
                    pss_free[srot] = te
                    ptok = te
                    if kb >= 4 * qt:
                        ptok = p.op("g", lambda g, prot=prot, qt=qt, kb=kb:
                                    g.affine_select(out=psb[prot][:], in_=psb[prot][:],
                                                    pattern=[[1, 512]],
                                                    compare_op=AL.is_ge, fill=0.0,
                                                    base=qt * 512 - kb * 128,
                                                    channel_multiplier=-1),
                                    waits=[te])
                    av_tok = p.op("t", lambda t, rot=rot, kb=kb, orot=orot, prot=prot,
                                  nkb=nkb: t.matmul(
                                      ps_out[orot][:],
                                      va[rot][:, kb, :],
                                      psb[prot][:], start=(kb == 0), stop=(kb == nkb - 1)),
                                  waits=last([ptok], [tva], [pso_free[orot]]),
                                  inc=(kb == nkb - 1))
                    psb_free[prot] = av_tok
                trc = p.op("v", lambda v, orot=orot: v.reciprocal(
                    rcp[:], ps_out[orot][64:65, :]), waits=[av_tok])
                trcb = p.op("v", lambda v: v.tensor_copy(rcpb[:], rcp[:]), waits=[trc])
                tmb = p.op("t", lambda t: t.matmul(ps_rb[:], c["ones_row"][:, 0:64],
                           rcpb[:], start=True, stop=True),
                           waits=last([trcb], c["ones_ready"], [("v", p.cnt["v"])]))
                trb = p.op("v", lambda v: v.tensor_copy(rbsb[:], ps_rb[:]), waits=[tmb])
                tst = p.op("v", lambda v, orot=orot, qsl=qsl, rot=rot:
                           v.scalar_tensor_tensor(out=ot[rot][:, qsl],
                                                  in0=ps_out[orot][0:64, :], scalar=1.0,
                                                  in1=rbsb[:], op0=AL.mult, op1=AL.mult),
                           waits=last([trb], [ot_free[rot]]))
                pso_free[orot] = tst
                strip_toks.append(tst)
            dout = p.op(otk[rot], lambda h, u=u, rot=rot: h.dma_start(
                out=OT[u], in_=ot[rot][:]), waits=strip_toks)
            ot_free[rot] = dout
            in_free[0][rot] = av_tok
            in_free[1][rot] = av_tok
            in_free[2][rot] = av_tok
            bias_free[rot] = av_tok

        p.op("S", lambda h: h.nop(), waits=all_dma_bar(p), inc=False)
        p.finalize(es)
    return p



def emit_ffn(p, es, c, name, xn_tiles, xn_toks, f1T, b1_sb, b1_tok, f2T,
             b2_sb, b2_tok, x_tiles, x_toks, tok=TOK):
    nc = p.nc
    NH = 32
    TTH = 512
    h_sb = es.enter_context(nc.sbuf_tensor(name + "_h", [128, NH, TTH], BF16))
    w1b = [es.enter_context(nc.sbuf_tensor(f"{name}_w1{i}", [128, NCH, 128], BF16))
           for i in range(2)]
    w2b = [es.enter_context(nc.sbuf_tensor(f"{name}_w2{i}", [128, NH, 128], BF16))
           for i in range(2)]
    w1k = [p.key("G", f"{name}_w1k{i}") for i in range(2)]
    w2k = [p.key("G", f"{name}_w2k{i}") for i in range(2)]
    pss = [c["ps"][0], c["ps"][1]]
    f1src = f1T.rearrange("(dc pp) o -> pp dc o", pp=128)
    f2src = f2T.rearrange("(hc pp) o -> pp hc o", pp=128)
    w1_free = [None, None]
    w2_free = [None, None]
    ps_free = [c["ps_free"].get(0), c["ps_free"].get(1)]
    h_free = [None] * NH
    new_x_toks = list(x_toks)
    for tp in range(tok // TTH):
        tsl0 = tp * TTH
        for hc in range(NH):
            ws = hc % 2
            wt = p.op(w1k[ws], lambda g, hc=hc, ws=ws: g.dma_start(
                out=w1b[ws][:], in_=f1src[:, :, hc * 128:(hc + 1) * 128]),
                waits=last([w1_free[ws]]))
            for sub in range(TTH // 512):
                psl = (hc * 2 + sub) % 2
                sl = slice(tsl0 + sub * 512, tsl0 + (sub + 1) * 512)
                m = None
                for dc in range(NCH):
                    m = p.op("t", lambda t, dc=dc, sl=sl, ws=ws, psl=psl:
                             t.matmul(pss[psl][:], w1b[ws][:, dc, :],
                                      xn_tiles[dc][:, sl], start=(dc == 0),
                                      stop=(dc == NCH - 1)),
                             waits=last([wt], [xn_toks[dc]], [ps_free[psl]]),
                             inc=(dc == NCH - 1))
                ev = p.op("s", lambda s, hc=hc, sub=sub, psl=psl: s.activation(
                    out=h_sb[:, hc, sub * 512:(sub + 1) * 512], in_=pss[psl][:],
                    func=AF.Gelu, bias=b1_sb[:, hc:hc + 1]),
                    waits=last([m], [b1_tok], [h_free[hc]]))
                ps_free[psl] = ev
                c["ps_free"][psl] = ev
                if sub == TTH // 512 - 1:
                    w1_free[ws] = m
        ev_h_last = None
        for oc in range(NCH):
            ws = oc % 2
            wt = p.op(w2k[ws], lambda g, oc=oc, ws=ws: g.dma_start(
                out=w2b[ws][:], in_=f2src[:, :, oc * 128:(oc + 1) * 128]),
                waits=last([w2_free[ws]]))
            for sub in range(TTH // 512):
                psl = (oc * 2 + sub) % 2
                sl = slice(tsl0 + sub * 512, tsl0 + (sub + 1) * 512)
                ssub = slice(sub * 512, (sub + 1) * 512)
                m = None
                for hc in range(NH):
                    m = p.op("t", lambda t, hc=hc, sl=sl, ssub=ssub, ws=ws, psl=psl:
                             t.matmul(pss[psl][:], w2b[ws][:, hc, :],
                                      h_sb[:, hc, ssub], start=(hc == 0),
                                      stop=(hc == NH - 1)),
                             waits=last([wt], [("s", p.cnt["s"])] if hc == 0 else [],
                                        [ps_free[psl]]),
                             inc=(hc == NH - 1))
                ev = p.op("v", lambda v, oc=oc, sl=sl, psl=psl:
                          v.scalar_tensor_tensor(out=x_tiles[oc][:, sl],
                                                 in0=pss[psl][:],
                                                 scalar=b2_sb[:, oc:oc + 1],
                                                 in1=x_tiles[oc][:, sl],
                                                 op0=AL.add, op1=AL.add),
                          waits=last([m], [b2_tok], [new_x_toks[oc]]))
                ps_free[psl] = ev
                c["ps_free"][psl] = ev
                new_x_toks[oc] = ev
                ev_h_last = m
            w2_free[ws] = ev_h_last
        for hc in range(NH):
            h_free[hc] = ev_h_last
    return new_x_toks


def build_tok(kind, with_rope=False, tok=TOK):
    p = Prog()
    nc = p.nc
    es = ExitStack()
    c = Ctx()
    xT = nc.dram_tensor("xT", [D, tok], F32, kind="ExternalInput").ap()
    attnT = nc.dram_tensor("attnT", [D, tok], BF16, kind="ExternalInput").ap()
    woT = nc.dram_tensor("woT", [D, D], F32, kind="ExternalInput").ap()
    bo = nc.dram_tensor("bo", [128, 8], F32, kind="ExternalInput").ap()
    if kind in ("inner", "final"):
        ln2p = nc.dram_tensor("ln2p", [128, 16], F32, kind="ExternalInput").ap()
        f1T = nc.dram_tensor("f1T", [D, 4 * D], F32, kind="ExternalInput").ap()
        b1 = nc.dram_tensor("b1", [128, 32], F32, kind="ExternalInput").ap()
        f2T = nc.dram_tensor("f2T", [4 * D, D], F32, kind="ExternalInput").ap()
        b2 = nc.dram_tensor("b2", [128, 8], F32, kind="ExternalInput").ap()
    if kind == "inner":
        ln1p = nc.dram_tensor("ln1p", [128, 16], F32, kind="ExternalInput").ap()
        wiT = nc.dram_tensor("wiT", [D, 3 * D], F32, kind="ExternalInput").ap()
        bqk = nc.dram_tensor("bqk", [128, 16], F32, kind="ExternalInput").ap()
        bv_row = nc.dram_tensor("bv_row", [1, D], F32, kind="ExternalInput").ap()
        qTd = nc.dram_tensor("qT", [D, tok], BF16, kind="ExternalOutput").ap()
        kTd = nc.dram_tensor("kT", [D, tok], BF16, kind="ExternalOutput").ap()
        vTokd = nc.dram_tensor("vTok", [tok, D], BF16, kind="ExternalOutput").ap()
    if kind == "final":
        lnfp = nc.dram_tensor("lnfp", [128, 16], F32, kind="ExternalInput").ap()
        xinT = nc.dram_tensor("xinT", [D, tok], F32, kind="ExternalInput").ap()
    xoT = nc.dram_tensor("xoT", [D, tok], F32, kind="ExternalOutput").ap()

    with es:
        setup_common(p, es, c)
        x_tiles, x_toks = load_x(p, es, c, xT, "x", F32, NCH, tok)
        bo_sb, _ = load_tile(p, es, "sb_bo", [128, 8], F32, bo)
        if kind in ("inner", "final"):
            ln2_sb, _ = load_tile(p, es, "sb_ln2", [128, 16], F32, ln2p)
            b1_sb, _ = load_tile(p, es, "sb_b1", [128, 32], F32, b1)
            b2_sb, _ = load_tile(p, es, "sb_b2", [128, 8], F32, b2)
        if kind == "inner":
            ln1_sb, _ = load_tile(p, es, "sb_ln1", [128, 16], F32, ln1p)
            bqk_sb, _ = load_tile(p, es, "sb_bqk", [128, 16], F32, bqk)
            bvb, _ = load_tile(p, es, "sb_bvb", [1, D], BF16, bv_row, queue="G")
        if kind == "final":
            lnf_sb, _ = load_tile(p, es, "sb_lnf", [128, 16], F32, lnfp)
        sbar = setup_bar(p)
        bo_tok = sbar
        ln2_tok = b1_tok = b2_tok = ln1_tok = lnf_tok = bv_tok = sbar
        if kind == "inner":
            bqk_tok = p.op("v", lambda v: v.tensor_scalar(out=bqk_sb[:, 0:8],
                           in0=bqk_sb[:, 0:8], scalar1=0.125, scalar2=None,
                           op0=AL.mult), waits=sbar)
        t1s = es.enter_context(nc.sbuf_tensor("t1s", [128, tok], F32))
        lnsc = ln_scratch(p, es, "lnsc", tok)

        def evac_resid(oc, tt, ps, mm_tok):
            sl = slice(tt * 512, (tt + 1) * 512)
            ev = p.op("v", lambda v, oc=oc, sl=sl, ps=ps: v.scalar_tensor_tensor(
                out=x_tiles[oc][:, sl], in0=ps[:], scalar=bo_sb[:, oc:oc + 1],
                in1=x_tiles[oc][:, sl], op0=AL.add, op1=AL.add),
                waits=last([mm_tok], [bo_tok], [x_toks[oc]]))
            x_toks[oc] = ev
            return ev

        with ExitStack() as es2:
            at_tiles, at_toks = load_x(p, es2, c, attnT, "at", BF16, NCH, tok)
            emit_proj_feat(p, es2, c, "wo", at_tiles, at_toks, woT, D, evac_resid, tok=tok)
        p.barrier()

        xn_tiles = [es.enter_context(nc.sbuf_tensor(f"xn{i}", [128, tok], BF16))
                    for i in range(NCH)]

        if kind in ("inner", "final"):
            xn_toks = emit_ln(p, es, c, "ln2", x_tiles, x_toks, xn_tiles,
                              w_ap=lambda cc: ln2_sb[:, cc:cc + 1],
                              b_ap=lambda cc: ln2_sb[:, 8 + cc:8 + cc + 1],
                              tok=tok, param_toks=ln2_tok, sc=lnsc,
                              t1_tiles=[t1s] * NCH, t1_toks=[None] * NCH)
            with ExitStack() as es3:
                x_toks = emit_ffn(p, es3, c, "ffn", xn_tiles, xn_toks, f1T, b1_sb,
                                  b1_tok, f2T, b2_sb, b2_tok, x_tiles, x_toks, tok=tok)
            p.barrier()

        if kind == "inner":
            xn_toks = emit_ln(p, es, c, "ln1n", x_tiles, x_toks, xn_tiles,
                              w_ap=lambda cc: ln1_sb[:, cc:cc + 1],
                              b_ap=lambda cc: ln1_sb[:, 8 + cc:8 + cc + 1],
                              tok=tok, param_toks=ln1_tok, sc=lnsc,
                              t1_tiles=[t1s] * NCH, t1_toks=[None] * NCH)
            emit_qkv_part(p, es, c, "qkv", xn_tiles, xn_toks, wiT, bqk_sb, bqk_tok,
                          bvb, bv_tok, qTd, kTd, vTokd, rope=None, tok=tok)
        elif kind == "final":
            y_toks = emit_ln(p, es, c, "lnf", x_tiles, x_toks, xn_tiles,
                             w_ap=lambda cc: lnf_sb[:, cc:cc + 1],
                             b_ap=lambda cc: lnf_sb[:, 8 + cc:8 + cc + 1],
                             tok=tok, param_toks=lnf_tok, sc=lnsc,
                             t1_tiles=[t1s] * NCH, t1_toks=[None] * NCH)
            xik = p.key("S", "xik")
            xin_last = None
            for cc in range(NCH):
                xin_last = p.op(xik, lambda h, cc=cc: h.dma_start(
                    out=x_tiles[cc][:], in_=xinT[cc * 128:(cc + 1) * 128, :]),
                    waits=[y_toks[cc], ("v", p.cnt["v"])])
            xin_toks = [xin_last] * NCH
            add_toks = [p.op("v", lambda v, cc=cc: v.tensor_tensor(
                out=x_tiles[cc][:], in0=x_tiles[cc][:], in1=xn_tiles[cc][:],
                op=AL.add), waits=[xin_toks[cc], y_toks[cc]]) for cc in range(NCH)]
            x_toks = emit_ln(p, es, c, "pln", x_tiles, add_toks, x_tiles,
                             tok=tok, sc=lnsc)
        elif kind == "wd":
            x_toks = emit_ln(p, es, c, "pln", x_tiles, x_toks, x_tiles,
                             tok=tok, sc=lnsc)

        xok = p.key("S", "xok")
        for cc in range(NCH):
            p.op(xok, lambda h, cc=cc: h.dma_start(
                out=xoT[cc * 128:(cc + 1) * 128, :], in_=x_tiles[cc][:]),
                waits=[x_toks[cc]])
        p.op("S", lambda h: h.nop(), waits=all_dma_bar(p), inc=False)
        p.finalize(es)
    return p



def emit_router_coef(p, es, c, name, xb_tiles, xb_toks, rw_sb, rw_tok, gb_sb,
                     gb_tok, mode, coefD, tok=TOK):
    nc = p.nc
    NE = 8
    lg = es.enter_context(nc.sbuf_tensor(name + "_lg", [128, NE], F32))
    l2 = es.enter_context(nc.sbuf_tensor(name + "_l2", [128, NE], F32))
    eq1 = es.enter_context(nc.sbuf_tensor(name + "_eq1", [128, NE], F32))
    eq2 = es.enter_context(nc.sbuf_tensor(name + "_eq2", [128, NE], F32))
    cf = es.enter_context(nc.sbuf_tensor(name + "_cf", [128, NE], F32))
    m1 = es.enter_context(nc.sbuf_tensor(name + "_m1", [128, 1], F32))
    m2 = es.enter_context(nc.sbuf_tensor(name + "_m2", [128, 1], F32))
    w1 = es.enter_context(nc.sbuf_tensor(name + "_w1", [128, 1], F32))
    w2 = es.enter_context(nc.sbuf_tensor(name + "_w2", [128, 1], F32))
    sm = es.enter_context(nc.sbuf_tensor(name + "_sm", [128, 1], F32))
    iot = es.enter_context(nc.sbuf_tensor(name + "_iot", [128, NE], F32))
    cfb = es.enter_context(nc.sbuf_tensor(name + "_cfb", [128, NE], BF16))
    wiRow = None
    wi_toks = []
    if mode == "moe1":
        wiRow = es.enter_context(nc.sbuf_tensor(name + "_wiR", [1, 8 * tok], BF16))
        pst = es.enter_context(nc.psum_tensor(name + "_pst", [1, 128], BF16))
        ident = c["ident"]
    onesf = es.enter_context(nc.sbuf_tensor(name + "_onesf", [1, 128], F32))
    of = p.op("g", lambda g: g.memset(onesf[:], 1.0))
    it = p.op("g", lambda g: g.iota(iot[:], pattern=[[1, NE]], base=0,
              channel_multiplier=0, allow_small_or_imprecise_dtypes=True))
    ps = c["ps"][4][:, 0:NE]
    prev = c["ps_free"].get(4)
    coef_toks = []
    for tt in range(tok // 128):
        tsl = slice(tt * 128, (tt + 1) * 128)
        m = None
        for dc in range(NCH):
            m = p.op("t", lambda t, dc=dc, tsl=tsl: t.matmul(
                ps, xb_tiles[dc][:, tsl], rw_sb[:, dc, :],
                start=(dc == 0), stop=False),
                waits=last([rw_tok], [xb_toks[dc]], [prev]), inc=False)
        m = p.op("t", lambda t: t.matmul(ps, onesf[:], gb_sb[:],
                 start=False, stop=True),
                 waits=last([gb_tok], [of]))
        tl = p.op("v", lambda v: v.tensor_copy(lg[:], ps), waits=last([m], [prev]))
        prev = tl
        c["ps_free"][4] = tl
        ncl = 7 if mode == "moe0" else 8
        a1 = p.op("v", lambda v, ncl=ncl: v.reduce_max(m1[:], lg[:, 0:ncl], axis=AX.X),
                  waits=[tl])
        a2 = p.op("v", lambda v: v.tensor_scalar(out=eq1[:], in0=lg[:], scalar1=m1[:],
                  scalar2=None, op0=AL.is_equal), waits=[a1])
        a3 = p.op("v", lambda v: v.scalar_tensor_tensor(out=l2[:], in0=eq1[:],
                  scalar=-1e30, in1=lg[:], op0=AL.mult, op1=AL.add), waits=[a2])
        a4 = p.op("v", lambda v, ncl=ncl: v.reduce_max(m2[:], l2[:, 0:ncl], axis=AX.X),
                  waits=[a3])
        a5 = p.op("v", lambda v: v.tensor_scalar(out=eq2[:], in0=lg[:], scalar1=m2[:],
                  scalar2=None, op0=AL.is_equal), waits=[a4])
        if mode == "moe0":
            b1t = p.op("v", lambda v: v.tensor_scalar(out=w2[:], in0=m2[:], scalar1=m1[:],
                       scalar2=None, op0=AL.subtract), waits=[a4])
            b2t = p.op("s", lambda s: s.activation(out=w1[:], in_=w2[:], func=AF.Sigmoid,
                       scale=-1.0), waits=[b1t])
            b3t = p.op("v", lambda v: v.tensor_scalar(out=w2[:], in0=w1[:], scalar1=-1.0,
                       scalar2=1.0, op0=AL.mult, op1=AL.add), waits=[b2t])
            b4t = p.op("v", lambda v: v.tensor_scalar(out=cf[:], in0=eq1[:], scalar1=w1[:],
                       scalar2=None, op0=AL.mult),
                       waits=last([b2t], [a2], coef_toks[-1:]))
            b5t = p.op("v", lambda v: v.scalar_tensor_tensor(out=cf[:, 0:7], in0=eq2[:, 0:7],
                       scalar=w2[:], in1=cf[:, 0:7], op0=AL.mult, op1=AL.add),
                       waits=[b3t, b4t, a5])
            b6t = p.op("g", lambda g: g.memset(cf[:, 7:8], 1.0), waits=[b5t])
            done = [b6t, b5t]
        else:
            en = p.op("v", lambda v: v.tensor_scalar(out=w2[:], in0=m1[:], scalar1=-1.0,
                      scalar2=None, op0=AL.mult), waits=[a1])
            e1 = p.op("s", lambda s: s.activation(out=l2[:], in_=lg[:], func=AF.Exp,
                      bias=w2[:]), waits=last([a5], [en], coef_toks[-1:]))
            e2 = p.op("v", lambda v: v.reduce_sum(sm[:], l2[:], axis=AX.X), waits=[e1])
            e3 = p.op("v", lambda v: v.reciprocal(w1[:], sm[:]), waits=[e2])
            e4 = p.op("v", lambda v: v.tensor_scalar(out=cf[:], in0=l2[:], scalar1=w1[:],
                      scalar2=None, op0=AL.mult), waits=[e3])
            s1t = p.op("v", lambda v: v.tensor_tensor(out=eq1[:], in0=eq1[:], in1=eq2[:],
                       op=AL.add), waits=[a5, e4])
            s2t = p.op("v", lambda v: v.tensor_tensor(out=cfb[:], in0=cf[:], in1=eq1[:],
                       op=AL.mult), waits=[s1t])
            s3t = p.op("v", lambda v: v.tensor_tensor(out=l2[:], in0=iot[:], in1=eq1[:],
                       op=AL.mult), waits=[s1t, it])
            s4t = p.op("v", lambda v: v.reduce_max(m2[:], l2[:], axis=AX.X), waits=[s3t])
            s5t = p.op("v", lambda v: v.tensor_scalar(out=cf[:], in0=iot[:], scalar1=m2[:],
                       scalar2=None, op0=AL.is_equal), waits=[s4t])
            wv = None
            for e in range(NE):
                tr = p.op("t", lambda t, e=e: t.transpose(pst[:], cfb[:, e:e + 1],
                          ident[:]), waits=last([s2t], [c["ident_tok"]], wi_toks[-1:]))
                wv = p.op("v", lambda v, e=e, tt=tt: v.tensor_copy(
                    wiRow[0:1, e * tok + tt * 128:e * tok + (tt + 1) * 128], pst[:]),
                    waits=[tr])
                wi_toks.append(wv)
            done = [s5t, wv]
        dt = p.op(p.key("S", "cfk"), lambda h, tt=tt: h.dma_start(
            out=coefD[tt * 128:(tt + 1) * 128, :], in_=cf[:]), waits=done)
        coef_toks.append(dt)
        prev = dt
    return coef_toks, wiRow, (wi_toks[-1] if wi_toks else None)


def build_moe_h(mode, tok=TOK):
    p = Prog()
    nc = p.nc
    es = ExitStack()
    c = Ctx()
    NE = 8
    xT = nc.dram_tensor("xT", [D, tok], F32, kind="ExternalInput").ap()
    rwT = nc.dram_tensor("rwT", [D, NE], F32, kind="ExternalInput").ap()
    gbRow = nc.dram_tensor("gbRow", [1, NE], F32, kind="ExternalInput").ap()
    if mode == "moe0":
        wgu = nc.dram_tensor("wgu", [NE, D, 2 * 4096], F32, kind="ExternalInput").ap()
    else:
        w1T = nc.dram_tensor("w1T", [NE, D, 4096], F32, kind="ExternalInput").ap()
        b1t = nc.dram_tensor("b1t", [128, NE, 32], F32, kind="ExternalInput").ap()
        identD = nc.dram_tensor("identD", [128, 128], F32, kind="ExternalInput").ap()
    hD = nc.dram_tensor("hD", [NE, 4096, tok], BF16, kind="ExternalOutput").ap()
    coefD = nc.dram_tensor("coefD", [tok, NE], F32, kind="ExternalOutput").ap()

    with es:
        setup_common(p, es, c)
        x_tiles, x_toks = load_x(p, es, c, xT, "x", F32, NCH, tok)
        xb_tiles = [es.enter_context(nc.sbuf_tensor(f"xb{i}", [128, tok], BF16))
                    for i in range(NCH)]
        xb_toks = [p.op("s", lambda s, cc=cc: s.activation(
            out=xb_tiles[cc][:], in_=x_tiles[cc][:], func=AF.Copy),
            waits=[x_toks[cc]]) for cc in range(NCH)]
        rw_sb, _ = load_tile(p, es, "sb_rw", [128, NCH, NE], F32,
                             rwT.rearrange("(dc pp) e -> pp dc e", pp=128))
        gb_sb, _ = load_tile(p, es, "sb_gb", [1, NE], F32, gbRow)
        if mode == "moe1":
            c["ident"], _ = load_tile(p, es, "sb_id", [128, 128], BF16,
                                      identD, queue="G")
            b1_sb, _ = load_tile(p, es, "sb_b1t", [128, NE, 32], F32, b1t)
        sbar = setup_bar(p)
        rw_tok = gb_tok = b1_tok = sbar
        c["ident_tok"] = sbar

        coef_toks, wiRow, wi_tok = emit_router_coef(
            p, es, c, "rt", x_tiles, x_toks, rw_sb, rw_tok, gb_sb, gb_tok,
            mode, coefD, tok=tok)

        wb = [es.enter_context(nc.sbuf_tensor(f"wgub{i}", [128, NCH, 128], BF16))
              for i in range(4)]
        wbk = [p.key("G", f"wgubk{i}") for i in range(4)]
        hsk = [p.key("S", f"hsk{i}") for i in range(2)]
        sg = [es.enter_context(nc.sbuf_tensor(f"sg{i}", [128, 512], F32))
              for i in range(2)]
        hstrip = [es.enter_context(nc.sbuf_tensor(f"hstrip{i}", [128, tok], BF16))
                  for i in range(2)]
        cbc = es.enter_context(nc.sbuf_tensor("cbc", [128, tok], BF16))
        wfree = [None] * 4
        sgfree = [None, None]
        hfree = [None, None]
        cbc_free = None
        ntt = tok // 512
        for e in range(NE):
            if mode == "moe1":
                cb_toks = []
                for tt in range(ntt):
                    sl = slice(tt * 512, (tt + 1) * 512)
                    m = p.op("t", lambda t, e=e, sl=sl: t.matmul(
                        c["ps"][5][:], c["ones_row"][:],
                        wiRow[0:1, e * tok + sl.start:e * tok + sl.stop],
                        start=True, stop=True),
                        waits=last([wi_tok], c["ones_ready"], [c["ps_free"].get(5)],
                                   [cbc_free]))
                    ev = p.op("v", lambda v, sl=sl: v.tensor_copy(cbc[:, sl],
                              c["ps"][5][:]), waits=[m])
                    c["ps_free"][5] = ev
                    cb_toks.append(ev)
            for hc in range(32):
                if mode == "moe0":
                    ws, ws2 = (hc % 2), 2 + (hc % 2)
                    wt1 = p.op(wbk[ws], lambda g, e=e, hc=hc, ws=ws: g.dma_start(
                        out=wb[ws][:], in_=wgu[e].rearrange("(dc pp) o -> pp dc o", pp=128)
                        [:, :, hc * 128:(hc + 1) * 128]), waits=last([wfree[ws]]))
                    wt2 = p.op(wbk[ws2], lambda g, e=e, hc=hc, ws2=ws2: g.dma_start(
                        out=wb[ws2][:], in_=wgu[e].rearrange("(dc pp) o -> pp dc o", pp=128)
                        [:, :, 4096 + hc * 128:4096 + (hc + 1) * 128]),
                        waits=last([wfree[ws2]]))
                else:
                    ws = hc % 2
                    wt1 = p.op(wbk[ws], lambda g, e=e, hc=hc, ws=ws: g.dma_start(
                        out=wb[ws][:], in_=w1T[e].rearrange("(dc pp) o -> pp dc o", pp=128)
                        [:, :, hc * 128:(hc + 1) * 128]), waits=last([wfree[ws]]))
                hrot = hc % 2
                for tt in range(ntt):
                    sl = slice(tt * 512, (tt + 1) * 512)
                    pa, pb = tt % 2, 2 + tt % 2
                    mg_ = None
                    for dc in range(NCH):
                        mg_ = p.op("t", lambda t, dc=dc, sl=sl, ws=ws, pa=pa:
                                   t.matmul(c["ps"][pa][:], wb[ws][:, dc, :],
                                            xb_tiles[dc][:, sl], start=(dc == 0),
                                            stop=(dc == NCH - 1)),
                                   waits=last([wt1], [xb_toks[dc]],
                                              [c["ps_free"].get(pa)]),
                                   inc=(dc == NCH - 1))
                    if mode == "moe0":
                        mu_ = None
                        for dc in range(NCH):
                            mu_ = p.op("t", lambda t, dc=dc, sl=sl, ws2=ws2, pb=pb:
                                       t.matmul(c["ps"][pb][:], wb[ws2][:, dc, :],
                                                xb_tiles[dc][:, sl], start=(dc == 0),
                                                stop=(dc == NCH - 1)),
                                       waits=last([wt2], [xb_toks[dc]],
                                                  [c["ps_free"].get(pb)]),
                                       inc=(dc == NCH - 1))
                        srot = tt % 2
                        tsg = p.op("s", lambda s, srot=srot, pa=pa: s.activation(
                            out=sg[srot][:], in_=c["ps"][pa][:], func=AF.Silu),
                            waits=last([mg_], [sgfree[srot]]))
                        c["ps_free"][pa] = tsg
                        th = p.op("v", lambda v, srot=srot, pb=pb, sl=sl, hrot=hrot:
                                  v.tensor_tensor(out=hstrip[hrot][:, sl],
                                                  in0=sg[srot][:], in1=c["ps"][pb][:],
                                                  op=AL.mult),
                                  waits=last([tsg], [mu_], [hfree[hrot]]))
                        c["ps_free"][pb] = th
                        sgfree[srot] = th
                    else:
                        srot = tt % 2
                        tm = p.op("v", lambda v, srot=srot, pa=pa, sl=sl:
                                  v.scalar_tensor_tensor(out=sg[srot][:], in0=cbc[:, sl],
                                                         scalar=1.0, in1=c["ps"][pa][:],
                                                         op0=AL.mult, op1=AL.mult),
                                  waits=last([mg_], cb_toks, [sgfree[srot]]))
                        c["ps_free"][pa] = tm
                        th = p.op("s", lambda s, srot=srot, sl=sl, hrot=hrot, e=e, hc=hc:
                                  s.activation(out=hstrip[hrot][:, sl], in_=sg[srot][:],
                                               func=AF.Relu,
                                               bias=b1_sb[:, e, hc:hc + 1]),
                                  waits=last([tm], [b1_tok], [hfree[hrot]]))
                        sgfree[srot] = th
                    if tt == ntt - 1:
                        wfree[ws] = mg_
                        if mode == "moe0":
                            wfree[ws2] = mu_
                        hd = p.op(hsk[hrot], lambda h, e=e, hc=hc, hrot=hrot: h.dma_start(
                            out=hD[e][hc * 128:(hc + 1) * 128, :], in_=hstrip[hrot][:]),
                            waits=[th])
                        hfree[hrot] = hd
            cbc_free = ("t", p.cnt["t"]) if mode == "moe1" else None
        p.op("S", lambda h: h.nop(), waits=all_dma_bar(p), inc=False)
        p.finalize(es)
    return p


def build_moe_down(mode, tok=TOK):
    p = Prog()
    nc = p.nc
    es = ExitStack()
    c = Ctx()
    NE = 8
    TP = 1024
    xT = nc.dram_tensor("xT", [D, tok], F32, kind="ExternalInput").ap()
    hD = nc.dram_tensor("hD", [NE, 4096, tok], BF16, kind="ExternalInput").ap()
    coefD = nc.dram_tensor("coefD", [tok, NE], F32, kind="ExternalInput").ap()
    wd = nc.dram_tensor("wd", [NE, 4096, D], F32, kind="ExternalInput").ap()
    b2Row = nc.dram_tensor("b2Row", [1, NE * D], F32, kind="ExternalInput").ap()
    identD = nc.dram_tensor("identD", [128, 128], F32, kind="ExternalInput").ap()
    xoT = nc.dram_tensor("xoT", [D, tok], F32, kind="ExternalOutput").ap()

    with es:
        setup_common(p, es, c)
        identf, _ = load_tile(p, es, "sb_idf", [128, 128], F32, identD)
        coef_sb, _ = load_tile(p, es, "sb_coef", [128, tok // 128, NE], F32,
                               coefD.rearrange("(tt pp) e -> pp tt e", pp=128))
        b2_sb, _ = load_tile(p, es, "sb_b2r", [1, NE * D], BF16, b2Row, queue="G")
        sbar = setup_bar(p)
        identf_tok = coef_tok = b2_tok = sbar
        acc = es.enter_context(nc.sbuf_tensor("acc", [128, TP // 128, D], F32))
        hbuf = [es.enter_context(nc.sbuf_tensor(f"hbuf{i}", [128, 16, TP], BF16))
                for i in range(2)]
        wdb = es.enter_context(nc.sbuf_tensor("wdb", [128, 32, 512], BF16))
        xs_tiles = [es.enter_context(nc.sbuf_tensor(f"xs{i}", [128, TP], F32))
                    for i in range(NCH)]
        lnsc = ln_scratch(p, es, "lnsc", TP)
        pstr = es.enter_context(nc.psum_tensor("pstr", [128, 128], F32))

        hfree = [None, None]
        wd_free = [None, None]
        acc_free = {}
        accb_free = {}
        pstr_free = None
        NTS = TP // 128

        for tp in range(tok // TP):
            tpsl = slice(tp * TP, (tp + 1) * TP)
            z_toks = {}
            tbar0 = ("t", p.cnt["t"])
            for ts in range(NTS):
                z_toks[ts] = p.op("g", lambda g, ts=ts: g.memset(acc[:, ts, :], 0.0),
                                  waits=[tbar0])
            for e in range(NE):
                ht = []
                hbk = [p.key("G", f"hbk{i}") for i in range(2)]
                for half in range(2):
                    ht.append(p.op(hbk[half], lambda g, e=e, half=half, tpsl=tpsl:
                                   g.dma_start(out=hbuf[half][:],
                                               in_=hD[e].rearrange("(hc pp) t -> pp hc t", pp=128)
                                               [:, half * 16:(half + 1) * 16, tpsl]),
                                   waits=last([hfree[half]])))
                for dc2 in range(2):
                    dsl = slice(dc2 * 512, (dc2 + 1) * 512)
                    wt = []
                    wdk = [p.key("G", f"wdk{i}") for i in range(2)]
                    for half in range(2):
                        wt.append(p.op(wdk[half], lambda g, e=e, half=half, dsl=dsl:
                                       g.dma_start(out=wdb[:, half * 16:(half + 1) * 16, :],
                                                   in_=wd[e].rearrange("(hc pp) o -> pp hc o", pp=128)
                                                   [:, half * 16:(half + 1) * 16, dsl]),
                                       waits=last([wd_free[half]])))
                    last_mm = None
                    for ts in range(NTS):
                        psl = ts % 4
                        m = None
                        for hc in range(32):
                            m = p.op("t", lambda t, hc=hc, ts=ts, psl=psl:
                                     t.matmul(c["ps"][psl][:],
                                              hbuf[hc // 16][:, hc % 16, ts * 128:(ts + 1) * 128],
                                              wdb[:, hc, :], start=(hc == 0), stop=False),
                                     waits=last([ht[hc // 16]], [wt[hc // 16]],
                                                [c["ps_free"].get(psl)]), inc=False)
                        m = p.op("t", lambda t, e=e, dc2=dc2, psl=psl: t.matmul(
                            c["ps"][psl][:], c["ones_row"][:],
                            b2_sb[:, e * D + dc2 * 512:e * D + (dc2 + 1) * 512],
                            start=False, stop=True),
                            waits=last([b2_tok], c["ones_ready"]))
                        ev = p.op("v", lambda v, e=e, ts=ts, tp=tp, dsl=dsl, psl=psl:
                                  v.scalar_tensor_tensor(
                                      out=acc[:, ts, dsl], in0=c["ps"][psl][:],
                                      scalar=coef_sb[:, tp * NTS + ts, e:e + 1],
                                      in1=acc[:, ts, dsl], op0=AL.mult, op1=AL.add),
                                  waits=last([m], [coef_tok], [z_toks[ts]],
                                             [acc_free.get((tp, ts, dsl.start))]))
                        acc_free[(tp, ts, dsl.start)] = ev
                        c["ps_free"][psl] = ev
                        last_mm = m
                    wd_free = [last_mm, last_mm]
                hfree = [last_mm, last_mm]
            xdk = p.key("S", "xdk")
            xok2 = p.key("S", "xok2")
            x_last = None
            for cc in range(NCH):
                x_last = p.op(xdk, lambda h, cc=cc, tpsl=tpsl: h.dma_start(
                    out=xs_tiles[cc][:], in_=xT[cc * 128:(cc + 1) * 128, tpsl]),
                    waits=last([("xok2", p.cnt["xok2"])] if tp > 0 else []))
            x_toks = [x_last] * NCH
            addt = {}
            for cc in range(NCH):
                for ts in range(NTS):
                    tr = p.op("t", lambda t, cc=cc, ts=ts: t.transpose(
                        pstr[:], acc[:, ts, cc * 128:(cc + 1) * 128], identf[:]),
                        waits=last([acc_free.get((tp, ts, 0))],
                                   [acc_free.get((tp, ts, 512))],
                                   [identf_tok], [pstr_free]))
                    ad = p.op("v", lambda v, cc=cc, ts=ts: v.scalar_tensor_tensor(
                        out=xs_tiles[cc][:, ts * 128:(ts + 1) * 128], in0=pstr[:],
                        scalar=1.0, in1=xs_tiles[cc][:, ts * 128:(ts + 1) * 128],
                        op0=AL.mult, op1=AL.add),
                        waits=last([tr], [x_toks[cc]]))
                    pstr_free = ad
                    addt[cc] = ad
            out_toks = emit_ln(p, es, c, f"pln{tp}", xs_tiles,
                               [addt[cc] for cc in range(NCH)], xs_tiles,
                               tok=TP, sc=lnsc)
            for cc in range(NCH):
                p.op(xok2, lambda h, cc=cc, tpsl=tpsl: h.dma_start(
                    out=xoT[cc * 128:(cc + 1) * 128, tpsl], in_=xs_tiles[cc][:]),
                    waits=[out_toks[cc]])
        p.op("S", lambda h: h.nop(), waits=all_dma_bar(p), inc=False)
        p.finalize(es)
    return p



def _tile8(v, n):
    return np.ascontiguousarray(v.reshape(n, 128).T).astype(np.float32)


def _lnp(w, b):
    return np.concatenate([_tile8(w, 8), _tile8(b, 8)], axis=1)


_PROG_CACHE = {}


def _prog(key, builder, *args, **kw):
    if key not in _PROG_CACHE:
        _PROG_CACHE[key] = builder(*args, **kw)
    return _PROG_CACHE[key]


def _run(p, in_maps):
    res = run_bass_kernel_spmd(p.nc, in_maps, core_ids=list(range(NCORES)))
    return res.results


def _attn_assemble(qTs, kTs, vTs, qpos_rows, ab_rows):
    QA = np.zeros((NCORES, 8, 65, T), dtype=BF)
    KA = np.zeros((NCORES, 8, 65, T), dtype=BF)
    VA = np.zeros((NCORES, 8, T, 65), dtype=BF)
    ABc = np.zeros((NCORES, 8, 128, 32), dtype=np.float32)
    for u in range(64):
        b, h = u // 16, u % 16
        cc, j = u // 8, u % 8
        q = np.concatenate([np.asarray(qTs[2 * b])[64 * h:64 * h + 64],
                            np.asarray(qTs[2 * b + 1])[64 * h:64 * h + 64]], axis=1)
        k = np.concatenate([np.asarray(kTs[2 * b])[64 * h:64 * h + 64],
                            np.asarray(kTs[2 * b + 1])[64 * h:64 * h + 64]], axis=1)
        v = np.concatenate([np.asarray(vTs[2 * b])[:, 64 * h:64 * h + 64],
                            np.asarray(vTs[2 * b + 1])[:, 64 * h:64 * h + 64]], axis=0)
        QA[cc, j, 0:64] = q
        QA[cc, j, 64] = qpos_rows[h]
        KA[cc, j, 0:64] = k
        KA[cc, j, 64] = 1.0
        VA[cc, j, :, 0:64] = v
        VA[cc, j, :, 64] = 1.0
        ABc[cc, j] = ab_rows[h]
    return QA, KA, VA, ABc


def _attn_out_slices(OTs):
    full = np.zeros((B, D, T), dtype=BF)
    for u in range(64):
        b, h = u // 16, u % 16
        cc, j = u // 8, u % 8
        full[b, 64 * h:64 * h + 64] = np.asarray(OTs[cc])[j]
    out = []
    for cc in range(NCORES):
        b, half = cc // 2, cc % 2
        out.append(np.ascontiguousarray(full[b][:, half * TOK:(half + 1) * TOK]))
    return out


_DBG_HOOK = None


def _dbg(name, xs):
    if _DBG_HOOK is not None:
        _DBG_HOOK(name, xs)


def kernel(**inp):
    f32 = np.float32
    x0 = np.asarray(inp["L0_attn_x"], dtype=f32)
    xT = np.ascontiguousarray(x0.reshape(B * T, D).T)
    xs = [np.ascontiguousarray(xT[:, c * TOK:(c + 1) * TOK]) for c in range(NCORES)]
    x_in = [a.copy() for a in xs]

    slopes = alibi_slopes(H)
    pos = np.arange(T, dtype=np.float64)
    qpos_rows = [(-slopes[h] * pos).astype(BF) for h in range(H)]
    p_idx = np.arange(128, dtype=np.float64)[:, None]
    kb_idx = np.arange(32, dtype=np.float64)[None, :]
    ab_rows = [(slopes[h] * (128 * kb_idx + p_idx)).astype(f32) for h in range(H)]
    ab_zero = [np.zeros((128, 32), f32)] * H
    qpos_zero = [np.zeros(T, BF)] * H
    ident = np.eye(128, dtype=f32)
    invf = 10000.0 ** (-np.arange(0, 64, 2) / 64.0)
    cosf = np.cos(pos[None, :] * invf[:, None]).astype(f32)
    sinf = np.sin(pos[None, :] * invf[:, None]).astype(f32)
    cosT = np.zeros((128, T), f32)
    sinT = np.zeros((128, T), f32)
    cosT[0:32] = cosf; cosT[64:96] = cosf
    sinT[0:32] = sinf; sinT[64:96] = sinf

    def seqpos_slice(arr2d, cc):
        half = cc % 2
        return np.ascontiguousarray(arr2d[:, half * TOK:(half + 1) * TOK])

    ipw = np.asarray(inp["L0_attn_in_proj_w"], f32)
    ipb = np.asarray(inp["L0_attn_in_proj_b"], f32)
    opw = np.asarray(inp["L0_attn_out_proj_w"], f32)
    opb = np.asarray(inp["L0_attn_out_proj_b"], f32)
    f1w = np.asarray(inp["L0_attn_ffn1_w"], f32)
    f1b = np.asarray(inp["L0_attn_ffn1_b"], f32)
    f2w = np.asarray(inp["L0_attn_ffn2_w"], f32)
    f2b = np.asarray(inp["L0_attn_ffn2_b"], f32)
    ln1w = np.asarray(inp["L0_attn_ln1_w"], f32)
    ln1b = np.asarray(inp["L0_attn_ln1_b"], f32)
    ln2w = np.asarray(inp["L0_attn_ln2_w"], f32)
    ln2b = np.asarray(inp["L0_attn_ln2_b"], f32)

    def l0_qkv_inputs(i):
        wiT = np.ascontiguousarray(ipw[i].T)
        bq = ipb[i]
        return {"wiT": wiT,
                "bqk": np.concatenate([_tile8(bq[:D], 8), _tile8(bq[D:2 * D], 8)], 1),
                "bv_row": np.ascontiguousarray(bq[2 * D:].reshape(1, D)),
                "lnp": _lnp(ln1w[i], ln1b[i])}

    p_qkv = _prog("qkv_ln", build_qkv, True, False)
    p_attn = _prog("attn", build_attn)
    p_inner = _prog("tok_inner", build_tok, "inner")
    p_final = _prog("tok_final", build_tok, "final")

    l0q = l0_qkv_inputs(0)
    r = _run(p_qkv, [dict(xT=xs[c], **l0q) for c in range(NCORES)])
    qTs = [r[c]["qT"] for c in range(NCORES)]
    kTs = [r[c]["kT"] for c in range(NCORES)]
    vTs = [r[c]["vTok"] for c in range(NCORES)]

    for i in range(4):
        QA, KA, VA, ABc = _attn_assemble(qTs, kTs, vTs, qpos_rows, ab_rows)
        r = _run(p_attn, [{"QA": QA[c], "KA": KA[c], "VA": VA[c], "AB": ABc[c]}
                          for c in range(NCORES)])
        at = _attn_out_slices([r[c]["OT"] for c in range(NCORES)])
        common = {"woT": np.ascontiguousarray(opw[i].T), "bo": _tile8(opb[i], 8),
                  "ln2p": _lnp(ln2w[i], ln2b[i]),
                  "f1T": np.ascontiguousarray(f1w[i].T), "b1": _tile8(f1b[i], 32),
                  "f2T": np.ascontiguousarray(f2w[i].T), "b2": _tile8(f2b[i], 8)}
        if i < 3:
            nq = l0_qkv_inputs(i + 1)
            r = _run(p_inner, [dict(xT=xs[c], attnT=at[c], **common, ln1p=nq["lnp"],
                                    wiT=nq["wiT"], bqk=nq["bqk"], bv_row=nq["bv_row"])
                               for c in range(NCORES)])
            xs = [np.asarray(r[c]["xoT"], f32) for c in range(NCORES)]
            _dbg(f"l0_{i}", xs)
            qTs = [r[c]["qT"] for c in range(NCORES)]
            kTs = [r[c]["kT"] for c in range(NCORES)]
            vTs = [r[c]["vTok"] for c in range(NCORES)]
        else:
            fin = {"lnfp": _lnp(np.asarray(inp["L0_attn_final_w"], f32),
                                np.asarray(inp["L0_attn_final_b"], f32))}
            r = _run(p_final, [dict(xT=xs[c], attnT=at[c], **common, **fin,
                                    xinT=x_in[c]) for c in range(NCORES)])
            xs = [np.asarray(r[c]["xoT"], f32) for c in range(NCORES)]
            _dbg("xb0", xs)

    rw0 = np.asarray(inp["L0_ffn_router_w"], f32)
    rwT0 = np.concatenate([rw0.T, np.zeros((D, 1), f32)], 1)
    wg8 = np.concatenate([np.asarray(inp["L0_ffn_w_gate"], f32),
                          np.asarray(inp["L0_ffn_sw_gate"], f32)], 0)
    wu8 = np.concatenate([np.asarray(inp["L0_ffn_w_up"], f32),
                          np.asarray(inp["L0_ffn_sw_up"], f32)], 0)
    wgu = np.ascontiguousarray(np.concatenate([wg8, wu8], axis=2))
    wd8 = np.ascontiguousarray(np.concatenate([np.asarray(inp["L0_ffn_w_down"], f32),
                                               np.asarray(inp["L0_ffn_sw_down"], f32)], 0))
    gb0 = np.zeros((1, 8), f32)
    b2z = np.zeros((1, 8 * D), f32)
    p_h0 = _prog("moe0_h", build_moe_h, "moe0")
    p_d0 = _prog("moe0_d", build_moe_down, "moe0")
    r = _run(p_h0, [{"xT": xs[c], "rwT": rwT0, "gbRow": gb0, "wgu": wgu}
                    for c in range(NCORES)])
    r = _run(p_d0, [{"xT": xs[c], "hD": np.asarray(r[c]["hD"]),
                     "coefD": np.asarray(r[c]["coefD"], f32), "wd": wd8,
                     "b2Row": b2z, "identD": ident} for c in range(NCORES)])
    xs = [np.asarray(r[c]["xoT"], f32) for c in range(NCORES)]
    _dbg("xb1", xs)

    wiT1 = np.ascontiguousarray(np.concatenate(
        [np.asarray(inp["L1_attn_Wq"], f32), np.asarray(inp["L1_attn_Wk"], f32),
         np.asarray(inp["L1_attn_Wv"], f32)], axis=1))
    bq1 = np.concatenate([np.asarray(inp["L1_attn_bq"], f32),
                          np.asarray(inp["L1_attn_bk"], f32)])
    bv1 = np.asarray(inp["L1_attn_bv"], f32)
    p_qkv1 = _prog("qkv_rope", build_qkv, False, True)
    r = _run(p_qkv1, [{"xT": xs[c], "wiT": wiT1,
                       "bqk": np.concatenate([_tile8(bq1[:D], 8), _tile8(bq1[D:], 8)], 1),
                       "bv_row": bv1.reshape(1, D),
                       "cosT": seqpos_slice(cosT, c), "sinT": seqpos_slice(sinT, c)}
                      for c in range(NCORES)])
    QA, KA, VA, ABc = _attn_assemble([r[c]["qT"] for c in range(NCORES)],
                                     [r[c]["kT"] for c in range(NCORES)],
                                     [r[c]["vTok"] for c in range(NCORES)],
                                     qpos_zero, ab_zero)
    r = _run(p_attn, [{"QA": QA[c], "KA": KA[c], "VA": VA[c], "AB": ABc[c]}
                      for c in range(NCORES)])
    at = _attn_out_slices([r[c]["OT"] for c in range(NCORES)])
    p_wd = _prog("tok_wd", build_tok, "wd")
    r = _run(p_wd, [{"xT": xs[c], "attnT": at[c],
                     "woT": np.asarray(inp["L1_attn_Wd"], f32),
                     "bo": _tile8(np.asarray(inp["L1_attn_bd"], f32), 8)}
                    for c in range(NCORES)])
    xs = [np.asarray(r[c]["xoT"], f32) for c in range(NCORES)]
    _dbg("xb2", xs)

    rwT1 = np.ascontiguousarray(np.asarray(inp["L1_ffn_gate_w"], f32).T)
    gb1 = np.asarray(inp["L1_ffn_gate_b"], f32).reshape(1, 8)
    w1T = np.ascontiguousarray(np.asarray(inp["L1_ffn_w1"], f32).transpose(0, 2, 1))
    b1m = np.asarray(inp["L1_ffn_b1"], f32)
    b1t = np.ascontiguousarray(b1m.reshape(8, 32, 128).transpose(2, 0, 1))
    w2T = np.ascontiguousarray(np.asarray(inp["L1_ffn_w2"], f32).transpose(0, 2, 1))
    b2r = np.ascontiguousarray(np.asarray(inp["L1_ffn_b2"], f32).reshape(1, 8 * D))
    p_h1 = _prog("moe1_h", build_moe_h, "moe1")
    p_d1 = _prog("moe1_d", build_moe_down, "moe1")
    r = _run(p_h1, [{"xT": xs[c], "rwT": rwT1, "gbRow": gb1, "w1T": w1T,
                     "b1t": b1t, "identD": ident} for c in range(NCORES)])
    r = _run(p_d1, [{"xT": xs[c], "hD": np.asarray(r[c]["hD"]),
                     "coefD": np.asarray(r[c]["coefD"], f32), "wd": w2T,
                     "b2Row": b2r, "identD": ident} for c in range(NCORES)])
    xs = [np.asarray(r[c]["xoT"], f32) for c in range(NCORES)]

    out = np.concatenate(xs, axis=1)
    return np.ascontiguousarray(out.T).reshape(B, T, D).astype(np.float32)


# version 06
# speedup vs baseline: 1.3634x; 1.0117x over previous
import numpy as np
import ml_dtypes
import concourse.bass as bass
import concourse.mybir as mybir
from concourse.bass_utils import run_bass_kernel_spmd
from concourse.alu_op_type import AluOpType as AL
from contextlib import ExitStack

F32 = mybir.dt.float32
BF16 = mybir.dt.bfloat16
AF = mybir.ActivationFunctionType
AX = mybir.AxisListType
BF = ml_dtypes.bfloat16

NCORES = 8
TOK = 2048
D = 1024
NCH = D // 128
B, T, H, HD = 4, 4096, 16, 64
NKB = T // 128
EPS = 1e-5
PRUNE_EXP = 93.0


def alibi_slopes(n=16):
    import math
    closest = 2 ** math.floor(math.log2(n))
    base = 2 ** (-2 ** (-(math.log2(closest) - 3)))
    slopes = base ** np.arange(1, closest + 1)
    if closest != n:
        eb = 2 ** (-2 ** (-(math.log2(2 * closest) - 3)))
        extra = eb ** np.arange(1, 2 * (n - closest) + 1, 2)
        slopes = np.concatenate([slopes, extra])
    return slopes.astype(np.float64)



class Prog:
    KEYS = ("t", "v", "s", "g", "S", "G", "A")
    ENG = {"t": "tensor", "v": "vector", "s": "scalar", "g": "gpsimd",
           "S": "sync", "G": "gpsimd", "A": "scalar"}
    DMA = {"S", "G", "A"}

    def __init__(self):
        self.nc = bass.Bass("TRN2", target_bir_lowering=False)
        self.order = {e: [] for e in ("tensor", "vector", "scalar", "gpsimd", "sync")}
        self.cnt = dict.fromkeys(self.KEYS, 0)
        self.engd = dict(self.ENG)
        self.dmaset = set(self.DMA)
        self.waited = {e: dict.fromkeys(self.KEYS, 0) for e in self.order}
        self.floor = {}

    def key(self, queue, name):
        if name in self.cnt:
            return name
        self.engd[name] = self.ENG[queue]
        self.dmaset.add(name)
        self.cnt[name] = 0
        for e in self.waited:
            self.waited[e][name] = 0
        return name

    def barrier(self, keys=("t", "v", "s", "g", "S", "G", "A")):
        for k in keys:
            if self.cnt[k] > self.floor.get(k, 0):
                self.floor[k] = self.cnt[k]

    def op(self, key, fn, waits=(), inc=True):
        eng = self.engd[key]
        w = {}
        for tokn in flat_toks(waits) + list(self.floor.items()):
            if tokn is None:
                continue
            k, v = tokn
            if v > self.waited[eng].get(k, 0) and w.get(k, 0) < v:
                w[k] = v
        for k, v in w.items():
            self.waited[eng][k] = v
        amt = (16 if key in self.dmaset else 1) if inc else 0
        self.cnt[key] += amt
        self.order[eng].append((tuple(w.items()), fn, key, amt))
        return (key, self.cnt[key])

    def finalize(self, es):
        nc = self.nc
        sems = {k: es.enter_context(nc.semaphore("sem_" + k)) for k in self.cnt}
        block = es.enter_context(nc.Block())

        def mk(eng):
            def body(h):
                for waits, fn, key, amt in self.order[eng]:
                    for k, v in waits:
                        h.wait_ge(sems[k], v)
                    inst = fn(h)
                    if amt:
                        inst.then_inc(sems[key], amt)
            return body

        for eng in self.order:
            if self.order[eng]:
                getattr(block, eng)(mk(eng))


def flat_toks(toks):
    out = []
    def add(t):
        if t is None:
            return
        if isinstance(t, tuple) and len(t) == 2 and isinstance(t[0], str):
            out.append(t)
            return
        for x in t:
            add(x)
    add(list(toks))
    return out


def last(*toks):
    return flat_toks(toks)


class Ctx(dict):
    pass


def setup_common(p, es, c, npool=6):
    nc = p.nc
    c["ones_col"] = es.enter_context(nc.sbuf_tensor("cm_ones_col", [128, 1], BF16))
    c["ones_row"] = es.enter_context(nc.sbuf_tensor("cm_ones_row", [1, 128], BF16))
    c["eps1"] = es.enter_context(nc.sbuf_tensor("cm_eps1", [1, 1], F32))
    c["ones_colf"] = es.enter_context(nc.sbuf_tensor("cm_ones_colf", [128, 1], F32))
    c["ps"] = [es.enter_context(nc.psum_tensor(f"cm_ps{i}", [128, 512], F32))
               for i in range(npool)]
    c["ps_free"] = {}
    t0 = p.op("g", lambda g: g.memset(c["ones_col"][:], 1.0))
    t1 = p.op("g", lambda g: g.memset(c["ones_row"][:], 1.0))
    t2 = p.op("g", lambda g: g.memset(c["eps1"][:], EPS))
    t3 = p.op("g", lambda g: g.memset(c["ones_colf"][:], 1.0))
    c["ones_ready"] = [t0, t1, t2, t3]


def load_tile(p, es, name, shape, dtype, src_ap, queue=None):
    t = es.enter_context(p.nc.sbuf_tensor(name, shape, dtype))
    if queue is None:
        queue = "S"
    k = p.key(queue, "setup_" + queue)
    tok = p.op(k, lambda h: h.dma_start(out=t[:], in_=src_ap))
    return t, tok


def setup_bar(p):
    return [(k, p.cnt[k]) for k in ("setup_S", "setup_G", "setup_A") if k in p.cnt]


def all_dma_bar(p):
    return [(k, p.cnt[k]) for k in p.cnt if k in p.dmaset and p.cnt[k] > 0]


def load_x(p, es, c, x_dram, name="x", dtype=F32, nch=NCH, tok=TOK):
    nc = p.nc
    q = "S" if dtype == F32 else "G"
    k = p.key(q, "ldx_" + name)
    tiles = []
    tok_last = None
    for cc in range(nch):
        t = es.enter_context(nc.sbuf_tensor(f"{name}{cc}", [128, tok], dtype))
        tiles.append(t)
        tok_last = p.op(k, lambda h, cc=cc, t=t: h.dma_start(
            out=t[:], in_=x_dram[cc * 128:(cc + 1) * 128, :]))
    return tiles, [tok_last] * nch



def ln_scratch(p, es, name, tok=TOK):
    nc = p.nc
    sc = {}
    for nm, shape, dt in [("xb", [128, 512], BF16), ("xb2", [128, 512], BF16),
                          ("rowA", [1, tok], F32), ("rowB", [1, tok], F32),
                          ("rowC", [1, tok], F32),
                          ("rstdb", [1, tok], BF16), ("mrsb", [1, tok], BF16),
                          ("rbc", [128, tok], BF16), ("mbc", [128, tok], BF16)]:
        sc[nm] = es.enter_context(nc.sbuf_tensor(f"{name}_{nm}", shape, dt))
    sc["busy"] = None
    return sc


def emit_ln(p, es, c, name, x_tiles, x_toks, out_tiles, w_ap=None, b_ap=None,
            tok=TOK, param_toks=(), sc=None, t1_tiles=None, t1_toks=None):
    nc = p.nc
    ntt = tok // 512
    if sc is None:
        sc = ln_scratch(p, es, name, tok)
    if t1_tiles is None:
        t1_tiles = x_tiles
        t1_toks = [None] * NCH
    ps_s = c["ps"][2][0:1, :]
    ps_b = c["ps"][3]
    pf = c["ps_free"]

    s1_toks, s2_toks = [], []
    ev = sc["busy"]
    for tt in range(ntt):
        sl = slice(tt * 512, (tt + 1) * 512)
        m = None
        for cc in range(NCH):
            m = p.op("t", lambda t, cc=cc, sl=sl: t.matmul(
                ps_s, c["ones_colf"][:], x_tiles[cc][:, sl],
                start=(cc == 0), stop=(cc == NCH - 1)),
                waits=last(c["ones_ready"], [x_toks[cc]], [ev], [pf.get(2)]),
                inc=(cc == NCH - 1))
        ev = p.op("v", lambda v, sl=sl: v.tensor_copy(
            sc["rowA"][:, sl], ps_s), waits=last([m], [sc["busy"]]))
        pf[2] = ev
        s1_toks.append(ev)
    sq_done = {}
    for tt in range(ntt):
        sl = slice(tt * 512, (tt + 1) * 512)
        m = None
        for cc in range(NCH):
            xb = sc["xb"] if (tt * NCH + cc) % 2 == 0 else sc["xb2"]
            ct = p.op("v", lambda v, cc=cc, sl=sl, xb=xb: v.tensor_tensor(
                out=xb[:], in0=x_tiles[cc][:, sl], in1=x_tiles[cc][:, sl], op=AL.mult),
                waits=last([x_toks[cc]], [sq_done.get(id(xb))]))
            m = p.op("t", lambda t, xb=xb, cc=cc: t.matmul(
                ps_s, c["ones_col"][:], xb[:],
                start=(cc == 0), stop=(cc == NCH - 1)),
                waits=last(c["ones_ready"], [ct], [ev], [pf.get(2)]), inc=True)
            sq_done[id(xb)] = m
        ev = p.op("v", lambda v, sl=sl: v.tensor_copy(
            sc["rowB"][:, sl], ps_s), waits=last([m]))
        pf[2] = ev
        s2_toks.append(ev)

    t_mu = p.op("v", lambda v: v.tensor_scalar(out=sc["rowA"][:], in0=sc["rowA"][:],
                scalar1=1.0 / D, scalar2=None, op0=AL.mult), waits=s1_toks)
    t_msq = p.op("v", lambda v: v.tensor_tensor(out=sc["rowC"][:], in0=sc["rowA"][:],
                 in1=sc["rowA"][:], op=AL.mult), waits=last([t_mu], [sc["busy"]]))
    t_var = p.op("v", lambda v: v.scalar_tensor_tensor(out=sc["rowB"][:], in0=sc["rowB"][:],
                 scalar=1.0 / D, in1=sc["rowC"][:], op0=AL.mult, op1=AL.subtract),
                 waits=last(s2_toks, [t_msq]))
    t_std = p.op("s", lambda s: s.activation(out=sc["rowC"][:], in_=sc["rowB"][:],
                 func=AF.Sqrt, bias=c["eps1"][:]),
                 waits=last([t_var], c["ones_ready"]))
    t_rstd = p.op("v", lambda v: v.reciprocal(sc["rowB"][:], sc["rowC"][:]), waits=[t_std])
    t_rb = p.op("v", lambda v: v.tensor_copy(sc["rstdb"][:], sc["rowB"][:]), waits=[t_rstd])
    t_mrs = p.op("v", lambda v: v.tensor_tensor(out=sc["rowA"][:], in0=sc["rowA"][:],
                 in1=sc["rowB"][:], op=AL.mult), waits=[t_rstd])
    t_mb = p.op("v", lambda v: v.tensor_copy(sc["mrsb"][:], sc["rowA"][:]), waits=[t_mrs])

    bc_toks = []
    evp = None
    for tt in range(ntt):
        sl = slice(tt * 512, (tt + 1) * 512)
        m = p.op("t", lambda t, sl=sl: t.matmul(ps_b[:], c["ones_row"][:],
                 sc["rstdb"][:, sl], start=True, stop=True),
                 waits=last([t_rb], [evp], c["ones_ready"], [pf.get(3)]))
        evp = p.op("v", lambda v, sl=sl: v.tensor_copy(sc["rbc"][:, sl], ps_b[:]),
                   waits=last([m], [sc["busy"]]))
        m2 = p.op("t", lambda t, sl=sl: t.matmul(ps_b[:], c["ones_row"][:],
                  sc["mrsb"][:, sl], start=True, stop=True),
                  waits=last([t_mb], [evp]))
        evp = p.op("v", lambda v, sl=sl: v.tensor_copy(sc["mbc"][:, sl], ps_b[:]),
                   waits=[m2])
        pf[3] = evp
        bc_toks.append(evp)

    out_toks = []
    for cc in range(NCH):
        tt1 = p.op("v", lambda v, cc=cc: v.tensor_tensor(out=t1_tiles[cc][:],
                   in0=x_tiles[cc][:], in1=sc["rbc"][:], op=AL.mult),
                   waits=last(bc_toks, [x_toks[cc]], [t1_toks[cc]],
                              out_toks[-1:] if out_toks else []))
        if w_ap is not None:
            tt2 = p.op("v", lambda v, cc=cc: v.scalar_tensor_tensor(out=t1_tiles[cc][:],
                       in0=sc["mbc"][:], scalar=-1.0, in1=t1_tiles[cc][:],
                       op0=AL.mult, op1=AL.add), waits=[tt1])
            tt3 = p.op("v", lambda v, cc=cc: v.tensor_scalar(out=out_tiles[cc][:],
                       in0=t1_tiles[cc][:], scalar1=w_ap(cc), scalar2=b_ap(cc),
                       op0=AL.mult, op1=AL.add), waits=last([tt2], list(param_toks)))
            out_toks.append(tt3)
        else:
            tt2 = p.op("v", lambda v, cc=cc: v.scalar_tensor_tensor(out=out_tiles[cc][:],
                       in0=sc["mbc"][:], scalar=-1.0, in1=t1_tiles[cc][:],
                       op0=AL.mult, op1=AL.add), waits=[tt1])
            out_toks.append(tt2)
    sc["busy"] = out_toks[-1]
    return out_toks



def emit_proj_feat(p, es, c, name, xn_tiles, xn_toks, w_dram, n_out, evac,
                   tok=TOK, nch_in=NCH, extra_mm=None, ps_idx=(0, 1)):
    nc = p.nc
    ntt = tok // 512
    noc = n_out // 128
    wbufs = [es.enter_context(nc.sbuf_tensor(f"{name}_w{i}", [128, nch_in, 128], BF16))
             for i in range(2)]
    wk = [p.key("G", f"{name}_wk{i}") for i in range(2)]
    pss = [c["ps"][ps_idx[0]], c["ps"][ps_idx[1]]]
    wlast = [None, None]
    plast = [c["ps_free"].get(ps_idx[0]), c["ps_free"].get(ps_idx[1])]
    wsrc = w_dram.rearrange("(dc pp) o -> pp dc o", pp=128)
    toks = []
    for oc in range(noc):
        slot = oc % 2
        wt = p.op(wk[slot], lambda g, oc=oc, slot=slot: g.dma_start(
            out=wbufs[slot][:], in_=wsrc[:, :, oc * 128:(oc + 1) * 128]),
            waits=last([wlast[slot]]))
        mm_last = None
        for tt in range(ntt):
            psl = (oc * ntt + tt) % 2
            sl = slice(tt * 512, (tt + 1) * 512)
            m = None
            for dc in range(nch_in):
                stop = (dc == nch_in - 1) and extra_mm is None
                m = p.op("t", lambda t, dc=dc, sl=sl, slot=slot, psl=psl, stop=stop:
                         t.matmul(pss[psl][:], wbufs[slot][:, dc, :],
                                  xn_tiles[dc][:, sl], start=(dc == 0), stop=stop),
                         waits=last([wt], [xn_toks[dc]], [plast[psl]]),
                         inc=(dc == nch_in - 1 and extra_mm is None))
            if extra_mm is not None:
                m = extra_mm(p, oc, tt, pss[psl])
            ev = evac(oc, tt, pss[psl], m)
            plast[psl] = ev
            c["ps_free"][ps_idx[psl]] = ev
            toks.append(ev)
            mm_last = m
        wlast[slot] = mm_last
    return toks


def emit_proj_tok(p, es, c, name, xn_tiles, xn_toks, w_dram, n_out, evac,
                  bias_row=None, bias_tok=None, tok=TOK, nch_in=NCH, ps_idx=(4, 5)):
    nc = p.nc
    ntt = tok // 128
    nvc = n_out // 512
    wbufs = [es.enter_context(nc.sbuf_tensor(f"{name}_w{i}", [128, nch_in, 512], BF16))
             for i in range(2)]
    wk = [p.key("G", f"{name}_wk{i}") for i in range(2)]
    pss = [c["ps"][ps_idx[0]], c["ps"][ps_idx[1]]]
    wlast = [None, None]
    plast = [c["ps_free"].get(ps_idx[0]), c["ps_free"].get(ps_idx[1])]
    wsrc = w_dram.rearrange("(dc pp) o -> pp dc o", pp=128)
    toks = []
    for vc in range(nvc):
        slot = vc % 2
        wt = p.op(wk[slot], lambda g, vc=vc, slot=slot: g.dma_start(
            out=wbufs[slot][:], in_=wsrc[:, :, vc * 512:(vc + 1) * 512]),
            waits=last([wlast[slot]]))
        mm_last = None
        for tt in range(ntt):
            psl = (vc * ntt + tt) % 2
            tsl = slice(tt * 128, (tt + 1) * 128)
            m = None
            for dc in range(nch_in):
                is_last = (dc == nch_in - 1) and bias_row is None
                m = p.op("t", lambda t, dc=dc, tsl=tsl, slot=slot, psl=psl,
                         is_last=is_last:
                         t.matmul(pss[psl][:], xn_tiles[dc][:, tsl],
                                  wbufs[slot][:, dc, :],
                                  start=(dc == 0), stop=is_last),
                         waits=last([wt], [xn_toks[dc]], [plast[psl]]),
                         inc=is_last)
            if bias_row is not None:
                m = p.op("t", lambda t, vc=vc, psl=psl: t.matmul(
                    pss[psl][:], c["ones_row"][:],
                    bias_row[:, vc * 512:(vc + 1) * 512], start=False, stop=True),
                    waits=last([bias_tok], c["ones_ready"]))
            ev = evac(tt, vc, pss[psl], m)
            plast[psl] = ev
            c["ps_free"][ps_idx[psl]] = ev
            toks.append(ev)
            mm_last = m
        wlast[slot] = mm_last
    return toks



def emit_rope(p, es, c, name, strip, ready_toks, cosb, sinb, const_toks, aux, aux_free):
    done = []
    rk = p.key("S", "ropek")
    for rb in (0, 64):
        d1 = p.op(rk, lambda h, rb=rb: h.dma_start(
            out=aux[rb:rb + 16, :], in_=strip[rb + 16:rb + 32, :]),
            waits=last(ready_toks, [aux_free]))
        d2 = p.op(rk, lambda h, rb=rb: h.dma_start(
            out=aux[rb + 16:rb + 32, :], in_=strip[rb:rb + 16, :]),
            waits=last(ready_toks, [aux_free]))
        t1 = p.op("v", lambda v, rb=rb: v.tensor_tensor(out=aux[rb:rb + 32, :],
                  in0=aux[rb:rb + 32, :], in1=sinb[rb:rb + 32, :], op=AL.mult),
                  waits=last([d1], [d2], const_toks))
        t2 = p.op("v", lambda v, rb=rb: v.tensor_tensor(out=strip[rb:rb + 32, :],
                  in0=strip[rb:rb + 32, :], in1=cosb[rb:rb + 32, :], op=AL.mult),
                  waits=last(ready_toks, [d1], [d2]))
        t3 = p.op("v", lambda v, rb=rb: v.tensor_tensor(out=strip[rb:rb + 32, :],
                  in0=strip[rb:rb + 32, :], in1=aux[rb:rb + 32, :], op=AL.subtract),
                  waits=[t1, t2])
        aux_free = t3
        done.append(t3)
    return done, aux_free



def emit_qkv_part(p, es, c, name, xn_tiles, xn_toks, wiT, bqk_sb, bqk_tok,
                  bvb, bv_tok, qT, kT, vTok, rope=None, tok=TOK):
    nc = p.nc
    strips = [es.enter_context(nc.sbuf_tensor(f"{name}_qks{i}", [128, tok], BF16))
              for i in range(2)]
    sk = [p.key("S", f"{name}_sk{i}") for i in range(2)]
    strip_free = [None, None]
    state = {"aux_free": None, "per_oc": {}}

    def evac_qk(oc, tt, ps, mm_tok):
        srot = oc % 2
        sl = slice(tt * 512, (tt + 1) * 512)
        scale = 0.125 if oc < 8 else 1.0
        ev = p.op("s", lambda s, oc=oc, sl=sl, ps=ps, srot=srot, scale=scale:
                  s.activation(out=strips[srot][:, sl], in_=ps[:], func=AF.Identity,
                               bias=bqk_sb[:, oc:oc + 1], scale=scale),
                  waits=last([mm_tok], [bqk_tok], [strip_free[srot]]))
        state["per_oc"].setdefault(oc, []).append(ev)
        if tt == tok // 512 - 1:
            evs = state["per_oc"].pop(oc)
            rtoks = evs
            if rope is not None:
                rt, state["aux_free"] = emit_rope(
                    p, es, c, f"{name}_rope{oc}", strips[srot], evs,
                    rope["cosb"], rope["sinb"], rope["const_toks"],
                    rope["aux"], state["aux_free"])
                rtoks = rt + evs
            dst = qT if oc < 8 else kT
            r0 = (oc % 8) * 128
            dtok = p.op(sk[srot], lambda h, dst=dst, r0=r0, srot=srot: h.dma_start(
                out=dst[r0:r0 + 128, :], in_=strips[srot][:]), waits=rtoks)
            strip_free[srot] = dtok
        return ev

    emit_proj_feat(p, es, c, name + "_qk", xn_tiles, xn_toks, wiT[:, 0:2 * D], 2 * D,
                   evac_qk, tok=tok)

    vtiles = [es.enter_context(nc.sbuf_tensor(f"{name}_vt{i}", [128, 512], BF16))
              for i in range(2)]
    vk = [p.key("S", f"{name}_vk{i}") for i in range(2)]
    vfree = [None, None]

    def evac_v(tt, vc, ps, mm_tok):
        rot = (vc * (tok // 128) + tt) % 2
        ev = p.op("s", lambda s, ps=ps, rot=rot: s.activation(
            out=vtiles[rot][:], in_=ps[:], func=AF.Copy),
            waits=last([mm_tok], [vfree[rot]]))
        dtok = p.op(vk[rot], lambda h, tt=tt, vc=vc, rot=rot: h.dma_start(
            out=vTok[tt * 128:(tt + 1) * 128, vc * 512:(vc + 1) * 512],
            in_=vtiles[rot][:]), waits=[ev])
        vfree[rot] = dtok
        return ev

    emit_proj_tok(p, es, c, name + "_v", xn_tiles, xn_toks, wiT[:, 2 * D:3 * D], D,
                  evac_v, bias_row=bvb, bias_tok=bv_tok, tok=tok)


def load_rope(p, es, c, cosT, sinT, tok=TOK):
    cosb, cos_tok = load_tile(p, es, "sb_cosb", [128, tok], BF16, cosT, queue="G")
    sinb, sin_tok = load_tile(p, es, "sb_sinb", [128, tok], BF16, sinT, queue="G")
    aux = es.enter_context(p.nc.sbuf_tensor("ropeaux", [128, tok], BF16))
    return {"cosb": cosb, "sinb": sinb, "aux": aux, "const_toks": [cos_tok, sin_tok]}



def build_qkv(with_ln, with_rope, tok=TOK):
    p = Prog()
    nc = p.nc
    es = ExitStack()
    c = Ctx()
    xT = nc.dram_tensor("xT", [D, tok], F32, kind="ExternalInput").ap()
    wiT = nc.dram_tensor("wiT", [D, 3 * D], F32, kind="ExternalInput").ap()
    bqk = nc.dram_tensor("bqk", [128, 16], F32, kind="ExternalInput").ap()
    bv_row = nc.dram_tensor("bv_row", [1, D], F32, kind="ExternalInput").ap()
    if with_ln:
        lnp = nc.dram_tensor("lnp", [128, 16], F32, kind="ExternalInput").ap()
    if with_rope:
        cosT = nc.dram_tensor("cosT", [128, tok], F32, kind="ExternalInput").ap()
        sinT = nc.dram_tensor("sinT", [128, tok], F32, kind="ExternalInput").ap()
    qT = nc.dram_tensor("qT", [D, tok], BF16, kind="ExternalOutput").ap()
    kT = nc.dram_tensor("kT", [D, tok], BF16, kind="ExternalOutput").ap()
    vTok = nc.dram_tensor("vTok", [tok, D], BF16, kind="ExternalOutput").ap()

    with es:
        setup_common(p, es, c)
        x_tiles, x_toks = load_x(p, es, c, xT, "x", F32, NCH, tok)
        bqk_sb, _bt = load_tile(p, es, "sb_bqk", [128, 16], F32, bqk)
        bvb, _bv = load_tile(p, es, "sb_bvb", [1, D], BF16, bv_row, queue="G")
        rope = load_rope(p, es, c, cosT, sinT, tok) if with_rope else None
        if with_ln:
            lnp_sb, _lt = load_tile(p, es, "sb_lnp", [128, 16], F32, lnp)
        sbar = setup_bar(p)
        bqk_tok = p.op("v", lambda v: v.tensor_scalar(out=bqk_sb[:, 0:8],
                       in0=bqk_sb[:, 0:8], scalar1=0.125, scalar2=None,
                       op0=AL.mult), waits=sbar)
        bv_tok = sbar
        if rope is not None:
            rope["const_toks"] = sbar

        xn_tiles = [es.enter_context(nc.sbuf_tensor(f"xn{i}", [128, tok], BF16))
                    for i in range(NCH)]
        if with_ln:
            xn_toks = emit_ln(p, es, c, "ln1", x_tiles, x_toks, xn_tiles,
                              w_ap=lambda cc: lnp_sb[:, cc:cc + 1],
                              b_ap=lambda cc: lnp_sb[:, 8 + cc:8 + cc + 1],
                              tok=tok, param_toks=sbar)
        else:
            xn_toks = [p.op("s", lambda s, cc=cc: s.activation(
                out=xn_tiles[cc][:], in_=x_tiles[cc][:], func=AF.Copy),
                waits=[x_toks[cc]]) for cc in range(NCH)]

        emit_qkv_part(p, es, c, "qkv", xn_tiles, xn_toks, wiT, bqk_sb, bqk_tok,
                      bvb, bv_tok, qT, kT, vTok, rope=rope, tok=tok)
        p.op("S", lambda h: h.nop(), waits=all_dma_bar(p), inc=False)
        p.finalize(es)
    return p



def build_attn(tok_q=T, nunits=8, slot_slopes=None):
    if slot_slopes is None:
        slot_slopes = [0.0] * nunits

    def kb_min(slope, qt):
        if slope <= 0:
            return 0
        for kb in range(4 * qt + 4):
            if slope * (512 * qt - 128 * kb - 127) < 100.0:
                return kb
        return 4 * qt + 3
    p = Prog()
    nc = p.nc
    es = ExitStack()
    c = Ctx()
    nu = nunits
    QA = nc.dram_tensor("QA", [nu, 65, T], BF16, kind="ExternalInput").ap()
    KA = nc.dram_tensor("KA", [nu, 65, T], BF16, kind="ExternalInput").ap()
    VA = nc.dram_tensor("VA", [nu, T, 65], BF16, kind="ExternalInput").ap()
    AB = nc.dram_tensor("AB", [nu, 128, 32], F32, kind="ExternalInput").ap()
    OT = nc.dram_tensor("OT", [nu, 64, T], BF16, kind="ExternalOutput").ap()

    NQT = T // 512

    with es:
        setup_common(p, es, c, npool=0)
        qa = [es.enter_context(nc.sbuf_tensor(f"qa{i}", [65, T], BF16)) for i in range(2)]
        ka = [es.enter_context(nc.sbuf_tensor(f"ka{i}", [65, T], BF16)) for i in range(2)]
        va = [es.enter_context(nc.sbuf_tensor(f"va{i}", [128, NKB, 65], BF16)) for i in range(2)]
        ab = [es.enter_context(nc.sbuf_tensor(f"ab{i}", [128, 32], F32)) for i in range(2)]
        ot = [es.enter_context(nc.sbuf_tensor(f"ot{i}", [64, T], BF16)) for i in range(2)]
        sq = es.enter_context(nc.sbuf_tensor("sq", [64, 512], BF16))
        nrm = es.enter_context(nc.sbuf_tensor("nrm", [1, 2 * T], F32))
        nb2 = es.enter_context(nc.sbuf_tensor("nb2", [1, 2], F32))
        mgsum = es.enter_context(nc.sbuf_tensor("mgsum", [1, 1], F32))
        mgb = es.enter_context(nc.sbuf_tensor("mgb", [1, 1], BF16))
        biasu = [es.enter_context(nc.sbuf_tensor(f"biasu{i}", [128, 32], F32))
                 for i in range(2)]
        psb = [es.enter_context(nc.sbuf_tensor(f"psb{i}", [128, 512], BF16)) for i in range(3)]
        rcp = es.enter_context(nc.sbuf_tensor("rcp", [1, 512], F32))
        rcpb = es.enter_context(nc.sbuf_tensor("rcpb", [1, 512], BF16))
        rbsb = es.enter_context(nc.sbuf_tensor("rbsb", [64, 512], BF16))

        ps_out = [es.enter_context(nc.psum_tensor(f"pso{i}", [65, 512], F32)) for i in range(2)]
        ps_s = [es.enter_context(nc.psum_tensor(f"pss{i}", [128, 512], F32)) for i in range(3)]
        ps_n = es.enter_context(nc.psum_tensor("psn", [1, 512], F32))
        ps_mg = es.enter_context(nc.psum_tensor("psmg", [128, 1], F32))
        ps_rb = es.enter_context(nc.psum_tensor("psrb", [64, 512], F32))

        in_free = [[None, None] for _ in range(4)]
        ot_free = [None, None]
        psb_free = [None] * 3
        pso_free = [None, None]
        pss_free = [None, None, None]
        bias_free = [None, None]
        last_tbias = None
        last_tmg = None

        otk = [p.key("S", f"otk{i}") for i in range(2)]
        qak = [p.key("S", f"qak{i}") for i in range(2)]
        kak = [p.key("S", f"kak{i}") for i in range(2)]
        vak = [p.key("G", f"vak{i}") for i in range(2)]
        abk = [p.key("G", f"abk{i}") for i in range(2)]
        for u in range(nu):
            rot = u % 2
            tqa = p.op(qak[rot], lambda h, u=u, rot=rot: h.dma_start(
                out=qa[rot][:], in_=QA[u]), waits=last([in_free[0][rot]]))
            tka = p.op(kak[rot], lambda h, u=u, rot=rot: h.dma_start(
                out=ka[rot][:], in_=KA[u]), waits=last([in_free[1][rot]]))
            tva = p.op(vak[rot], lambda g, u=u, rot=rot: g.dma_start(
                out=va[rot][:], in_=VA[u].rearrange("(kb pp) e -> pp kb e", pp=128)),
                waits=last([in_free[2][rot]]))
            tab = p.op(abk[rot], lambda g, u=u, rot=rot: g.dma_start(
                out=ab[rot][:], in_=AB[u]), waits=last([in_free[3][rot]]))

            prev_sqmm = None
            nsegs = T // 512
            for which, src, tsrc in ((0, qa[rot], tqa), (1, ka[rot], tka)):
                ev = None
                for seg in range(nsegs):
                    sl = slice(seg * 512, (seg + 1) * 512)
                    tsq = p.op("v", lambda v, src=src, sl=sl: v.tensor_tensor(
                        out=sq[:], in0=src[0:64, sl], in1=src[0:64, sl], op=AL.mult),
                        waits=last([tsrc], [prev_sqmm]))
                    m = p.op("t", lambda t: t.matmul(ps_n[:], c["ones_col"][0:64, :],
                             sq[:], start=True, stop=True),
                             waits=last([tsq], c["ones_ready"], [ev]))
                    prev_sqmm = m
                    ev = p.op("v", lambda v, which=which, seg=seg: v.tensor_copy(
                        nrm[:, (which * nsegs + seg) * 512:(which * nsegs + seg + 1) * 512],
                        ps_n[:]), waits=[m])
                p.op("v", lambda v, which=which: v.reduce_max(
                    nb2[:, which:which + 1],
                    nrm[:, which * nsegs * 512:(which + 1) * nsegs * 512],
                    axis=AX.X), waits=[ev])
            tadd = p.op("v", lambda v: v.tensor_tensor(out=mgsum[:], in0=nb2[:, 0:1],
                        in1=nb2[:, 1:2], op=AL.mult), waits=[("v", p.cnt["v"])])
            tln = p.op("s", lambda s: s.activation(out=mgsum[:], in_=mgsum[:],
                       func=AF.Ln), waits=[tadd])
            texp = p.op("s", lambda s: s.activation(out=mgsum[:], in_=mgsum[:],
                        func=AF.Exp, scale=0.5), waits=[tln])
            tmgb = p.op("v", lambda v: v.tensor_copy(mgb[:], mgsum[:]),
                        waits=last([texp], [last_tmg]))
            tmg = p.op("t", lambda t: t.matmul(ps_mg[:], c["ones_row"][:], mgb[:],
                       start=True, stop=True),
                       waits=last([tmgb], c["ones_ready"], [last_tbias]))
            last_tmg = tmg
            tbias = p.op("v", lambda v, rot=rot: v.tensor_scalar(out=biasu[rot][:],
                         in0=ab[rot][:], scalar1=ps_mg[:], scalar2=None,
                         op0=AL.subtract),
                         waits=last([tmg], [tab], [bias_free[rot]]))
            last_tbias = tbias
            in_free[3][rot] = tbias

            strip_toks = []
            for qt in range(NQT):
                orot = qt % 2
                qsl = slice(qt * 512, (qt + 1) * 512)
                nkb = 4 * qt + 4
                kb0 = kb_min(slot_slopes[u], qt)
                kbs = list(range(kb0, nkb))
                av_tok = None
                ms_toks = {}

                def emit_ms(kb, qsl=qsl, rot=rot):
                    srot = kb % 3
                    ksl = slice(kb * 128, (kb + 1) * 128)
                    ms_toks[kb] = p.op("t", lambda t, rot=rot, ksl=ksl, qsl=qsl,
                                       srot=srot:
                                       t.matmul(ps_s[srot][:], ka[rot][:, ksl],
                                                qa[rot][:, qsl], start=True, stop=True),
                                       waits=last([tka], [tqa], [pss_free[srot]]))

                emit_ms(kbs[0])
                if len(kbs) > 1:
                    emit_ms(kbs[1])
                for ki, kb in enumerate(kbs):
                    if ki + 2 < len(kbs):
                        emit_ms(kbs[ki + 2])
                    srot = kb % 3
                    prot = kb % 3
                    te = p.op("s", lambda s, srot=srot, prot=prot, kb=kb, rot=rot:
                              s.activation(out=psb[prot][:], in_=ps_s[srot][:],
                                           func=AF.Exp, bias=biasu[rot][:, kb:kb + 1]),
                              waits=last([ms_toks[kb]], [tbias], [psb_free[prot]]))
                    pss_free[srot] = te
                    ptok = te
                    if kb >= 4 * qt:
                        ptok = p.op("g", lambda g, prot=prot, qt=qt, kb=kb:
                                    g.affine_select(out=psb[prot][:], in_=psb[prot][:],
                                                    pattern=[[1, 512]],
                                                    compare_op=AL.is_ge, fill=0.0,
                                                    base=qt * 512 - kb * 128,
                                                    channel_multiplier=-1),
                                    waits=[te])
                    av_tok = p.op("t", lambda t, rot=rot, kb=kb, orot=orot, prot=prot,
                                  kb0=kb0, nkb=nkb: t.matmul(
                                      ps_out[orot][:],
                                      va[rot][:, kb, :],
                                      psb[prot][:], start=(kb == kb0),
                                      stop=(kb == nkb - 1)),
                                  waits=last([ptok], [tva], [pso_free[orot]]),
                                  inc=(kb == nkb - 1))
                    psb_free[prot] = av_tok
                trc = p.op("v", lambda v, orot=orot: v.reciprocal(
                    rcp[:], ps_out[orot][64:65, :]), waits=[av_tok])
                trcb = p.op("v", lambda v: v.tensor_copy(rcpb[:], rcp[:]), waits=[trc])
                tmb = p.op("t", lambda t: t.matmul(ps_rb[:], c["ones_row"][:, 0:64],
                           rcpb[:], start=True, stop=True),
                           waits=last([trcb], c["ones_ready"], [("v", p.cnt["v"])]))
                trb = p.op("v", lambda v: v.tensor_copy(rbsb[:], ps_rb[:]), waits=[tmb])
                tst = p.op("v", lambda v, orot=orot, qsl=qsl, rot=rot:
                           v.scalar_tensor_tensor(out=ot[rot][:, qsl],
                                                  in0=ps_out[orot][0:64, :], scalar=1.0,
                                                  in1=rbsb[:], op0=AL.mult, op1=AL.mult),
                           waits=last([trb], [ot_free[rot]]))
                pso_free[orot] = tst
                strip_toks.append(tst)
            dout = p.op(otk[rot], lambda h, u=u, rot=rot: h.dma_start(
                out=OT[u], in_=ot[rot][:]), waits=strip_toks)
            ot_free[rot] = dout
            in_free[0][rot] = av_tok
            in_free[1][rot] = av_tok
            in_free[2][rot] = av_tok
            bias_free[rot] = av_tok

        p.op("S", lambda h: h.nop(), waits=all_dma_bar(p), inc=False)
        p.finalize(es)
    return p



def emit_ffn(p, es, c, name, xn_tiles, xn_toks, f1T, b1_sb, b1_tok, f2T,
             b2_sb, b2_tok, x_tiles, x_toks, tok=TOK):
    nc = p.nc
    NH = 32
    TTH = 512
    h_sb = es.enter_context(nc.sbuf_tensor(name + "_h", [128, NH, TTH], BF16))
    w1b = [es.enter_context(nc.sbuf_tensor(f"{name}_w1{i}", [128, NCH, 128], BF16))
           for i in range(2)]
    w2b = [es.enter_context(nc.sbuf_tensor(f"{name}_w2{i}", [128, NH, 128], BF16))
           for i in range(2)]
    w1k = [p.key("G", f"{name}_w1k{i}") for i in range(2)]
    w2k = [p.key("G", f"{name}_w2k{i}") for i in range(2)]
    pss = [c["ps"][0], c["ps"][1]]
    f1src = f1T.rearrange("(dc pp) o -> pp dc o", pp=128)
    f2src = f2T.rearrange("(hc pp) o -> pp hc o", pp=128)
    w1_free = [None, None]
    w2_free = [None, None]
    ps_free = [c["ps_free"].get(0), c["ps_free"].get(1)]
    h_free = [None] * NH
    new_x_toks = list(x_toks)
    for tp in range(tok // TTH):
        tsl0 = tp * TTH
        for hc in range(NH):
            ws = hc % 2
            wt = p.op(w1k[ws], lambda g, hc=hc, ws=ws: g.dma_start(
                out=w1b[ws][:], in_=f1src[:, :, hc * 128:(hc + 1) * 128]),
                waits=last([w1_free[ws]]))
            for sub in range(TTH // 512):
                psl = (hc * 2 + sub) % 2
                sl = slice(tsl0 + sub * 512, tsl0 + (sub + 1) * 512)
                m = None
                for dc in range(NCH):
                    m = p.op("t", lambda t, dc=dc, sl=sl, ws=ws, psl=psl:
                             t.matmul(pss[psl][:], w1b[ws][:, dc, :],
                                      xn_tiles[dc][:, sl], start=(dc == 0),
                                      stop=(dc == NCH - 1)),
                             waits=last([wt], [xn_toks[dc]], [ps_free[psl]]),
                             inc=(dc == NCH - 1))
                ev = p.op("s", lambda s, hc=hc, sub=sub, psl=psl: s.activation(
                    out=h_sb[:, hc, sub * 512:(sub + 1) * 512], in_=pss[psl][:],
                    func=AF.Gelu, bias=b1_sb[:, hc:hc + 1]),
                    waits=last([m], [b1_tok], [h_free[hc]]))
                ps_free[psl] = ev
                c["ps_free"][psl] = ev
                if sub == TTH // 512 - 1:
                    w1_free[ws] = m
        ev_h_last = None
        for oc in range(NCH):
            ws = oc % 2
            wt = p.op(w2k[ws], lambda g, oc=oc, ws=ws: g.dma_start(
                out=w2b[ws][:], in_=f2src[:, :, oc * 128:(oc + 1) * 128]),
                waits=last([w2_free[ws]]))
            for sub in range(TTH // 512):
                psl = (oc * 2 + sub) % 2
                sl = slice(tsl0 + sub * 512, tsl0 + (sub + 1) * 512)
                ssub = slice(sub * 512, (sub + 1) * 512)
                m = None
                for hc in range(NH):
                    m = p.op("t", lambda t, hc=hc, sl=sl, ssub=ssub, ws=ws, psl=psl:
                             t.matmul(pss[psl][:], w2b[ws][:, hc, :],
                                      h_sb[:, hc, ssub], start=(hc == 0),
                                      stop=(hc == NH - 1)),
                             waits=last([wt], [("s", p.cnt["s"])] if hc == 0 else [],
                                        [ps_free[psl]]),
                             inc=(hc == NH - 1))
                ev = p.op("v", lambda v, oc=oc, sl=sl, psl=psl:
                          v.scalar_tensor_tensor(out=x_tiles[oc][:, sl],
                                                 in0=pss[psl][:],
                                                 scalar=b2_sb[:, oc:oc + 1],
                                                 in1=x_tiles[oc][:, sl],
                                                 op0=AL.add, op1=AL.add),
                          waits=last([m], [b2_tok], [new_x_toks[oc]]))
                ps_free[psl] = ev
                c["ps_free"][psl] = ev
                new_x_toks[oc] = ev
                ev_h_last = m
            w2_free[ws] = ev_h_last
        for hc in range(NH):
            h_free[hc] = ev_h_last
    return new_x_toks


def build_tok(kind, with_rope=False, tok=TOK):
    p = Prog()
    nc = p.nc
    es = ExitStack()
    c = Ctx()
    xT = nc.dram_tensor("xT", [D, tok], F32, kind="ExternalInput").ap()
    attnT = nc.dram_tensor("attnT", [D, tok], BF16, kind="ExternalInput").ap()
    woT = nc.dram_tensor("woT", [D, D], F32, kind="ExternalInput").ap()
    bo = nc.dram_tensor("bo", [128, 8], F32, kind="ExternalInput").ap()
    if kind in ("inner", "final"):
        ln2p = nc.dram_tensor("ln2p", [128, 16], F32, kind="ExternalInput").ap()
        f1T = nc.dram_tensor("f1T", [D, 4 * D], F32, kind="ExternalInput").ap()
        b1 = nc.dram_tensor("b1", [128, 32], F32, kind="ExternalInput").ap()
        f2T = nc.dram_tensor("f2T", [4 * D, D], F32, kind="ExternalInput").ap()
        b2 = nc.dram_tensor("b2", [128, 8], F32, kind="ExternalInput").ap()
    if kind == "inner":
        ln1p = nc.dram_tensor("ln1p", [128, 16], F32, kind="ExternalInput").ap()
        wiT = nc.dram_tensor("wiT", [D, 3 * D], F32, kind="ExternalInput").ap()
        bqk = nc.dram_tensor("bqk", [128, 16], F32, kind="ExternalInput").ap()
        bv_row = nc.dram_tensor("bv_row", [1, D], F32, kind="ExternalInput").ap()
        qTd = nc.dram_tensor("qT", [D, tok], BF16, kind="ExternalOutput").ap()
        kTd = nc.dram_tensor("kT", [D, tok], BF16, kind="ExternalOutput").ap()
        vTokd = nc.dram_tensor("vTok", [tok, D], BF16, kind="ExternalOutput").ap()
    if kind == "final":
        lnfp = nc.dram_tensor("lnfp", [128, 16], F32, kind="ExternalInput").ap()
        xinT = nc.dram_tensor("xinT", [D, tok], F32, kind="ExternalInput").ap()
    xoT = nc.dram_tensor("xoT", [D, tok], F32, kind="ExternalOutput").ap()

    with es:
        setup_common(p, es, c)
        x_tiles, x_toks = load_x(p, es, c, xT, "x", F32, NCH, tok)
        bo_sb, _ = load_tile(p, es, "sb_bo", [128, 8], F32, bo)
        if kind in ("inner", "final"):
            ln2_sb, _ = load_tile(p, es, "sb_ln2", [128, 16], F32, ln2p)
            b1_sb, _ = load_tile(p, es, "sb_b1", [128, 32], F32, b1)
            b2_sb, _ = load_tile(p, es, "sb_b2", [128, 8], F32, b2)
        if kind == "inner":
            ln1_sb, _ = load_tile(p, es, "sb_ln1", [128, 16], F32, ln1p)
            bqk_sb, _ = load_tile(p, es, "sb_bqk", [128, 16], F32, bqk)
            bvb, _ = load_tile(p, es, "sb_bvb", [1, D], BF16, bv_row, queue="G")
        if kind == "final":
            lnf_sb, _ = load_tile(p, es, "sb_lnf", [128, 16], F32, lnfp)
        sbar = setup_bar(p)
        bo_tok = sbar
        ln2_tok = b1_tok = b2_tok = ln1_tok = lnf_tok = bv_tok = sbar
        if kind == "inner":
            bqk_tok = p.op("v", lambda v: v.tensor_scalar(out=bqk_sb[:, 0:8],
                           in0=bqk_sb[:, 0:8], scalar1=0.125, scalar2=None,
                           op0=AL.mult), waits=sbar)
        t1s = es.enter_context(nc.sbuf_tensor("t1s", [128, tok], F32))
        lnsc = ln_scratch(p, es, "lnsc", tok)

        def evac_resid(oc, tt, ps, mm_tok):
            sl = slice(tt * 512, (tt + 1) * 512)
            ev = p.op("v", lambda v, oc=oc, sl=sl, ps=ps: v.scalar_tensor_tensor(
                out=x_tiles[oc][:, sl], in0=ps[:], scalar=bo_sb[:, oc:oc + 1],
                in1=x_tiles[oc][:, sl], op0=AL.add, op1=AL.add),
                waits=last([mm_tok], [bo_tok], [x_toks[oc]]))
            x_toks[oc] = ev
            return ev

        with ExitStack() as es2:
            at_tiles, at_toks = load_x(p, es2, c, attnT, "at", BF16, NCH, tok)
            emit_proj_feat(p, es2, c, "wo", at_tiles, at_toks, woT, D, evac_resid, tok=tok)
        p.barrier()

        xn_tiles = [es.enter_context(nc.sbuf_tensor(f"xn{i}", [128, tok], BF16))
                    for i in range(NCH)]

        if kind in ("inner", "final"):
            xn_toks = emit_ln(p, es, c, "ln2", x_tiles, x_toks, xn_tiles,
                              w_ap=lambda cc: ln2_sb[:, cc:cc + 1],
                              b_ap=lambda cc: ln2_sb[:, 8 + cc:8 + cc + 1],
                              tok=tok, param_toks=ln2_tok, sc=lnsc,
                              t1_tiles=[t1s] * NCH, t1_toks=[None] * NCH)
            with ExitStack() as es3:
                x_toks = emit_ffn(p, es3, c, "ffn", xn_tiles, xn_toks, f1T, b1_sb,
                                  b1_tok, f2T, b2_sb, b2_tok, x_tiles, x_toks, tok=tok)
            p.barrier()

        if kind == "inner":
            xn_toks = emit_ln(p, es, c, "ln1n", x_tiles, x_toks, xn_tiles,
                              w_ap=lambda cc: ln1_sb[:, cc:cc + 1],
                              b_ap=lambda cc: ln1_sb[:, 8 + cc:8 + cc + 1],
                              tok=tok, param_toks=ln1_tok, sc=lnsc,
                              t1_tiles=[t1s] * NCH, t1_toks=[None] * NCH)
            emit_qkv_part(p, es, c, "qkv", xn_tiles, xn_toks, wiT, bqk_sb, bqk_tok,
                          bvb, bv_tok, qTd, kTd, vTokd, rope=None, tok=tok)
        elif kind == "final":
            y_toks = emit_ln(p, es, c, "lnf", x_tiles, x_toks, xn_tiles,
                             w_ap=lambda cc: lnf_sb[:, cc:cc + 1],
                             b_ap=lambda cc: lnf_sb[:, 8 + cc:8 + cc + 1],
                             tok=tok, param_toks=lnf_tok, sc=lnsc,
                             t1_tiles=[t1s] * NCH, t1_toks=[None] * NCH)
            xik = p.key("S", "xik")
            xin_last = None
            for cc in range(NCH):
                xin_last = p.op(xik, lambda h, cc=cc: h.dma_start(
                    out=x_tiles[cc][:], in_=xinT[cc * 128:(cc + 1) * 128, :]),
                    waits=[y_toks[cc], ("v", p.cnt["v"])])
            xin_toks = [xin_last] * NCH
            add_toks = [p.op("v", lambda v, cc=cc: v.tensor_tensor(
                out=x_tiles[cc][:], in0=x_tiles[cc][:], in1=xn_tiles[cc][:],
                op=AL.add), waits=[xin_toks[cc], y_toks[cc]]) for cc in range(NCH)]
            x_toks = emit_ln(p, es, c, "pln", x_tiles, add_toks, x_tiles,
                             tok=tok, sc=lnsc)
        elif kind == "wd":
            x_toks = emit_ln(p, es, c, "pln", x_tiles, x_toks, x_tiles,
                             tok=tok, sc=lnsc)

        xok = p.key("S", "xok")
        for cc in range(NCH):
            p.op(xok, lambda h, cc=cc: h.dma_start(
                out=xoT[cc * 128:(cc + 1) * 128, :], in_=x_tiles[cc][:]),
                waits=[x_toks[cc]])
        p.op("S", lambda h: h.nop(), waits=all_dma_bar(p), inc=False)
        p.finalize(es)
    return p



def emit_router_coef(p, es, c, name, xb_tiles, xb_toks, rw_sb, rw_tok, gb_sb,
                     gb_tok, mode, coefD, tok=TOK):
    nc = p.nc
    NE = 8
    lg = es.enter_context(nc.sbuf_tensor(name + "_lg", [128, NE], F32))
    l2 = es.enter_context(nc.sbuf_tensor(name + "_l2", [128, NE], F32))
    eq1 = es.enter_context(nc.sbuf_tensor(name + "_eq1", [128, NE], F32))
    eq2 = es.enter_context(nc.sbuf_tensor(name + "_eq2", [128, NE], F32))
    cf = es.enter_context(nc.sbuf_tensor(name + "_cf", [128, NE], F32))
    m1 = es.enter_context(nc.sbuf_tensor(name + "_m1", [128, 1], F32))
    m2 = es.enter_context(nc.sbuf_tensor(name + "_m2", [128, 1], F32))
    w1 = es.enter_context(nc.sbuf_tensor(name + "_w1", [128, 1], F32))
    w2 = es.enter_context(nc.sbuf_tensor(name + "_w2", [128, 1], F32))
    sm = es.enter_context(nc.sbuf_tensor(name + "_sm", [128, 1], F32))
    iot = es.enter_context(nc.sbuf_tensor(name + "_iot", [128, NE], F32))
    cfb = es.enter_context(nc.sbuf_tensor(name + "_cfb", [128, NE], BF16))
    wiRow = None
    wi_toks = []
    if mode == "moe1":
        wiRow = es.enter_context(nc.sbuf_tensor(name + "_wiR", [1, 8 * tok], BF16))
        pst = es.enter_context(nc.psum_tensor(name + "_pst", [1, 128], BF16))
        ident = c["ident"]
    onesf = es.enter_context(nc.sbuf_tensor(name + "_onesf", [1, 128], F32))
    of = p.op("g", lambda g: g.memset(onesf[:], 1.0))
    it = p.op("g", lambda g: g.iota(iot[:], pattern=[[1, NE]], base=0,
              channel_multiplier=0, allow_small_or_imprecise_dtypes=True))
    ps = c["ps"][4][:, 0:NE]
    prev = c["ps_free"].get(4)
    coef_toks = []
    for tt in range(tok // 128):
        tsl = slice(tt * 128, (tt + 1) * 128)
        m = None
        for dc in range(NCH):
            m = p.op("t", lambda t, dc=dc, tsl=tsl: t.matmul(
                ps, xb_tiles[dc][:, tsl], rw_sb[:, dc, :],
                start=(dc == 0), stop=False),
                waits=last([rw_tok], [xb_toks[dc]], [prev]), inc=False)
        m = p.op("t", lambda t: t.matmul(ps, onesf[:], gb_sb[:],
                 start=False, stop=True),
                 waits=last([gb_tok], [of]))
        tl = p.op("v", lambda v: v.tensor_copy(lg[:], ps), waits=last([m], [prev]))
        prev = tl
        c["ps_free"][4] = tl
        ncl = 7 if mode == "moe0" else 8
        a1 = p.op("v", lambda v, ncl=ncl: v.reduce_max(m1[:], lg[:, 0:ncl], axis=AX.X),
                  waits=[tl])
        a2 = p.op("v", lambda v: v.tensor_scalar(out=eq1[:], in0=lg[:], scalar1=m1[:],
                  scalar2=None, op0=AL.is_equal), waits=[a1])
        a3 = p.op("v", lambda v: v.scalar_tensor_tensor(out=l2[:], in0=eq1[:],
                  scalar=-1e30, in1=lg[:], op0=AL.mult, op1=AL.add), waits=[a2])
        a4 = p.op("v", lambda v, ncl=ncl: v.reduce_max(m2[:], l2[:, 0:ncl], axis=AX.X),
                  waits=[a3])
        a5 = p.op("v", lambda v: v.tensor_scalar(out=eq2[:], in0=lg[:], scalar1=m2[:],
                  scalar2=None, op0=AL.is_equal), waits=[a4])
        if mode == "moe0":
            b1t = p.op("v", lambda v: v.tensor_scalar(out=w2[:], in0=m2[:], scalar1=m1[:],
                       scalar2=None, op0=AL.subtract), waits=[a4])
            b2t = p.op("s", lambda s: s.activation(out=w1[:], in_=w2[:], func=AF.Sigmoid,
                       scale=-1.0), waits=[b1t])
            b3t = p.op("v", lambda v: v.tensor_scalar(out=w2[:], in0=w1[:], scalar1=-1.0,
                       scalar2=1.0, op0=AL.mult, op1=AL.add), waits=[b2t])
            b4t = p.op("v", lambda v: v.tensor_scalar(out=cf[:], in0=eq1[:], scalar1=w1[:],
                       scalar2=None, op0=AL.mult),
                       waits=last([b2t], [a2], coef_toks[-1:]))
            b5t = p.op("v", lambda v: v.scalar_tensor_tensor(out=cf[:, 0:7], in0=eq2[:, 0:7],
                       scalar=w2[:], in1=cf[:, 0:7], op0=AL.mult, op1=AL.add),
                       waits=[b3t, b4t, a5])
            b6t = p.op("g", lambda g: g.memset(cf[:, 7:8], 1.0), waits=[b5t])
            done = [b6t, b5t]
        else:
            en = p.op("v", lambda v: v.tensor_scalar(out=w2[:], in0=m1[:], scalar1=-1.0,
                      scalar2=None, op0=AL.mult), waits=[a1])
            e1 = p.op("s", lambda s: s.activation(out=l2[:], in_=lg[:], func=AF.Exp,
                      bias=w2[:]), waits=last([a5], [en], coef_toks[-1:]))
            e2 = p.op("v", lambda v: v.reduce_sum(sm[:], l2[:], axis=AX.X), waits=[e1])
            e3 = p.op("v", lambda v: v.reciprocal(w1[:], sm[:]), waits=[e2])
            e4 = p.op("v", lambda v: v.tensor_scalar(out=cf[:], in0=l2[:], scalar1=w1[:],
                      scalar2=None, op0=AL.mult), waits=[e3])
            s1t = p.op("v", lambda v: v.tensor_tensor(out=eq1[:], in0=eq1[:], in1=eq2[:],
                       op=AL.add), waits=[a5, e4])
            s2t = p.op("v", lambda v: v.tensor_tensor(out=cfb[:], in0=cf[:], in1=eq1[:],
                       op=AL.mult), waits=[s1t])
            s3t = p.op("v", lambda v: v.tensor_tensor(out=l2[:], in0=iot[:], in1=eq1[:],
                       op=AL.mult), waits=[s1t, it])
            s4t = p.op("v", lambda v: v.reduce_max(m2[:], l2[:], axis=AX.X), waits=[s3t])
            s5t = p.op("v", lambda v: v.tensor_scalar(out=cf[:], in0=iot[:], scalar1=m2[:],
                       scalar2=None, op0=AL.is_equal), waits=[s4t])
            wv = None
            for e in range(NE):
                tr = p.op("t", lambda t, e=e: t.transpose(pst[:], cfb[:, e:e + 1],
                          ident[:]), waits=last([s2t], [c["ident_tok"]], wi_toks[-1:]))
                wv = p.op("v", lambda v, e=e, tt=tt: v.tensor_copy(
                    wiRow[0:1, e * tok + tt * 128:e * tok + (tt + 1) * 128], pst[:]),
                    waits=[tr])
                wi_toks.append(wv)
            done = [s5t, wv]
        dt = p.op(p.key("S", "cfk"), lambda h, tt=tt: h.dma_start(
            out=coefD[tt * 128:(tt + 1) * 128, :], in_=cf[:]), waits=done)
        coef_toks.append(dt)
        prev = dt
    return coef_toks, wiRow, (wi_toks[-1] if wi_toks else None)


def build_moe_h(mode, tok=TOK):
    p = Prog()
    nc = p.nc
    es = ExitStack()
    c = Ctx()
    NE = 8
    xT = nc.dram_tensor("xT", [D, tok], F32, kind="ExternalInput").ap()
    rwT = nc.dram_tensor("rwT", [D, NE], F32, kind="ExternalInput").ap()
    gbRow = nc.dram_tensor("gbRow", [1, NE], F32, kind="ExternalInput").ap()
    if mode == "moe0":
        wgu = nc.dram_tensor("wgu", [NE, D, 2 * 4096], F32, kind="ExternalInput").ap()
    else:
        w1T = nc.dram_tensor("w1T", [NE, D, 4096], F32, kind="ExternalInput").ap()
        b1t = nc.dram_tensor("b1t", [128, NE, 32], F32, kind="ExternalInput").ap()
        identD = nc.dram_tensor("identD", [128, 128], F32, kind="ExternalInput").ap()
    hD = nc.dram_tensor("hD", [NE, 4096, tok], BF16, kind="ExternalOutput").ap()
    coefD = nc.dram_tensor("coefD", [tok, NE], F32, kind="ExternalOutput").ap()

    with es:
        setup_common(p, es, c)
        x_tiles, x_toks = load_x(p, es, c, xT, "x", F32, NCH, tok)
        xb_tiles = [es.enter_context(nc.sbuf_tensor(f"xb{i}", [128, tok], BF16))
                    for i in range(NCH)]
        xb_toks = [p.op("s", lambda s, cc=cc: s.activation(
            out=xb_tiles[cc][:], in_=x_tiles[cc][:], func=AF.Copy),
            waits=[x_toks[cc]]) for cc in range(NCH)]
        rw_sb, _ = load_tile(p, es, "sb_rw", [128, NCH, NE], F32,
                             rwT.rearrange("(dc pp) e -> pp dc e", pp=128))
        gb_sb, _ = load_tile(p, es, "sb_gb", [1, NE], F32, gbRow)
        if mode == "moe1":
            c["ident"], _ = load_tile(p, es, "sb_id", [128, 128], BF16,
                                      identD, queue="G")
            b1_sb, _ = load_tile(p, es, "sb_b1t", [128, NE, 32], F32, b1t)
        sbar = setup_bar(p)
        rw_tok = gb_tok = b1_tok = sbar
        c["ident_tok"] = sbar

        coef_toks, wiRow, wi_tok = emit_router_coef(
            p, es, c, "rt", x_tiles, x_toks, rw_sb, rw_tok, gb_sb, gb_tok,
            mode, coefD, tok=tok)

        wb = [es.enter_context(nc.sbuf_tensor(f"wgub{i}", [128, NCH, 128], BF16))
              for i in range(4)]
        wbk = [p.key("G", f"wgubk{i}") for i in range(4)]
        hsk = [p.key("S", f"hsk{i}") for i in range(2)]
        sg = [es.enter_context(nc.sbuf_tensor(f"sg{i}", [128, 512], F32))
              for i in range(2)]
        hstrip = [es.enter_context(nc.sbuf_tensor(f"hstrip{i}", [128, tok], BF16))
                  for i in range(2)]
        cbc = es.enter_context(nc.sbuf_tensor("cbc", [128, tok], BF16))
        wfree = [None] * 4
        sgfree = [None, None]
        hfree = [None, None]
        cbc_free = None
        ntt = tok // 512
        for e in range(NE):
            if mode == "moe1":
                cb_toks = []
                for tt in range(ntt):
                    sl = slice(tt * 512, (tt + 1) * 512)
                    m = p.op("t", lambda t, e=e, sl=sl: t.matmul(
                        c["ps"][5][:], c["ones_row"][:],
                        wiRow[0:1, e * tok + sl.start:e * tok + sl.stop],
                        start=True, stop=True),
                        waits=last([wi_tok], c["ones_ready"], [c["ps_free"].get(5)],
                                   [cbc_free]))
                    ev = p.op("v", lambda v, sl=sl: v.tensor_copy(cbc[:, sl],
                              c["ps"][5][:]), waits=[m])
                    c["ps_free"][5] = ev
                    cb_toks.append(ev)
            for hc in range(32):
                if mode == "moe0":
                    ws, ws2 = (hc % 2), 2 + (hc % 2)
                    wt1 = p.op(wbk[ws], lambda g, e=e, hc=hc, ws=ws: g.dma_start(
                        out=wb[ws][:], in_=wgu[e].rearrange("(dc pp) o -> pp dc o", pp=128)
                        [:, :, hc * 128:(hc + 1) * 128]), waits=last([wfree[ws]]))
                    wt2 = p.op(wbk[ws2], lambda g, e=e, hc=hc, ws2=ws2: g.dma_start(
                        out=wb[ws2][:], in_=wgu[e].rearrange("(dc pp) o -> pp dc o", pp=128)
                        [:, :, 4096 + hc * 128:4096 + (hc + 1) * 128]),
                        waits=last([wfree[ws2]]))
                else:
                    ws = hc % 2
                    wt1 = p.op(wbk[ws], lambda g, e=e, hc=hc, ws=ws: g.dma_start(
                        out=wb[ws][:], in_=w1T[e].rearrange("(dc pp) o -> pp dc o", pp=128)
                        [:, :, hc * 128:(hc + 1) * 128]), waits=last([wfree[ws]]))
                hrot = hc % 2
                for tt in range(ntt):
                    sl = slice(tt * 512, (tt + 1) * 512)
                    pa, pb = tt % 2, 2 + tt % 2
                    mg_ = None
                    for dc in range(NCH):
                        mg_ = p.op("t", lambda t, dc=dc, sl=sl, ws=ws, pa=pa:
                                   t.matmul(c["ps"][pa][:], wb[ws][:, dc, :],
                                            xb_tiles[dc][:, sl], start=(dc == 0),
                                            stop=(dc == NCH - 1)),
                                   waits=last([wt1], [xb_toks[dc]],
                                              [c["ps_free"].get(pa)]),
                                   inc=(dc == NCH - 1))
                    if mode == "moe0":
                        mu_ = None
                        for dc in range(NCH):
                            mu_ = p.op("t", lambda t, dc=dc, sl=sl, ws2=ws2, pb=pb:
                                       t.matmul(c["ps"][pb][:], wb[ws2][:, dc, :],
                                                xb_tiles[dc][:, sl], start=(dc == 0),
                                                stop=(dc == NCH - 1)),
                                       waits=last([wt2], [xb_toks[dc]],
                                                  [c["ps_free"].get(pb)]),
                                       inc=(dc == NCH - 1))
                        srot = tt % 2
                        tsg = p.op("s", lambda s, srot=srot, pa=pa: s.activation(
                            out=sg[srot][:], in_=c["ps"][pa][:], func=AF.Silu),
                            waits=last([mg_], [sgfree[srot]]))
                        c["ps_free"][pa] = tsg
                        th = p.op("v", lambda v, srot=srot, pb=pb, sl=sl, hrot=hrot:
                                  v.tensor_tensor(out=hstrip[hrot][:, sl],
                                                  in0=sg[srot][:], in1=c["ps"][pb][:],
                                                  op=AL.mult),
                                  waits=last([tsg], [mu_], [hfree[hrot]]))
                        c["ps_free"][pb] = th
                        sgfree[srot] = th
                    else:
                        srot = tt % 2
                        tm = p.op("v", lambda v, srot=srot, pa=pa, sl=sl:
                                  v.scalar_tensor_tensor(out=sg[srot][:], in0=cbc[:, sl],
                                                         scalar=1.0, in1=c["ps"][pa][:],
                                                         op0=AL.mult, op1=AL.mult),
                                  waits=last([mg_], cb_toks, [sgfree[srot]]))
                        c["ps_free"][pa] = tm
                        th = p.op("s", lambda s, srot=srot, sl=sl, hrot=hrot, e=e, hc=hc:
                                  s.activation(out=hstrip[hrot][:, sl], in_=sg[srot][:],
                                               func=AF.Relu,
                                               bias=b1_sb[:, e, hc:hc + 1]),
                                  waits=last([tm], [b1_tok], [hfree[hrot]]))
                        sgfree[srot] = th
                    if tt == ntt - 1:
                        wfree[ws] = mg_
                        if mode == "moe0":
                            wfree[ws2] = mu_
                        hd = p.op(hsk[hrot], lambda h, e=e, hc=hc, hrot=hrot: h.dma_start(
                            out=hD[e][hc * 128:(hc + 1) * 128, :], in_=hstrip[hrot][:]),
                            waits=[th])
                        hfree[hrot] = hd
            cbc_free = ("t", p.cnt["t"]) if mode == "moe1" else None
        p.op("S", lambda h: h.nop(), waits=all_dma_bar(p), inc=False)
        p.finalize(es)
    return p


def build_moe_down(mode, tok=TOK):
    p = Prog()
    nc = p.nc
    es = ExitStack()
    c = Ctx()
    NE = 8
    TP = 1024
    xT = nc.dram_tensor("xT", [D, tok], F32, kind="ExternalInput").ap()
    hD = nc.dram_tensor("hD", [NE, 4096, tok], BF16, kind="ExternalInput").ap()
    coefD = nc.dram_tensor("coefD", [tok, NE], F32, kind="ExternalInput").ap()
    wd = nc.dram_tensor("wd", [NE, 4096, D], F32, kind="ExternalInput").ap()
    b2Row = nc.dram_tensor("b2Row", [1, NE * D], F32, kind="ExternalInput").ap()
    identD = nc.dram_tensor("identD", [128, 128], F32, kind="ExternalInput").ap()
    xoT = nc.dram_tensor("xoT", [D, tok], F32, kind="ExternalOutput").ap()

    with es:
        setup_common(p, es, c)
        identf, _ = load_tile(p, es, "sb_idf", [128, 128], F32, identD)
        coef_sb, _ = load_tile(p, es, "sb_coef", [128, tok // 128, NE], F32,
                               coefD.rearrange("(tt pp) e -> pp tt e", pp=128))
        b2_sb, _ = load_tile(p, es, "sb_b2r", [1, NE * D], BF16, b2Row, queue="G")
        sbar = setup_bar(p)
        identf_tok = coef_tok = b2_tok = sbar
        acc = es.enter_context(nc.sbuf_tensor("acc", [128, TP // 128, D], F32))
        hbuf = [es.enter_context(nc.sbuf_tensor(f"hbuf{i}", [128, 16, TP], BF16))
                for i in range(2)]
        wdb = es.enter_context(nc.sbuf_tensor("wdb", [128, 32, 512], BF16))
        xs_tiles = [es.enter_context(nc.sbuf_tensor(f"xs{i}", [128, TP], F32))
                    for i in range(NCH)]
        lnsc = ln_scratch(p, es, "lnsc", TP)
        pstr = es.enter_context(nc.psum_tensor("pstr", [128, 128], F32))

        hfree = [None, None]
        wd_free = [None, None]
        acc_free = {}
        accb_free = {}
        pstr_free = None
        NTS = TP // 128

        for tp in range(tok // TP):
            tpsl = slice(tp * TP, (tp + 1) * TP)
            z_toks = {}
            tbar0 = ("t", p.cnt["t"])
            for ts in range(NTS):
                z_toks[ts] = p.op("g", lambda g, ts=ts: g.memset(acc[:, ts, :], 0.0),
                                  waits=[tbar0])
            for e in range(NE):
                ht = []
                hbk = [p.key("G", f"hbk{i}") for i in range(2)]
                for half in range(2):
                    ht.append(p.op(hbk[half], lambda g, e=e, half=half, tpsl=tpsl:
                                   g.dma_start(out=hbuf[half][:],
                                               in_=hD[e].rearrange("(hc pp) t -> pp hc t", pp=128)
                                               [:, half * 16:(half + 1) * 16, tpsl]),
                                   waits=last([hfree[half]])))
                for dc2 in range(2):
                    dsl = slice(dc2 * 512, (dc2 + 1) * 512)
                    wt = []
                    wdk = [p.key("G", f"wdk{i}") for i in range(2)]
                    for half in range(2):
                        wt.append(p.op(wdk[half], lambda g, e=e, half=half, dsl=dsl:
                                       g.dma_start(out=wdb[:, half * 16:(half + 1) * 16, :],
                                                   in_=wd[e].rearrange("(hc pp) o -> pp hc o", pp=128)
                                                   [:, half * 16:(half + 1) * 16, dsl]),
                                       waits=last([wd_free[half]])))
                    last_mm = None
                    for ts in range(NTS):
                        psl = ts % 4
                        m = None
                        for hc in range(32):
                            m = p.op("t", lambda t, hc=hc, ts=ts, psl=psl:
                                     t.matmul(c["ps"][psl][:],
                                              hbuf[hc // 16][:, hc % 16, ts * 128:(ts + 1) * 128],
                                              wdb[:, hc, :], start=(hc == 0), stop=False),
                                     waits=last([ht[hc // 16]], [wt[hc // 16]],
                                                [c["ps_free"].get(psl)]), inc=False)
                        m = p.op("t", lambda t, e=e, dc2=dc2, psl=psl: t.matmul(
                            c["ps"][psl][:], c["ones_row"][:],
                            b2_sb[:, e * D + dc2 * 512:e * D + (dc2 + 1) * 512],
                            start=False, stop=True),
                            waits=last([b2_tok], c["ones_ready"]))
                        ev = p.op("v", lambda v, e=e, ts=ts, tp=tp, dsl=dsl, psl=psl:
                                  v.scalar_tensor_tensor(
                                      out=acc[:, ts, dsl], in0=c["ps"][psl][:],
                                      scalar=coef_sb[:, tp * NTS + ts, e:e + 1],
                                      in1=acc[:, ts, dsl], op0=AL.mult, op1=AL.add),
                                  waits=last([m], [coef_tok], [z_toks[ts]],
                                             [acc_free.get((tp, ts, dsl.start))]))
                        acc_free[(tp, ts, dsl.start)] = ev
                        c["ps_free"][psl] = ev
                        last_mm = m
                    wd_free = [last_mm, last_mm]
                hfree = [last_mm, last_mm]
            xdk = p.key("S", "xdk")
            xok2 = p.key("S", "xok2")
            x_last = None
            for cc in range(NCH):
                x_last = p.op(xdk, lambda h, cc=cc, tpsl=tpsl: h.dma_start(
                    out=xs_tiles[cc][:], in_=xT[cc * 128:(cc + 1) * 128, tpsl]),
                    waits=last([("xok2", p.cnt["xok2"])] if tp > 0 else []))
            x_toks = [x_last] * NCH
            addt = {}
            for cc in range(NCH):
                for ts in range(NTS):
                    tr = p.op("t", lambda t, cc=cc, ts=ts: t.transpose(
                        pstr[:], acc[:, ts, cc * 128:(cc + 1) * 128], identf[:]),
                        waits=last([acc_free.get((tp, ts, 0))],
                                   [acc_free.get((tp, ts, 512))],
                                   [identf_tok], [pstr_free]))
                    ad = p.op("v", lambda v, cc=cc, ts=ts: v.scalar_tensor_tensor(
                        out=xs_tiles[cc][:, ts * 128:(ts + 1) * 128], in0=pstr[:],
                        scalar=1.0, in1=xs_tiles[cc][:, ts * 128:(ts + 1) * 128],
                        op0=AL.mult, op1=AL.add),
                        waits=last([tr], [x_toks[cc]]))
                    pstr_free = ad
                    addt[cc] = ad
            out_toks = emit_ln(p, es, c, f"pln{tp}", xs_tiles,
                               [addt[cc] for cc in range(NCH)], xs_tiles,
                               tok=TP, sc=lnsc)
            for cc in range(NCH):
                p.op(xok2, lambda h, cc=cc, tpsl=tpsl: h.dma_start(
                    out=xoT[cc * 128:(cc + 1) * 128, tpsl], in_=xs_tiles[cc][:]),
                    waits=[out_toks[cc]])
        p.op("S", lambda h: h.nop(), waits=all_dma_bar(p), inc=False)
        p.finalize(es)
    return p



def _tile8(v, n):
    return np.ascontiguousarray(v.reshape(n, 128).T).astype(np.float32)


def _lnp(w, b):
    return np.concatenate([_tile8(w, 8), _tile8(b, 8)], axis=1)


_PROG_CACHE = {}


def _prog(key, builder, *args, **kw):
    if key not in _PROG_CACHE:
        _PROG_CACHE[key] = builder(*args, **kw)
    return _PROG_CACHE[key]


def _run(p, in_maps):
    res = run_bass_kernel_spmd(p.nc, in_maps, core_ids=list(range(NCORES)))
    return res.results


def _attn_assemble(qTs, kTs, vTs, qpos_rows, ab_rows):
    QA = np.zeros((NCORES, 8, 65, T), dtype=BF)
    KA = np.zeros((NCORES, 8, 65, T), dtype=BF)
    VA = np.zeros((NCORES, 8, T, 65), dtype=BF)
    ABc = np.zeros((NCORES, 8, 128, 32), dtype=np.float32)
    for cc in range(NCORES):
      for j in range(8):
        b, h = cc % 4, 2 * j + cc // 4
        q = np.concatenate([np.asarray(qTs[2 * b])[64 * h:64 * h + 64],
                            np.asarray(qTs[2 * b + 1])[64 * h:64 * h + 64]], axis=1)
        k = np.concatenate([np.asarray(kTs[2 * b])[64 * h:64 * h + 64],
                            np.asarray(kTs[2 * b + 1])[64 * h:64 * h + 64]], axis=1)
        v = np.concatenate([np.asarray(vTs[2 * b])[:, 64 * h:64 * h + 64],
                            np.asarray(vTs[2 * b + 1])[:, 64 * h:64 * h + 64]], axis=0)
        QA[cc, j, 0:64] = q
        QA[cc, j, 64] = qpos_rows[h]
        KA[cc, j, 0:64] = k
        KA[cc, j, 64] = 1.0
        VA[cc, j, :, 0:64] = v
        VA[cc, j, :, 64] = 1.0
        ABc[cc, j] = ab_rows[h]
    return QA, KA, VA, ABc


def _attn_out_slices(OTs):
    full = np.zeros((B, D, T), dtype=BF)
    for cc in range(NCORES):
      for j in range(8):
        b, h = cc % 4, 2 * j + cc // 4
        full[b, 64 * h:64 * h + 64] = np.asarray(OTs[cc])[j]
    out = []
    for cc in range(NCORES):
        b, half = cc // 2, cc % 2
        out.append(np.ascontiguousarray(full[b][:, half * TOK:(half + 1) * TOK]))
    return out


_DBG_HOOK = None


def _dbg(name, xs):
    if _DBG_HOOK is not None:
        _DBG_HOOK(name, xs)


def kernel(**inp):
    f32 = np.float32
    x0 = np.asarray(inp["L0_attn_x"], dtype=f32)
    xT = np.ascontiguousarray(x0.reshape(B * T, D).T)
    xs = [np.ascontiguousarray(xT[:, c * TOK:(c + 1) * TOK]) for c in range(NCORES)]
    x_in = [a.copy() for a in xs]

    slopes = alibi_slopes(H)
    pos = np.arange(T, dtype=np.float64)
    qpos_rows = [(-slopes[h] * pos).astype(BF) for h in range(H)]
    p_idx = np.arange(128, dtype=np.float64)[:, None]
    kb_idx = np.arange(32, dtype=np.float64)[None, :]
    ab_rows = [(slopes[h] * (128 * kb_idx + p_idx)).astype(f32) for h in range(H)]
    ab_zero = [np.zeros((128, 32), f32)] * H
    qpos_zero = [np.zeros(T, BF)] * H
    ident = np.eye(128, dtype=f32)
    invf = 10000.0 ** (-np.arange(0, 64, 2) / 64.0)
    cosf = np.cos(pos[None, :] * invf[:, None]).astype(f32)
    sinf = np.sin(pos[None, :] * invf[:, None]).astype(f32)
    cosT = np.zeros((128, T), f32)
    sinT = np.zeros((128, T), f32)
    cosT[0:32] = cosf; cosT[64:96] = cosf
    sinT[0:32] = sinf; sinT[64:96] = sinf

    def seqpos_slice(arr2d, cc):
        half = cc % 2
        return np.ascontiguousarray(arr2d[:, half * TOK:(half + 1) * TOK])

    ipw = np.asarray(inp["L0_attn_in_proj_w"], f32)
    ipb = np.asarray(inp["L0_attn_in_proj_b"], f32)
    opw = np.asarray(inp["L0_attn_out_proj_w"], f32)
    opb = np.asarray(inp["L0_attn_out_proj_b"], f32)
    f1w = np.asarray(inp["L0_attn_ffn1_w"], f32)
    f1b = np.asarray(inp["L0_attn_ffn1_b"], f32)
    f2w = np.asarray(inp["L0_attn_ffn2_w"], f32)
    f2b = np.asarray(inp["L0_attn_ffn2_b"], f32)
    ln1w = np.asarray(inp["L0_attn_ln1_w"], f32)
    ln1b = np.asarray(inp["L0_attn_ln1_b"], f32)
    ln2w = np.asarray(inp["L0_attn_ln2_w"], f32)
    ln2b = np.asarray(inp["L0_attn_ln2_b"], f32)

    def l0_qkv_inputs(i):
        wiT = np.ascontiguousarray(ipw[i].T)
        bq = ipb[i]
        return {"wiT": wiT,
                "bqk": np.concatenate([_tile8(bq[:D], 8), _tile8(bq[D:2 * D], 8)], 1),
                "bv_row": np.ascontiguousarray(bq[2 * D:].reshape(1, D)),
                "lnp": _lnp(ln1w[i], ln1b[i])}

    p_qkv = _prog("qkv_ln", build_qkv, True, False)
    slot_slopes0 = tuple(float(slopes[min(2 * j + 1, 15)]) for j in range(8))
    p_attn = _prog("attn_l0", build_attn, T, 8, slot_slopes0)
    p_inner = _prog("tok_inner", build_tok, "inner")
    p_final = _prog("tok_final", build_tok, "final")

    l0q = l0_qkv_inputs(0)
    r = _run(p_qkv, [dict(xT=xs[c], **l0q) for c in range(NCORES)])
    qTs = [r[c]["qT"] for c in range(NCORES)]
    kTs = [r[c]["kT"] for c in range(NCORES)]
    vTs = [r[c]["vTok"] for c in range(NCORES)]

    for i in range(4):
        QA, KA, VA, ABc = _attn_assemble(qTs, kTs, vTs, qpos_rows, ab_rows)
        r = _run(p_attn, [{"QA": QA[c], "KA": KA[c], "VA": VA[c], "AB": ABc[c]}
                          for c in range(NCORES)])
        at = _attn_out_slices([r[c]["OT"] for c in range(NCORES)])
        common = {"woT": np.ascontiguousarray(opw[i].T), "bo": _tile8(opb[i], 8),
                  "ln2p": _lnp(ln2w[i], ln2b[i]),
                  "f1T": np.ascontiguousarray(f1w[i].T), "b1": _tile8(f1b[i], 32),
                  "f2T": np.ascontiguousarray(f2w[i].T), "b2": _tile8(f2b[i], 8)}
        if i < 3:
            nq = l0_qkv_inputs(i + 1)
            r = _run(p_inner, [dict(xT=xs[c], attnT=at[c], **common, ln1p=nq["lnp"],
                                    wiT=nq["wiT"], bqk=nq["bqk"], bv_row=nq["bv_row"])
                               for c in range(NCORES)])
            xs = [np.asarray(r[c]["xoT"], f32) for c in range(NCORES)]
            _dbg(f"l0_{i}", xs)
            qTs = [r[c]["qT"] for c in range(NCORES)]
            kTs = [r[c]["kT"] for c in range(NCORES)]
            vTs = [r[c]["vTok"] for c in range(NCORES)]
        else:
            fin = {"lnfp": _lnp(np.asarray(inp["L0_attn_final_w"], f32),
                                np.asarray(inp["L0_attn_final_b"], f32))}
            r = _run(p_final, [dict(xT=xs[c], attnT=at[c], **common, **fin,
                                    xinT=x_in[c]) for c in range(NCORES)])
            xs = [np.asarray(r[c]["xoT"], f32) for c in range(NCORES)]
            _dbg("xb0", xs)

    rw0 = np.asarray(inp["L0_ffn_router_w"], f32)
    rwT0 = np.concatenate([rw0.T, np.zeros((D, 1), f32)], 1)
    wg8 = np.concatenate([np.asarray(inp["L0_ffn_w_gate"], f32),
                          np.asarray(inp["L0_ffn_sw_gate"], f32)], 0)
    wu8 = np.concatenate([np.asarray(inp["L0_ffn_w_up"], f32),
                          np.asarray(inp["L0_ffn_sw_up"], f32)], 0)
    wgu = np.ascontiguousarray(np.concatenate([wg8, wu8], axis=2))
    wd8 = np.ascontiguousarray(np.concatenate([np.asarray(inp["L0_ffn_w_down"], f32),
                                               np.asarray(inp["L0_ffn_sw_down"], f32)], 0))
    gb0 = np.zeros((1, 8), f32)
    b2z = np.zeros((1, 8 * D), f32)
    p_h0 = _prog("moe0_h", build_moe_h, "moe0")
    p_d0 = _prog("moe0_d", build_moe_down, "moe0")
    r = _run(p_h0, [{"xT": xs[c], "rwT": rwT0, "gbRow": gb0, "wgu": wgu}
                    for c in range(NCORES)])
    r = _run(p_d0, [{"xT": xs[c], "hD": np.asarray(r[c]["hD"]),
                     "coefD": np.asarray(r[c]["coefD"], f32), "wd": wd8,
                     "b2Row": b2z, "identD": ident} for c in range(NCORES)])
    xs = [np.asarray(r[c]["xoT"], f32) for c in range(NCORES)]
    _dbg("xb1", xs)

    wiT1 = np.ascontiguousarray(np.concatenate(
        [np.asarray(inp["L1_attn_Wq"], f32), np.asarray(inp["L1_attn_Wk"], f32),
         np.asarray(inp["L1_attn_Wv"], f32)], axis=1))
    bq1 = np.concatenate([np.asarray(inp["L1_attn_bq"], f32),
                          np.asarray(inp["L1_attn_bk"], f32)])
    bv1 = np.asarray(inp["L1_attn_bv"], f32)
    p_qkv1 = _prog("qkv_rope", build_qkv, False, True)
    r = _run(p_qkv1, [{"xT": xs[c], "wiT": wiT1,
                       "bqk": np.concatenate([_tile8(bq1[:D], 8), _tile8(bq1[D:], 8)], 1),
                       "bv_row": bv1.reshape(1, D),
                       "cosT": seqpos_slice(cosT, c), "sinT": seqpos_slice(sinT, c)}
                      for c in range(NCORES)])
    QA, KA, VA, ABc = _attn_assemble([r[c]["qT"] for c in range(NCORES)],
                                     [r[c]["kT"] for c in range(NCORES)],
                                     [r[c]["vTok"] for c in range(NCORES)],
                                     qpos_zero, ab_zero)
    p_attn1 = _prog("attn_l1", build_attn)
    r = _run(p_attn1, [{"QA": QA[c], "KA": KA[c], "VA": VA[c], "AB": ABc[c]}
                       for c in range(NCORES)])
    at = _attn_out_slices([r[c]["OT"] for c in range(NCORES)])
    p_wd = _prog("tok_wd", build_tok, "wd")
    r = _run(p_wd, [{"xT": xs[c], "attnT": at[c],
                     "woT": np.asarray(inp["L1_attn_Wd"], f32),
                     "bo": _tile8(np.asarray(inp["L1_attn_bd"], f32), 8)}
                    for c in range(NCORES)])
    xs = [np.asarray(r[c]["xoT"], f32) for c in range(NCORES)]
    _dbg("xb2", xs)

    rwT1 = np.ascontiguousarray(np.asarray(inp["L1_ffn_gate_w"], f32).T)
    gb1 = np.asarray(inp["L1_ffn_gate_b"], f32).reshape(1, 8)
    w1T = np.ascontiguousarray(np.asarray(inp["L1_ffn_w1"], f32).transpose(0, 2, 1))
    b1m = np.asarray(inp["L1_ffn_b1"], f32)
    b1t = np.ascontiguousarray(b1m.reshape(8, 32, 128).transpose(2, 0, 1))
    w2T = np.ascontiguousarray(np.asarray(inp["L1_ffn_w2"], f32).transpose(0, 2, 1))
    b2r = np.ascontiguousarray(np.asarray(inp["L1_ffn_b2"], f32).reshape(1, 8 * D))
    p_h1 = _prog("moe1_h", build_moe_h, "moe1")
    p_d1 = _prog("moe1_d", build_moe_down, "moe1")
    r = _run(p_h1, [{"xT": xs[c], "rwT": rwT1, "gbRow": gb1, "w1T": w1T,
                     "b1t": b1t, "identD": ident} for c in range(NCORES)])
    r = _run(p_d1, [{"xT": xs[c], "hD": np.asarray(r[c]["hD"]),
                     "coefD": np.asarray(r[c]["coefD"], f32), "wd": w2T,
                     "b2Row": b2r, "identD": ident} for c in range(NCORES)])
    xs = [np.asarray(r[c]["xoT"], f32) for c in range(NCORES)]

    out = np.concatenate(xs, axis=1)
    return np.ascontiguousarray(out.T).reshape(B, T, D).astype(np.float32)


# version 07
# speedup vs baseline: 1.3970x; 1.0247x over previous
import numpy as np
import ml_dtypes
import concourse.bass as bass
import concourse.mybir as mybir
from concourse.bass_utils import run_bass_kernel_spmd
from concourse.alu_op_type import AluOpType as AL
from contextlib import ExitStack

F32 = mybir.dt.float32
BF16 = mybir.dt.bfloat16
AF = mybir.ActivationFunctionType
AX = mybir.AxisListType
BF = ml_dtypes.bfloat16

NCORES = 8
TOK = 2048
D = 1024
NCH = D // 128
B, T, H, HD = 4, 4096, 16, 64
NKB = T // 128
EPS = 1e-5
PRUNE_EXP = 93.0


def alibi_slopes(n=16):
    import math
    closest = 2 ** math.floor(math.log2(n))
    base = 2 ** (-2 ** (-(math.log2(closest) - 3)))
    slopes = base ** np.arange(1, closest + 1)
    if closest != n:
        eb = 2 ** (-2 ** (-(math.log2(2 * closest) - 3)))
        extra = eb ** np.arange(1, 2 * (n - closest) + 1, 2)
        slopes = np.concatenate([slopes, extra])
    return slopes.astype(np.float64)



class Prog:
    KEYS = ("t", "v", "s", "g", "S", "G", "A")
    ENG = {"t": "tensor", "v": "vector", "s": "scalar", "g": "gpsimd",
           "S": "sync", "G": "gpsimd", "A": "scalar"}
    DMA = {"S", "G", "A"}

    def __init__(self):
        self.nc = bass.Bass("TRN2", target_bir_lowering=False)
        self.order = {e: [] for e in ("tensor", "vector", "scalar", "gpsimd", "sync")}
        self.cnt = dict.fromkeys(self.KEYS, 0)
        self.engd = dict(self.ENG)
        self.dmaset = set(self.DMA)
        self.waited = {e: dict.fromkeys(self.KEYS, 0) for e in self.order}
        self.floor = {}

    def key(self, queue, name):
        if name in self.cnt:
            return name
        self.engd[name] = self.ENG[queue]
        self.dmaset.add(name)
        self.cnt[name] = 0
        for e in self.waited:
            self.waited[e][name] = 0
        return name

    def barrier(self, keys=("t", "v", "s", "g", "S", "G", "A")):
        for k in keys:
            if self.cnt[k] > self.floor.get(k, 0):
                self.floor[k] = self.cnt[k]

    def op(self, key, fn, waits=(), inc=True):
        eng = self.engd[key]
        w = {}
        for tokn in flat_toks(waits) + list(self.floor.items()):
            if tokn is None:
                continue
            k, v = tokn
            if v > self.waited[eng].get(k, 0) and w.get(k, 0) < v:
                w[k] = v
        for k, v in w.items():
            self.waited[eng][k] = v
        amt = (16 if key in self.dmaset else 1) if inc else 0
        self.cnt[key] += amt
        self.order[eng].append((tuple(w.items()), fn, key, amt))
        return (key, self.cnt[key])

    def finalize(self, es):
        nc = self.nc
        sems = {k: es.enter_context(nc.semaphore("sem_" + k)) for k in self.cnt}
        block = es.enter_context(nc.Block())

        def mk(eng):
            def body(h):
                for waits, fn, key, amt in self.order[eng]:
                    for k, v in waits:
                        h.wait_ge(sems[k], v)
                    inst = fn(h)
                    if amt:
                        inst.then_inc(sems[key], amt)
            return body

        for eng in self.order:
            if self.order[eng]:
                getattr(block, eng)(mk(eng))


def flat_toks(toks):
    out = []
    def add(t):
        if t is None:
            return
        if isinstance(t, tuple) and len(t) == 2 and isinstance(t[0], str):
            out.append(t)
            return
        for x in t:
            add(x)
    add(list(toks))
    return out


def last(*toks):
    return flat_toks(toks)


class Ctx(dict):
    pass


def setup_common(p, es, c, npool=6):
    nc = p.nc
    c["ones_col"] = es.enter_context(nc.sbuf_tensor("cm_ones_col", [128, 1], BF16))
    c["ones_row"] = es.enter_context(nc.sbuf_tensor("cm_ones_row", [1, 128], BF16))
    c["eps1"] = es.enter_context(nc.sbuf_tensor("cm_eps1", [1, 1], F32))
    c["ones_colf"] = es.enter_context(nc.sbuf_tensor("cm_ones_colf", [128, 1], F32))
    c["ps"] = [es.enter_context(nc.psum_tensor(f"cm_ps{i}", [128, 512], F32))
               for i in range(npool)]
    c["ps_free"] = {}
    t0 = p.op("g", lambda g: g.memset(c["ones_col"][:], 1.0))
    t1 = p.op("g", lambda g: g.memset(c["ones_row"][:], 1.0))
    t2 = p.op("g", lambda g: g.memset(c["eps1"][:], EPS))
    t3 = p.op("g", lambda g: g.memset(c["ones_colf"][:], 1.0))
    c["ones_ready"] = [t0, t1, t2, t3]


def load_tile(p, es, name, shape, dtype, src_ap, queue=None):
    t = es.enter_context(p.nc.sbuf_tensor(name, shape, dtype))
    if queue is None:
        queue = "S"
    k = p.key(queue, "setup_" + queue)
    tok = p.op(k, lambda h: h.dma_start(out=t[:], in_=src_ap))
    return t, tok


def setup_bar(p):
    return [(k, p.cnt[k]) for k in ("setup_S", "setup_G", "setup_A") if k in p.cnt]


def all_dma_bar(p):
    return [(k, p.cnt[k]) for k in p.cnt if k in p.dmaset and p.cnt[k] > 0]


def load_x(p, es, c, x_dram, name="x", dtype=F32, nch=NCH, tok=TOK):
    nc = p.nc
    q = "S" if dtype == F32 else "G"
    k = p.key(q, "ldx_" + name)
    tiles = []
    tok_last = None
    for cc in range(nch):
        t = es.enter_context(nc.sbuf_tensor(f"{name}{cc}", [128, tok], dtype))
        tiles.append(t)
        tok_last = p.op(k, lambda h, cc=cc, t=t: h.dma_start(
            out=t[:], in_=x_dram[cc * 128:(cc + 1) * 128, :]))
    return tiles, [tok_last] * nch



def ln_scratch(p, es, name, tok=TOK):
    nc = p.nc
    sc = {}
    for nm, shape, dt in [("xb", [128, 512], BF16), ("xb2", [128, 512], BF16),
                          ("rowA", [1, tok], F32), ("rowB", [1, tok], F32),
                          ("rowC", [1, tok], F32),
                          ("rstdb", [1, tok], BF16), ("mrsb", [1, tok], BF16),
                          ("rbc", [128, tok], BF16), ("mbc", [128, tok], BF16)]:
        sc[nm] = es.enter_context(nc.sbuf_tensor(f"{name}_{nm}", shape, dt))
    sc["busy"] = None
    return sc


def emit_ln(p, es, c, name, x_tiles, x_toks, out_tiles, w_ap=None, b_ap=None,
            tok=TOK, param_toks=(), sc=None, t1_tiles=None, t1_toks=None):
    nc = p.nc
    ntt = tok // 512
    if sc is None:
        sc = ln_scratch(p, es, name, tok)
    if t1_tiles is None:
        t1_tiles = x_tiles
        t1_toks = [None] * NCH
    ps_s = c["ps"][2][0:1, :]
    ps_b = c["ps"][3]
    pf = c["ps_free"]

    s1_toks, s2_toks = [], []
    ev = sc["busy"]
    for tt in range(ntt):
        sl = slice(tt * 512, (tt + 1) * 512)
        m = None
        for cc in range(NCH):
            m = p.op("t", lambda t, cc=cc, sl=sl: t.matmul(
                ps_s, c["ones_colf"][:], x_tiles[cc][:, sl],
                start=(cc == 0), stop=(cc == NCH - 1)),
                waits=last(c["ones_ready"], [x_toks[cc]], [ev], [pf.get(2)]),
                inc=(cc == NCH - 1))
        ev = p.op("v", lambda v, sl=sl: v.tensor_copy(
            sc["rowA"][:, sl], ps_s), waits=last([m], [sc["busy"]]))
        pf[2] = ev
        s1_toks.append(ev)
    sq_done = {}
    for tt in range(ntt):
        sl = slice(tt * 512, (tt + 1) * 512)
        m = None
        for cc in range(NCH):
            xb = sc["xb"] if (tt * NCH + cc) % 2 == 0 else sc["xb2"]
            ct = p.op("v", lambda v, cc=cc, sl=sl, xb=xb: v.tensor_tensor(
                out=xb[:], in0=x_tiles[cc][:, sl], in1=x_tiles[cc][:, sl], op=AL.mult),
                waits=last([x_toks[cc]], [sq_done.get(id(xb))]))
            m = p.op("t", lambda t, xb=xb, cc=cc: t.matmul(
                ps_s, c["ones_col"][:], xb[:],
                start=(cc == 0), stop=(cc == NCH - 1)),
                waits=last(c["ones_ready"], [ct], [ev], [pf.get(2)]), inc=True)
            sq_done[id(xb)] = m
        ev = p.op("v", lambda v, sl=sl: v.tensor_copy(
            sc["rowB"][:, sl], ps_s), waits=last([m]))
        pf[2] = ev
        s2_toks.append(ev)

    t_mu = p.op("v", lambda v: v.tensor_scalar(out=sc["rowA"][:], in0=sc["rowA"][:],
                scalar1=1.0 / D, scalar2=None, op0=AL.mult), waits=s1_toks)
    t_msq = p.op("v", lambda v: v.tensor_tensor(out=sc["rowC"][:], in0=sc["rowA"][:],
                 in1=sc["rowA"][:], op=AL.mult), waits=last([t_mu], [sc["busy"]]))
    t_var = p.op("v", lambda v: v.scalar_tensor_tensor(out=sc["rowB"][:], in0=sc["rowB"][:],
                 scalar=1.0 / D, in1=sc["rowC"][:], op0=AL.mult, op1=AL.subtract),
                 waits=last(s2_toks, [t_msq]))
    t_std = p.op("s", lambda s: s.activation(out=sc["rowC"][:], in_=sc["rowB"][:],
                 func=AF.Sqrt, bias=c["eps1"][:]),
                 waits=last([t_var], c["ones_ready"]))
    t_rstd = p.op("v", lambda v: v.reciprocal(sc["rowB"][:], sc["rowC"][:]), waits=[t_std])
    t_rb = p.op("v", lambda v: v.tensor_copy(sc["rstdb"][:], sc["rowB"][:]), waits=[t_rstd])
    t_mrs = p.op("v", lambda v: v.tensor_tensor(out=sc["rowA"][:], in0=sc["rowA"][:],
                 in1=sc["rowB"][:], op=AL.mult), waits=[t_rstd])
    t_mb = p.op("v", lambda v: v.tensor_copy(sc["mrsb"][:], sc["rowA"][:]), waits=[t_mrs])

    bc_toks = []
    evp = None
    for tt in range(ntt):
        sl = slice(tt * 512, (tt + 1) * 512)
        m = p.op("t", lambda t, sl=sl: t.matmul(ps_b[:], c["ones_row"][:],
                 sc["rstdb"][:, sl], start=True, stop=True),
                 waits=last([t_rb], [evp], c["ones_ready"], [pf.get(3)]))
        evp = p.op("v", lambda v, sl=sl: v.tensor_copy(sc["rbc"][:, sl], ps_b[:]),
                   waits=last([m], [sc["busy"]]))
        m2 = p.op("t", lambda t, sl=sl: t.matmul(ps_b[:], c["ones_row"][:],
                  sc["mrsb"][:, sl], start=True, stop=True),
                  waits=last([t_mb], [evp]))
        evp = p.op("v", lambda v, sl=sl: v.tensor_copy(sc["mbc"][:, sl], ps_b[:]),
                   waits=[m2])
        pf[3] = evp
        bc_toks.append(evp)

    out_toks = []
    for cc in range(NCH):
        tt1 = p.op("v", lambda v, cc=cc: v.tensor_tensor(out=t1_tiles[cc][:],
                   in0=x_tiles[cc][:], in1=sc["rbc"][:], op=AL.mult),
                   waits=last(bc_toks, [x_toks[cc]], [t1_toks[cc]],
                              out_toks[-1:] if out_toks else []))
        if w_ap is not None:
            tt2 = p.op("v", lambda v, cc=cc: v.scalar_tensor_tensor(out=t1_tiles[cc][:],
                       in0=sc["mbc"][:], scalar=-1.0, in1=t1_tiles[cc][:],
                       op0=AL.mult, op1=AL.add), waits=[tt1])
            tt3 = p.op("v", lambda v, cc=cc: v.tensor_scalar(out=out_tiles[cc][:],
                       in0=t1_tiles[cc][:], scalar1=w_ap(cc), scalar2=b_ap(cc),
                       op0=AL.mult, op1=AL.add), waits=last([tt2], list(param_toks)))
            out_toks.append(tt3)
        else:
            tt2 = p.op("v", lambda v, cc=cc: v.scalar_tensor_tensor(out=out_tiles[cc][:],
                       in0=sc["mbc"][:], scalar=-1.0, in1=t1_tiles[cc][:],
                       op0=AL.mult, op1=AL.add), waits=[tt1])
            out_toks.append(tt2)
    sc["busy"] = out_toks[-1]
    return out_toks



def emit_proj_feat(p, es, c, name, xn_tiles, xn_toks, w_dram, n_out, evac,
                   tok=TOK, nch_in=NCH, extra_mm=None, ps_idx=(0, 1)):
    nc = p.nc
    ntt = tok // 512
    noc = n_out // 128
    wbufs = [es.enter_context(nc.sbuf_tensor(f"{name}_w{i}", [128, nch_in, 128], BF16))
             for i in range(2)]
    wk = [p.key("G", f"{name}_wk{i}") for i in range(2)]
    pss = [c["ps"][ps_idx[0]], c["ps"][ps_idx[1]]]
    wlast = [None, None]
    plast = [c["ps_free"].get(ps_idx[0]), c["ps_free"].get(ps_idx[1])]
    wsrc = w_dram.rearrange("(dc pp) o -> pp dc o", pp=128)
    toks = []
    for oc in range(noc):
        slot = oc % 2
        wt = p.op(wk[slot], lambda g, oc=oc, slot=slot: g.dma_start(
            out=wbufs[slot][:], in_=wsrc[:, :, oc * 128:(oc + 1) * 128]),
            waits=last([wlast[slot]]))
        mm_last = None
        for tt in range(ntt):
            psl = (oc * ntt + tt) % 2
            sl = slice(tt * 512, (tt + 1) * 512)
            m = None
            for dc in range(nch_in):
                stop = (dc == nch_in - 1) and extra_mm is None
                m = p.op("t", lambda t, dc=dc, sl=sl, slot=slot, psl=psl, stop=stop:
                         t.matmul(pss[psl][:], wbufs[slot][:, dc, :],
                                  xn_tiles[dc][:, sl], start=(dc == 0), stop=stop),
                         waits=last([wt], [xn_toks[dc]], [plast[psl]]),
                         inc=(dc == nch_in - 1 and extra_mm is None))
            if extra_mm is not None:
                m = extra_mm(p, oc, tt, pss[psl])
            ev = evac(oc, tt, pss[psl], m)
            plast[psl] = ev
            c["ps_free"][ps_idx[psl]] = ev
            toks.append(ev)
            mm_last = m
        wlast[slot] = mm_last
    return toks


def emit_proj_tok(p, es, c, name, xn_tiles, xn_toks, w_dram, n_out, evac,
                  bias_row=None, bias_tok=None, tok=TOK, nch_in=NCH, ps_idx=(4, 5)):
    nc = p.nc
    ntt = tok // 128
    nvc = n_out // 512
    wbufs = [es.enter_context(nc.sbuf_tensor(f"{name}_w{i}", [128, nch_in, 512], BF16))
             for i in range(2)]
    wk = [p.key("G", f"{name}_wk{i}") for i in range(2)]
    pss = [c["ps"][ps_idx[0]], c["ps"][ps_idx[1]]]
    wlast = [None, None]
    plast = [c["ps_free"].get(ps_idx[0]), c["ps_free"].get(ps_idx[1])]
    wsrc = w_dram.rearrange("(dc pp) o -> pp dc o", pp=128)
    toks = []
    for vc in range(nvc):
        slot = vc % 2
        wt = p.op(wk[slot], lambda g, vc=vc, slot=slot: g.dma_start(
            out=wbufs[slot][:], in_=wsrc[:, :, vc * 512:(vc + 1) * 512]),
            waits=last([wlast[slot]]))
        mm_last = None
        for tt in range(ntt):
            psl = (vc * ntt + tt) % 2
            tsl = slice(tt * 128, (tt + 1) * 128)
            m = None
            for dc in range(nch_in):
                is_last = (dc == nch_in - 1) and bias_row is None
                m = p.op("t", lambda t, dc=dc, tsl=tsl, slot=slot, psl=psl,
                         is_last=is_last:
                         t.matmul(pss[psl][:], xn_tiles[dc][:, tsl],
                                  wbufs[slot][:, dc, :],
                                  start=(dc == 0), stop=is_last),
                         waits=last([wt], [xn_toks[dc]], [plast[psl]]),
                         inc=is_last)
            if bias_row is not None:
                m = p.op("t", lambda t, vc=vc, psl=psl: t.matmul(
                    pss[psl][:], c["ones_row"][:],
                    bias_row[:, vc * 512:(vc + 1) * 512], start=False, stop=True),
                    waits=last([bias_tok], c["ones_ready"]))
            ev = evac(tt, vc, pss[psl], m)
            plast[psl] = ev
            c["ps_free"][ps_idx[psl]] = ev
            toks.append(ev)
            mm_last = m
        wlast[slot] = mm_last
    return toks



def emit_rope(p, es, c, name, strip, ready_toks, cosb, sinb, const_toks, aux, aux_free):
    done = []
    rk = p.key("S", "ropek")
    for rb in (0, 64):
        d1 = p.op(rk, lambda h, rb=rb: h.dma_start(
            out=aux[rb:rb + 16, :], in_=strip[rb + 16:rb + 32, :]),
            waits=last(ready_toks, [aux_free]))
        d2 = p.op(rk, lambda h, rb=rb: h.dma_start(
            out=aux[rb + 16:rb + 32, :], in_=strip[rb:rb + 16, :]),
            waits=last(ready_toks, [aux_free]))
        t1 = p.op("v", lambda v, rb=rb: v.tensor_tensor(out=aux[rb:rb + 32, :],
                  in0=aux[rb:rb + 32, :], in1=sinb[rb:rb + 32, :], op=AL.mult),
                  waits=last([d1], [d2], const_toks))
        t2 = p.op("v", lambda v, rb=rb: v.tensor_tensor(out=strip[rb:rb + 32, :],
                  in0=strip[rb:rb + 32, :], in1=cosb[rb:rb + 32, :], op=AL.mult),
                  waits=last(ready_toks, [d1], [d2]))
        t3 = p.op("v", lambda v, rb=rb: v.tensor_tensor(out=strip[rb:rb + 32, :],
                  in0=strip[rb:rb + 32, :], in1=aux[rb:rb + 32, :], op=AL.subtract),
                  waits=[t1, t2])
        aux_free = t3
        done.append(t3)
    return done, aux_free



def emit_qkv_part(p, es, c, name, xn_tiles, xn_toks, wiT, bqk_sb, bqk_tok,
                  bvb, bv_tok, qT, kT, vTok, rope=None, tok=TOK):
    nc = p.nc
    strips = [es.enter_context(nc.sbuf_tensor(f"{name}_qks{i}", [128, tok], BF16))
              for i in range(2)]
    sk = [p.key("S", f"{name}_sk{i}") for i in range(2)]
    strip_free = [None, None]
    state = {"aux_free": None, "per_oc": {}}

    def evac_qk(oc, tt, ps, mm_tok):
        srot = oc % 2
        sl = slice(tt * 512, (tt + 1) * 512)
        scale = 0.125 if oc < 8 else 1.0
        ev = p.op("s", lambda s, oc=oc, sl=sl, ps=ps, srot=srot, scale=scale:
                  s.activation(out=strips[srot][:, sl], in_=ps[:], func=AF.Identity,
                               bias=bqk_sb[:, oc:oc + 1], scale=scale),
                  waits=last([mm_tok], [bqk_tok], [strip_free[srot]]))
        state["per_oc"].setdefault(oc, []).append(ev)
        if tt == tok // 512 - 1:
            evs = state["per_oc"].pop(oc)
            rtoks = evs
            if rope is not None:
                rt, state["aux_free"] = emit_rope(
                    p, es, c, f"{name}_rope{oc}", strips[srot], evs,
                    rope["cosb"], rope["sinb"], rope["const_toks"],
                    rope["aux"], state["aux_free"])
                rtoks = rt + evs
            dst = qT if oc < 8 else kT
            r0 = (oc % 8) * 128
            dtok = p.op(sk[srot], lambda h, dst=dst, r0=r0, srot=srot: h.dma_start(
                out=dst[r0:r0 + 128, :], in_=strips[srot][:]), waits=rtoks)
            strip_free[srot] = dtok
        return ev

    emit_proj_feat(p, es, c, name + "_qk", xn_tiles, xn_toks, wiT[:, 0:2 * D], 2 * D,
                   evac_qk, tok=tok)

    vtiles = [es.enter_context(nc.sbuf_tensor(f"{name}_vt{i}", [128, 512], BF16))
              for i in range(2)]
    vk = [p.key("S", f"{name}_vk{i}") for i in range(2)]
    vfree = [None, None]

    def evac_v(tt, vc, ps, mm_tok):
        rot = (vc * (tok // 128) + tt) % 2
        ev = p.op("s", lambda s, ps=ps, rot=rot: s.activation(
            out=vtiles[rot][:], in_=ps[:], func=AF.Copy),
            waits=last([mm_tok], [vfree[rot]]))
        dtok = p.op(vk[rot], lambda h, tt=tt, vc=vc, rot=rot: h.dma_start(
            out=vTok[tt * 128:(tt + 1) * 128, vc * 512:(vc + 1) * 512],
            in_=vtiles[rot][:]), waits=[ev])
        vfree[rot] = dtok
        return ev

    emit_proj_tok(p, es, c, name + "_v", xn_tiles, xn_toks, wiT[:, 2 * D:3 * D], D,
                  evac_v, bias_row=bvb, bias_tok=bv_tok, tok=tok)


def load_rope(p, es, c, cosT, sinT, tok=TOK):
    cosb, cos_tok = load_tile(p, es, "sb_cosb", [128, tok], BF16, cosT, queue="G")
    sinb, sin_tok = load_tile(p, es, "sb_sinb", [128, tok], BF16, sinT, queue="G")
    aux = es.enter_context(p.nc.sbuf_tensor("ropeaux", [128, tok], BF16))
    return {"cosb": cosb, "sinb": sinb, "aux": aux, "const_toks": [cos_tok, sin_tok]}



def build_qkv(with_ln, with_rope, tok=TOK):
    p = Prog()
    nc = p.nc
    es = ExitStack()
    c = Ctx()
    xT = nc.dram_tensor("xT", [D, tok], F32, kind="ExternalInput").ap()
    wiT = nc.dram_tensor("wiT", [D, 3 * D], F32, kind="ExternalInput").ap()
    bqk = nc.dram_tensor("bqk", [128, 16], F32, kind="ExternalInput").ap()
    bv_row = nc.dram_tensor("bv_row", [1, D], F32, kind="ExternalInput").ap()
    if with_ln:
        lnp = nc.dram_tensor("lnp", [128, 16], F32, kind="ExternalInput").ap()
    if with_rope:
        cosT = nc.dram_tensor("cosT", [128, tok], F32, kind="ExternalInput").ap()
        sinT = nc.dram_tensor("sinT", [128, tok], F32, kind="ExternalInput").ap()
    qT = nc.dram_tensor("qT", [D, tok], BF16, kind="ExternalOutput").ap()
    kT = nc.dram_tensor("kT", [D, tok], BF16, kind="ExternalOutput").ap()
    vTok = nc.dram_tensor("vTok", [tok, D], BF16, kind="ExternalOutput").ap()

    with es:
        setup_common(p, es, c)
        x_tiles, x_toks = load_x(p, es, c, xT, "x", F32, NCH, tok)
        bqk_sb, _bt = load_tile(p, es, "sb_bqk", [128, 16], F32, bqk)
        bvb, _bv = load_tile(p, es, "sb_bvb", [1, D], BF16, bv_row, queue="G")
        rope = load_rope(p, es, c, cosT, sinT, tok) if with_rope else None
        if with_ln:
            lnp_sb, _lt = load_tile(p, es, "sb_lnp", [128, 16], F32, lnp)
        sbar = setup_bar(p)
        bqk_tok = p.op("v", lambda v: v.tensor_scalar(out=bqk_sb[:, 0:8],
                       in0=bqk_sb[:, 0:8], scalar1=0.125, scalar2=None,
                       op0=AL.mult), waits=sbar)
        bv_tok = sbar
        if rope is not None:
            rope["const_toks"] = sbar

        xn_tiles = [es.enter_context(nc.sbuf_tensor(f"xn{i}", [128, tok], BF16))
                    for i in range(NCH)]
        if with_ln:
            xn_toks = emit_ln(p, es, c, "ln1", x_tiles, x_toks, xn_tiles,
                              w_ap=lambda cc: lnp_sb[:, cc:cc + 1],
                              b_ap=lambda cc: lnp_sb[:, 8 + cc:8 + cc + 1],
                              tok=tok, param_toks=sbar)
        else:
            xn_toks = [p.op("s", lambda s, cc=cc: s.activation(
                out=xn_tiles[cc][:], in_=x_tiles[cc][:], func=AF.Copy),
                waits=[x_toks[cc]]) for cc in range(NCH)]

        emit_qkv_part(p, es, c, "qkv", xn_tiles, xn_toks, wiT, bqk_sb, bqk_tok,
                      bvb, bv_tok, qT, kT, vTok, rope=rope, tok=tok)
        p.op("S", lambda h: h.nop(), waits=all_dma_bar(p), inc=False)
        p.finalize(es)
    return p



def build_attn(tok_q=T, nunits=8, slot_slopes=None):
    if slot_slopes is None:
        slot_slopes = [0.0] * nunits

    def kb_min(slope, qt):
        if slope <= 0:
            return 0
        for kb in range(4 * qt + 4):
            if slope * (512 * qt - 128 * kb - 127) < 100.0:
                return kb
        return 4 * qt + 3
    p = Prog()
    nc = p.nc
    es = ExitStack()
    c = Ctx()
    nu = nunits
    QA = nc.dram_tensor("QA", [nu, 65, T], BF16, kind="ExternalInput").ap()
    KA = nc.dram_tensor("KA", [nu, 65, T], BF16, kind="ExternalInput").ap()
    VA = nc.dram_tensor("VA", [nu, T, 65], BF16, kind="ExternalInput").ap()
    AB = nc.dram_tensor("AB", [nu, 128, 32], F32, kind="ExternalInput").ap()
    OT = nc.dram_tensor("OT", [nu, 64, T], BF16, kind="ExternalOutput").ap()

    NQT = T // 512

    with es:
        setup_common(p, es, c, npool=0)
        qa = [es.enter_context(nc.sbuf_tensor(f"qa{i}", [65, T], BF16)) for i in range(2)]
        ka = [es.enter_context(nc.sbuf_tensor(f"ka{i}", [65, T], BF16)) for i in range(2)]
        va = [es.enter_context(nc.sbuf_tensor(f"va{i}", [128, NKB, 65], BF16)) for i in range(2)]
        ab = [es.enter_context(nc.sbuf_tensor(f"ab{i}", [128, 32], F32)) for i in range(2)]
        ot = [es.enter_context(nc.sbuf_tensor(f"ot{i}", [64, T], BF16)) for i in range(2)]
        sq = es.enter_context(nc.sbuf_tensor("sq", [64, 512], BF16))
        nrm = es.enter_context(nc.sbuf_tensor("nrm", [1, 2 * T], F32))
        nb2 = es.enter_context(nc.sbuf_tensor("nb2", [1, 2], F32))
        mgsum = es.enter_context(nc.sbuf_tensor("mgsum", [1, 1], F32))
        mgb = es.enter_context(nc.sbuf_tensor("mgb", [1, 1], BF16))
        biasu = [es.enter_context(nc.sbuf_tensor(f"biasu{i}", [128, 32], F32))
                 for i in range(2)]
        psb = [es.enter_context(nc.sbuf_tensor(f"psb{i}", [128, 512], BF16)) for i in range(3)]
        rcp = es.enter_context(nc.sbuf_tensor("rcp", [1, 512], F32))
        rcpb = es.enter_context(nc.sbuf_tensor("rcpb", [1, 512], BF16))
        rbsb = es.enter_context(nc.sbuf_tensor("rbsb", [64, 512], BF16))

        ps_out = [es.enter_context(nc.psum_tensor(f"pso{i}", [65, 512], F32)) for i in range(2)]
        ps_s = [es.enter_context(nc.psum_tensor(f"pss{i}", [128, 512], F32)) for i in range(3)]
        ps_n = es.enter_context(nc.psum_tensor("psn", [1, 512], F32))
        ps_mg = es.enter_context(nc.psum_tensor("psmg", [128, 1], F32))
        ps_rb = es.enter_context(nc.psum_tensor("psrb", [64, 512], F32))

        in_free = [[None, None] for _ in range(4)]
        ot_free = [None, None]
        psb_free = [None] * 3
        pso_free = [None, None]
        pss_free = [None, None, None]
        bias_free = [None, None]
        last_tbias = None
        last_tmg = None

        otk = [p.key("S", f"otk{i}") for i in range(2)]
        qak = [p.key("S", f"qak{i}") for i in range(2)]
        kak = [p.key("S", f"kak{i}") for i in range(2)]
        vak = [p.key("G", f"vak{i}") for i in range(2)]
        abk = [p.key("G", f"abk{i}") for i in range(2)]
        for u in range(nu):
            rot = u % 2
            tqa = p.op(qak[rot], lambda h, u=u, rot=rot: h.dma_start(
                out=qa[rot][:], in_=QA[u]), waits=last([in_free[0][rot]]))
            tka = p.op(kak[rot], lambda h, u=u, rot=rot: h.dma_start(
                out=ka[rot][:], in_=KA[u]), waits=last([in_free[1][rot]]))
            tva = p.op(vak[rot], lambda g, u=u, rot=rot: g.dma_start(
                out=va[rot][:], in_=VA[u].rearrange("(kb pp) e -> pp kb e", pp=128)),
                waits=last([in_free[2][rot]]))
            tab = p.op(abk[rot], lambda g, u=u, rot=rot: g.dma_start(
                out=ab[rot][:], in_=AB[u]), waits=last([in_free[3][rot]]))

            prev_sqmm = None
            nsegs = T // 512
            for which, src, tsrc in ((0, qa[rot], tqa), (1, ka[rot], tka)):
                ev = None
                for seg in range(nsegs):
                    sl = slice(seg * 512, (seg + 1) * 512)
                    tsq = p.op("v", lambda v, src=src, sl=sl: v.tensor_tensor(
                        out=sq[:], in0=src[0:64, sl], in1=src[0:64, sl], op=AL.mult),
                        waits=last([tsrc], [prev_sqmm]))
                    m = p.op("t", lambda t: t.matmul(ps_n[:], c["ones_col"][0:64, :],
                             sq[:], start=True, stop=True),
                             waits=last([tsq], c["ones_ready"], [ev]))
                    prev_sqmm = m
                    ev = p.op("v", lambda v, which=which, seg=seg: v.tensor_copy(
                        nrm[:, (which * nsegs + seg) * 512:(which * nsegs + seg + 1) * 512],
                        ps_n[:]), waits=[m])
                p.op("v", lambda v, which=which: v.reduce_max(
                    nb2[:, which:which + 1],
                    nrm[:, which * nsegs * 512:(which + 1) * nsegs * 512],
                    axis=AX.X), waits=[ev])
            tadd = p.op("v", lambda v: v.tensor_tensor(out=mgsum[:], in0=nb2[:, 0:1],
                        in1=nb2[:, 1:2], op=AL.mult), waits=[("v", p.cnt["v"])])
            tln = p.op("s", lambda s: s.activation(out=mgsum[:], in_=mgsum[:],
                       func=AF.Ln), waits=[tadd])
            texp = p.op("s", lambda s: s.activation(out=mgsum[:], in_=mgsum[:],
                        func=AF.Exp, scale=0.5), waits=[tln])
            tmgb = p.op("v", lambda v: v.tensor_copy(mgb[:], mgsum[:]),
                        waits=last([texp], [last_tmg]))
            tmg = p.op("t", lambda t: t.matmul(ps_mg[:], c["ones_row"][:], mgb[:],
                       start=True, stop=True),
                       waits=last([tmgb], c["ones_ready"], [last_tbias]))
            last_tmg = tmg
            tbias = p.op("v", lambda v, rot=rot: v.tensor_scalar(out=biasu[rot][:],
                         in0=ab[rot][:], scalar1=ps_mg[:], scalar2=None,
                         op0=AL.subtract),
                         waits=last([tmg], [tab], [bias_free[rot]]))
            last_tbias = tbias
            in_free[3][rot] = tbias

            strip_toks = []
            for qt in range(NQT):
                orot = qt % 2
                qsl = slice(qt * 512, (qt + 1) * 512)
                nkb = 4 * qt + 4
                kb0 = kb_min(slot_slopes[u], qt)
                kbs = list(range(kb0, nkb))
                av_tok = None
                ms_toks = {}

                def emit_ms(kb, qsl=qsl, rot=rot):
                    srot = kb % 3
                    ksl = slice(kb * 128, (kb + 1) * 128)
                    ms_toks[kb] = p.op("t", lambda t, rot=rot, ksl=ksl, qsl=qsl,
                                       srot=srot:
                                       t.matmul(ps_s[srot][:], ka[rot][:, ksl],
                                                qa[rot][:, qsl], start=True, stop=True),
                                       waits=last([tka], [tqa], [pss_free[srot]]))

                emit_ms(kbs[0])
                if len(kbs) > 1:
                    emit_ms(kbs[1])
                for ki, kb in enumerate(kbs):
                    if ki + 2 < len(kbs):
                        emit_ms(kbs[ki + 2])
                    srot = kb % 3
                    prot = kb % 3
                    te = p.op("s", lambda s, srot=srot, prot=prot, kb=kb, rot=rot:
                              s.activation(out=psb[prot][:], in_=ps_s[srot][:],
                                           func=AF.Exp, bias=biasu[rot][:, kb:kb + 1]),
                              waits=last([ms_toks[kb]], [tbias], [psb_free[prot]]))
                    pss_free[srot] = te
                    ptok = te
                    if kb >= 4 * qt:
                        ptok = p.op("g", lambda g, prot=prot, qt=qt, kb=kb:
                                    g.affine_select(out=psb[prot][:], in_=psb[prot][:],
                                                    pattern=[[1, 512]],
                                                    compare_op=AL.is_ge, fill=0.0,
                                                    base=qt * 512 - kb * 128,
                                                    channel_multiplier=-1),
                                    waits=[te])
                    av_tok = p.op("t", lambda t, rot=rot, kb=kb, orot=orot, prot=prot,
                                  kb0=kb0, nkb=nkb: t.matmul(
                                      ps_out[orot][:],
                                      va[rot][:, kb, :],
                                      psb[prot][:], start=(kb == kb0),
                                      stop=(kb == nkb - 1)),
                                  waits=last([ptok], [tva], [pso_free[orot]]),
                                  inc=(kb == nkb - 1))
                    psb_free[prot] = av_tok
                trc = p.op("v", lambda v, orot=orot: v.reciprocal(
                    rcp[:], ps_out[orot][64:65, :]), waits=[av_tok])
                trcb = p.op("v", lambda v: v.tensor_copy(rcpb[:], rcp[:]), waits=[trc])
                tmb = p.op("t", lambda t: t.matmul(ps_rb[:], c["ones_row"][:, 0:64],
                           rcpb[:], start=True, stop=True),
                           waits=last([trcb], c["ones_ready"], [("v", p.cnt["v"])]))
                trb = p.op("v", lambda v: v.tensor_copy(rbsb[:], ps_rb[:]), waits=[tmb])
                tst = p.op("v", lambda v, orot=orot, qsl=qsl, rot=rot:
                           v.scalar_tensor_tensor(out=ot[rot][:, qsl],
                                                  in0=ps_out[orot][0:64, :], scalar=1.0,
                                                  in1=rbsb[:], op0=AL.mult, op1=AL.mult),
                           waits=last([trb], [ot_free[rot]]))
                pso_free[orot] = tst
                strip_toks.append(tst)
            dout = p.op(otk[rot], lambda h, u=u, rot=rot: h.dma_start(
                out=OT[u], in_=ot[rot][:]), waits=strip_toks)
            ot_free[rot] = dout
            in_free[0][rot] = av_tok
            in_free[1][rot] = av_tok
            in_free[2][rot] = av_tok
            bias_free[rot] = av_tok

        p.op("S", lambda h: h.nop(), waits=all_dma_bar(p), inc=False)
        p.finalize(es)
    return p



def emit_ffn(p, es, c, name, xn_tiles, xn_toks, f1T, b1_sb, b1_tok, f2T,
             b2_sb, b2_tok, x_tiles, x_toks, tok=TOK):
    nc = p.nc
    NH = 32
    TTH = 512
    h_sb = es.enter_context(nc.sbuf_tensor(name + "_h", [128, NH, TTH], BF16))
    w1b = [es.enter_context(nc.sbuf_tensor(f"{name}_w1{i}", [128, NCH, 128], BF16))
           for i in range(2)]
    w2b = [es.enter_context(nc.sbuf_tensor(f"{name}_w2{i}", [128, NH, 128], BF16))
           for i in range(2)]
    w1k = [p.key("G", f"{name}_w1k{i}") for i in range(2)]
    w2k = [p.key("G", f"{name}_w2k{i}") for i in range(2)]
    pss = [c["ps"][0], c["ps"][1]]
    f1src = f1T.rearrange("(dc pp) o -> pp dc o", pp=128)
    f2src = f2T.rearrange("(hc pp) o -> pp hc o", pp=128)
    w1_free = [None, None]
    w2_free = [None, None]
    ps_free = [c["ps_free"].get(0), c["ps_free"].get(1)]
    h_free = [None] * NH
    new_x_toks = list(x_toks)
    for tp in range(tok // TTH):
        tsl0 = tp * TTH
        for hc in range(NH):
            ws = hc % 2
            wt = p.op(w1k[ws], lambda g, hc=hc, ws=ws: g.dma_start(
                out=w1b[ws][:], in_=f1src[:, :, hc * 128:(hc + 1) * 128]),
                waits=last([w1_free[ws]]))
            for sub in range(TTH // 512):
                psl = (hc * 2 + sub) % 2
                sl = slice(tsl0 + sub * 512, tsl0 + (sub + 1) * 512)
                m = None
                for dc in range(NCH):
                    m = p.op("t", lambda t, dc=dc, sl=sl, ws=ws, psl=psl:
                             t.matmul(pss[psl][:], w1b[ws][:, dc, :],
                                      xn_tiles[dc][:, sl], start=(dc == 0),
                                      stop=(dc == NCH - 1)),
                             waits=last([wt], [xn_toks[dc]], [ps_free[psl]]),
                             inc=(dc == NCH - 1))
                ev = p.op("s", lambda s, hc=hc, sub=sub, psl=psl: s.activation(
                    out=h_sb[:, hc, sub * 512:(sub + 1) * 512], in_=pss[psl][:],
                    func=AF.Gelu, bias=b1_sb[:, hc:hc + 1]),
                    waits=last([m], [b1_tok], [h_free[hc]]))
                ps_free[psl] = ev
                c["ps_free"][psl] = ev
                if sub == TTH // 512 - 1:
                    w1_free[ws] = m
        ev_h_last = None
        for oc in range(NCH):
            ws = oc % 2
            wt = p.op(w2k[ws], lambda g, oc=oc, ws=ws: g.dma_start(
                out=w2b[ws][:], in_=f2src[:, :, oc * 128:(oc + 1) * 128]),
                waits=last([w2_free[ws]]))
            for sub in range(TTH // 512):
                psl = (oc * 2 + sub) % 2
                sl = slice(tsl0 + sub * 512, tsl0 + (sub + 1) * 512)
                ssub = slice(sub * 512, (sub + 1) * 512)
                m = None
                for hc in range(NH):
                    m = p.op("t", lambda t, hc=hc, sl=sl, ssub=ssub, ws=ws, psl=psl:
                             t.matmul(pss[psl][:], w2b[ws][:, hc, :],
                                      h_sb[:, hc, ssub], start=(hc == 0),
                                      stop=(hc == NH - 1)),
                             waits=last([wt], [("s", p.cnt["s"])] if hc == 0 else [],
                                        [ps_free[psl]]),
                             inc=(hc == NH - 1))
                ev = p.op("v", lambda v, oc=oc, sl=sl, psl=psl:
                          v.scalar_tensor_tensor(out=x_tiles[oc][:, sl],
                                                 in0=pss[psl][:],
                                                 scalar=b2_sb[:, oc:oc + 1],
                                                 in1=x_tiles[oc][:, sl],
                                                 op0=AL.add, op1=AL.add),
                          waits=last([m], [b2_tok], [new_x_toks[oc]]))
                ps_free[psl] = ev
                c["ps_free"][psl] = ev
                new_x_toks[oc] = ev
                ev_h_last = m
            w2_free[ws] = ev_h_last
        for hc in range(NH):
            h_free[hc] = ev_h_last
    return new_x_toks


def build_tok(kind, with_rope=False, tok=TOK):
    p = Prog()
    nc = p.nc
    es = ExitStack()
    c = Ctx()
    xT = nc.dram_tensor("xT", [D, tok], F32, kind="ExternalInput").ap()
    attnT = nc.dram_tensor("attnT", [D, tok], BF16, kind="ExternalInput").ap()
    woT = nc.dram_tensor("woT", [D, D], F32, kind="ExternalInput").ap()
    bo = nc.dram_tensor("bo", [128, 8], F32, kind="ExternalInput").ap()
    if kind in ("inner", "final"):
        ln2p = nc.dram_tensor("ln2p", [128, 16], F32, kind="ExternalInput").ap()
        f1T = nc.dram_tensor("f1T", [D, 4 * D], F32, kind="ExternalInput").ap()
        b1 = nc.dram_tensor("b1", [128, 32], F32, kind="ExternalInput").ap()
        f2T = nc.dram_tensor("f2T", [4 * D, D], F32, kind="ExternalInput").ap()
        b2 = nc.dram_tensor("b2", [128, 8], F32, kind="ExternalInput").ap()
    if kind == "inner":
        ln1p = nc.dram_tensor("ln1p", [128, 16], F32, kind="ExternalInput").ap()
        wiT = nc.dram_tensor("wiT", [D, 3 * D], F32, kind="ExternalInput").ap()
        bqk = nc.dram_tensor("bqk", [128, 16], F32, kind="ExternalInput").ap()
        bv_row = nc.dram_tensor("bv_row", [1, D], F32, kind="ExternalInput").ap()
        qTd = nc.dram_tensor("qT", [D, tok], BF16, kind="ExternalOutput").ap()
        kTd = nc.dram_tensor("kT", [D, tok], BF16, kind="ExternalOutput").ap()
        vTokd = nc.dram_tensor("vTok", [tok, D], BF16, kind="ExternalOutput").ap()
    if kind == "final":
        lnfp = nc.dram_tensor("lnfp", [128, 16], F32, kind="ExternalInput").ap()
        xinT = nc.dram_tensor("xinT", [D, tok], F32, kind="ExternalInput").ap()
    xoT = nc.dram_tensor("xoT", [D, tok], F32, kind="ExternalOutput").ap()

    with es:
        setup_common(p, es, c)
        x_tiles, x_toks = load_x(p, es, c, xT, "x", F32, NCH, tok)
        bo_sb, _ = load_tile(p, es, "sb_bo", [128, 8], F32, bo)
        if kind in ("inner", "final"):
            ln2_sb, _ = load_tile(p, es, "sb_ln2", [128, 16], F32, ln2p)
            b1_sb, _ = load_tile(p, es, "sb_b1", [128, 32], F32, b1)
            b2_sb, _ = load_tile(p, es, "sb_b2", [128, 8], F32, b2)
        if kind == "inner":
            ln1_sb, _ = load_tile(p, es, "sb_ln1", [128, 16], F32, ln1p)
            bqk_sb, _ = load_tile(p, es, "sb_bqk", [128, 16], F32, bqk)
            bvb, _ = load_tile(p, es, "sb_bvb", [1, D], BF16, bv_row, queue="G")
        if kind == "final":
            lnf_sb, _ = load_tile(p, es, "sb_lnf", [128, 16], F32, lnfp)
        sbar = setup_bar(p)
        bo_tok = sbar
        ln2_tok = b1_tok = b2_tok = ln1_tok = lnf_tok = bv_tok = sbar
        if kind == "inner":
            bqk_tok = p.op("v", lambda v: v.tensor_scalar(out=bqk_sb[:, 0:8],
                           in0=bqk_sb[:, 0:8], scalar1=0.125, scalar2=None,
                           op0=AL.mult), waits=sbar)
        t1s = es.enter_context(nc.sbuf_tensor("t1s", [128, tok], F32))
        lnsc = ln_scratch(p, es, "lnsc", tok)

        def evac_resid(oc, tt, ps, mm_tok):
            sl = slice(tt * 512, (tt + 1) * 512)
            ev = p.op("v", lambda v, oc=oc, sl=sl, ps=ps: v.scalar_tensor_tensor(
                out=x_tiles[oc][:, sl], in0=ps[:], scalar=bo_sb[:, oc:oc + 1],
                in1=x_tiles[oc][:, sl], op0=AL.add, op1=AL.add),
                waits=last([mm_tok], [bo_tok], [x_toks[oc]]))
            x_toks[oc] = ev
            return ev

        with ExitStack() as es2:
            at_tiles, at_toks = load_x(p, es2, c, attnT, "at", BF16, NCH, tok)
            emit_proj_feat(p, es2, c, "wo", at_tiles, at_toks, woT, D, evac_resid, tok=tok)
        p.barrier()

        xn_tiles = [es.enter_context(nc.sbuf_tensor(f"xn{i}", [128, tok], BF16))
                    for i in range(NCH)]

        if kind in ("inner", "final"):
            xn_toks = emit_ln(p, es, c, "ln2", x_tiles, x_toks, xn_tiles,
                              w_ap=lambda cc: ln2_sb[:, cc:cc + 1],
                              b_ap=lambda cc: ln2_sb[:, 8 + cc:8 + cc + 1],
                              tok=tok, param_toks=ln2_tok, sc=lnsc,
                              t1_tiles=[t1s] * NCH, t1_toks=[None] * NCH)
            with ExitStack() as es3:
                x_toks = emit_ffn(p, es3, c, "ffn", xn_tiles, xn_toks, f1T, b1_sb,
                                  b1_tok, f2T, b2_sb, b2_tok, x_tiles, x_toks, tok=tok)
            p.barrier()

        if kind == "inner":
            xn_toks = emit_ln(p, es, c, "ln1n", x_tiles, x_toks, xn_tiles,
                              w_ap=lambda cc: ln1_sb[:, cc:cc + 1],
                              b_ap=lambda cc: ln1_sb[:, 8 + cc:8 + cc + 1],
                              tok=tok, param_toks=ln1_tok, sc=lnsc,
                              t1_tiles=[t1s] * NCH, t1_toks=[None] * NCH)
            emit_qkv_part(p, es, c, "qkv", xn_tiles, xn_toks, wiT, bqk_sb, bqk_tok,
                          bvb, bv_tok, qTd, kTd, vTokd, rope=None, tok=tok)
        elif kind == "final":
            y_toks = emit_ln(p, es, c, "lnf", x_tiles, x_toks, xn_tiles,
                             w_ap=lambda cc: lnf_sb[:, cc:cc + 1],
                             b_ap=lambda cc: lnf_sb[:, 8 + cc:8 + cc + 1],
                             tok=tok, param_toks=lnf_tok, sc=lnsc,
                             t1_tiles=[t1s] * NCH, t1_toks=[None] * NCH)
            xik = p.key("S", "xik")
            xin_last = None
            for cc in range(NCH):
                xin_last = p.op(xik, lambda h, cc=cc: h.dma_start(
                    out=x_tiles[cc][:], in_=xinT[cc * 128:(cc + 1) * 128, :]),
                    waits=[y_toks[cc], ("v", p.cnt["v"])])
            xin_toks = [xin_last] * NCH
            add_toks = [p.op("v", lambda v, cc=cc: v.tensor_tensor(
                out=x_tiles[cc][:], in0=x_tiles[cc][:], in1=xn_tiles[cc][:],
                op=AL.add), waits=[xin_toks[cc], y_toks[cc]]) for cc in range(NCH)]
            x_toks = emit_ln(p, es, c, "pln", x_tiles, add_toks, x_tiles,
                             tok=tok, sc=lnsc)
        elif kind == "wd":
            x_toks = emit_ln(p, es, c, "pln", x_tiles, x_toks, x_tiles,
                             tok=tok, sc=lnsc)

        xok = p.key("S", "xok")
        for cc in range(NCH):
            p.op(xok, lambda h, cc=cc: h.dma_start(
                out=xoT[cc * 128:(cc + 1) * 128, :], in_=x_tiles[cc][:]),
                waits=[x_toks[cc]])
        p.op("S", lambda h: h.nop(), waits=all_dma_bar(p), inc=False)
        p.finalize(es)
    return p



def emit_router_coef(p, es, c, name, xb_tiles, xb_toks, rw_sb, rw_tok, gb_sb,
                     gb_tok, mode, coefD, tok=TOK):
    nc = p.nc
    NE = 8
    lg = es.enter_context(nc.sbuf_tensor(name + "_lg", [128, NE], F32))
    l2 = es.enter_context(nc.sbuf_tensor(name + "_l2", [128, NE], F32))
    eq1 = es.enter_context(nc.sbuf_tensor(name + "_eq1", [128, NE], F32))
    eq2 = es.enter_context(nc.sbuf_tensor(name + "_eq2", [128, NE], F32))
    cf = es.enter_context(nc.sbuf_tensor(name + "_cf", [128, NE], F32))
    m1 = es.enter_context(nc.sbuf_tensor(name + "_m1", [128, 1], F32))
    m2 = es.enter_context(nc.sbuf_tensor(name + "_m2", [128, 1], F32))
    w1 = es.enter_context(nc.sbuf_tensor(name + "_w1", [128, 1], F32))
    w2 = es.enter_context(nc.sbuf_tensor(name + "_w2", [128, 1], F32))
    sm = es.enter_context(nc.sbuf_tensor(name + "_sm", [128, 1], F32))
    iot = es.enter_context(nc.sbuf_tensor(name + "_iot", [128, NE], F32))
    cfb = es.enter_context(nc.sbuf_tensor(name + "_cfb", [128, NE], BF16))
    wiRow = None
    wi_toks = []
    if mode == "moe1":
        wiRow = es.enter_context(nc.sbuf_tensor(name + "_wiR", [1, 8 * tok], BF16))
        pst = es.enter_context(nc.psum_tensor(name + "_pst", [1, 128], BF16))
        ident = c["ident"]
    onesf = es.enter_context(nc.sbuf_tensor(name + "_onesf", [1, 128], F32))
    of = p.op("g", lambda g: g.memset(onesf[:], 1.0))
    it = p.op("g", lambda g: g.iota(iot[:], pattern=[[1, NE]], base=0,
              channel_multiplier=0, allow_small_or_imprecise_dtypes=True))
    ps = c["ps"][4][:, 0:NE]
    prev = c["ps_free"].get(4)
    coef_toks = []
    for tt in range(tok // 128):
        tsl = slice(tt * 128, (tt + 1) * 128)
        m = None
        for dc in range(NCH):
            m = p.op("t", lambda t, dc=dc, tsl=tsl: t.matmul(
                ps, xb_tiles[dc][:, tsl], rw_sb[:, dc, :],
                start=(dc == 0), stop=False),
                waits=last([rw_tok], [xb_toks[dc]], [prev]), inc=False)
        m = p.op("t", lambda t: t.matmul(ps, onesf[:], gb_sb[:],
                 start=False, stop=True),
                 waits=last([gb_tok], [of]))
        tl = p.op("v", lambda v: v.tensor_copy(lg[:], ps), waits=last([m], [prev]))
        prev = tl
        c["ps_free"][4] = tl
        ncl = 7 if mode == "moe0" else 8
        a1 = p.op("v", lambda v, ncl=ncl: v.reduce_max(m1[:], lg[:, 0:ncl], axis=AX.X),
                  waits=[tl])
        a2 = p.op("v", lambda v: v.tensor_scalar(out=eq1[:], in0=lg[:], scalar1=m1[:],
                  scalar2=None, op0=AL.is_equal), waits=[a1])
        a3 = p.op("v", lambda v: v.scalar_tensor_tensor(out=l2[:], in0=eq1[:],
                  scalar=-1e30, in1=lg[:], op0=AL.mult, op1=AL.add), waits=[a2])
        a4 = p.op("v", lambda v, ncl=ncl: v.reduce_max(m2[:], l2[:, 0:ncl], axis=AX.X),
                  waits=[a3])
        a5 = p.op("v", lambda v: v.tensor_scalar(out=eq2[:], in0=lg[:], scalar1=m2[:],
                  scalar2=None, op0=AL.is_equal), waits=[a4])
        if mode == "moe0":
            b1t = p.op("v", lambda v: v.tensor_scalar(out=w2[:], in0=m2[:], scalar1=m1[:],
                       scalar2=None, op0=AL.subtract), waits=[a4])
            b2t = p.op("s", lambda s: s.activation(out=w1[:], in_=w2[:], func=AF.Sigmoid,
                       scale=-1.0), waits=[b1t])
            b3t = p.op("v", lambda v: v.tensor_scalar(out=w2[:], in0=w1[:], scalar1=-1.0,
                       scalar2=1.0, op0=AL.mult, op1=AL.add), waits=[b2t])
            b4t = p.op("v", lambda v: v.tensor_scalar(out=cf[:], in0=eq1[:], scalar1=w1[:],
                       scalar2=None, op0=AL.mult),
                       waits=last([b2t], [a2], coef_toks[-1:]))
            b5t = p.op("v", lambda v: v.scalar_tensor_tensor(out=cf[:, 0:7], in0=eq2[:, 0:7],
                       scalar=w2[:], in1=cf[:, 0:7], op0=AL.mult, op1=AL.add),
                       waits=[b3t, b4t, a5])
            b6t = p.op("g", lambda g: g.memset(cf[:, 7:8], 1.0), waits=[b5t])
            done = [b6t, b5t]
        else:
            en = p.op("v", lambda v: v.tensor_scalar(out=w2[:], in0=m1[:], scalar1=-1.0,
                      scalar2=None, op0=AL.mult), waits=[a1])
            e1 = p.op("s", lambda s: s.activation(out=l2[:], in_=lg[:], func=AF.Exp,
                      bias=w2[:]), waits=last([a5], [en], coef_toks[-1:]))
            e2 = p.op("v", lambda v: v.reduce_sum(sm[:], l2[:], axis=AX.X), waits=[e1])
            e3 = p.op("v", lambda v: v.reciprocal(w1[:], sm[:]), waits=[e2])
            e4 = p.op("v", lambda v: v.tensor_scalar(out=cf[:], in0=l2[:], scalar1=w1[:],
                      scalar2=None, op0=AL.mult), waits=[e3])
            s1t = p.op("v", lambda v: v.tensor_tensor(out=eq1[:], in0=eq1[:], in1=eq2[:],
                       op=AL.add), waits=[a5, e4])
            s2t = p.op("v", lambda v: v.tensor_tensor(out=cfb[:], in0=cf[:], in1=eq1[:],
                       op=AL.mult), waits=[s1t])
            s3t = p.op("v", lambda v: v.tensor_tensor(out=l2[:], in0=iot[:], in1=eq1[:],
                       op=AL.mult), waits=[s1t, it])
            s4t = p.op("v", lambda v: v.reduce_max(m2[:], l2[:], axis=AX.X), waits=[s3t])
            s5t = p.op("v", lambda v: v.tensor_scalar(out=cf[:], in0=iot[:], scalar1=m2[:],
                       scalar2=None, op0=AL.is_equal), waits=[s4t])
            wv = None
            for e in range(NE):
                tr = p.op("t", lambda t, e=e: t.transpose(pst[:], cfb[:, e:e + 1],
                          ident[:]), waits=last([s2t], [c["ident_tok"]], wi_toks[-1:]))
                wv = p.op("v", lambda v, e=e, tt=tt: v.tensor_copy(
                    wiRow[0:1, e * tok + tt * 128:e * tok + (tt + 1) * 128], pst[:]),
                    waits=[tr])
                wi_toks.append(wv)
            done = [s5t, wv]
        dt = p.op(p.key("S", "cfk"), lambda h, tt=tt: h.dma_start(
            out=coefD[tt * 128:(tt + 1) * 128, :], in_=cf[:]), waits=done)
        coef_toks.append(dt)
        prev = dt
    return coef_toks, wiRow, (wi_toks[-1] if wi_toks else None)


def build_moe_h(mode, tok=TOK):
    p = Prog()
    nc = p.nc
    es = ExitStack()
    c = Ctx()
    NE = 8
    xT = nc.dram_tensor("xT", [D, tok], F32, kind="ExternalInput").ap()
    rwT = nc.dram_tensor("rwT", [D, NE], F32, kind="ExternalInput").ap()
    gbRow = nc.dram_tensor("gbRow", [1, NE], F32, kind="ExternalInput").ap()
    if mode == "moe0":
        wgu = nc.dram_tensor("wgu", [NE, D, 2 * 4096], F32, kind="ExternalInput").ap()
    else:
        w1T = nc.dram_tensor("w1T", [NE, D, 4096], F32, kind="ExternalInput").ap()
        b1t = nc.dram_tensor("b1t", [128, NE, 32], F32, kind="ExternalInput").ap()
        identD = nc.dram_tensor("identD", [128, 128], F32, kind="ExternalInput").ap()
    hD = nc.dram_tensor("hD", [NE, 4096, tok], BF16, kind="ExternalOutput").ap()
    coefD = nc.dram_tensor("coefD", [tok, NE], F32, kind="ExternalOutput").ap()

    with es:
        setup_common(p, es, c)
        x_tiles, x_toks = load_x(p, es, c, xT, "x", F32, NCH, tok)
        xb_tiles = [es.enter_context(nc.sbuf_tensor(f"xb{i}", [128, tok], BF16))
                    for i in range(NCH)]
        xb_toks = [p.op("s", lambda s, cc=cc: s.activation(
            out=xb_tiles[cc][:], in_=x_tiles[cc][:], func=AF.Copy),
            waits=[x_toks[cc]]) for cc in range(NCH)]
        rw_sb, _ = load_tile(p, es, "sb_rw", [128, NCH, NE], F32,
                             rwT.rearrange("(dc pp) e -> pp dc e", pp=128))
        gb_sb, _ = load_tile(p, es, "sb_gb", [1, NE], F32, gbRow)
        if mode == "moe1":
            c["ident"], _ = load_tile(p, es, "sb_id", [128, 128], BF16,
                                      identD, queue="G")
            b1_sb, _ = load_tile(p, es, "sb_b1t", [128, NE, 32], F32, b1t)
        sbar = setup_bar(p)
        rw_tok = gb_tok = b1_tok = sbar
        c["ident_tok"] = sbar

        coef_toks, wiRow, wi_tok = emit_router_coef(
            p, es, c, "rt", x_tiles, x_toks, rw_sb, rw_tok, gb_sb, gb_tok,
            mode, coefD, tok=tok)

        wb = [es.enter_context(nc.sbuf_tensor(f"wgub{i}", [128, NCH, 128], BF16))
              for i in range(4)]
        wbk = [p.key("G", f"wgubk{i}") for i in range(4)]
        hsk = [p.key("S", f"hsk{i}") for i in range(2)]
        sg = [es.enter_context(nc.sbuf_tensor(f"sg{i}", [128, 512], F32))
              for i in range(2)]
        hstrip = [es.enter_context(nc.sbuf_tensor(f"hstrip{i}", [128, tok], BF16))
                  for i in range(2)]
        cbc = es.enter_context(nc.sbuf_tensor("cbc", [128, tok], BF16))
        wfree = [None] * 4
        sgfree = [None, None]
        hfree = [None, None]
        cbc_free = None
        ntt = tok // 512
        for e in range(NE):
            if mode == "moe1":
                cb_toks = []
                for tt in range(ntt):
                    sl = slice(tt * 512, (tt + 1) * 512)
                    m = p.op("t", lambda t, e=e, sl=sl: t.matmul(
                        c["ps"][5][:], c["ones_row"][:],
                        wiRow[0:1, e * tok + sl.start:e * tok + sl.stop],
                        start=True, stop=True),
                        waits=last([wi_tok], c["ones_ready"], [c["ps_free"].get(5)],
                                   [cbc_free]))
                    ev = p.op("v", lambda v, sl=sl: v.tensor_copy(cbc[:, sl],
                              c["ps"][5][:]), waits=[m])
                    c["ps_free"][5] = ev
                    cb_toks.append(ev)
            for hc in range(32):
                if mode == "moe0":
                    ws, ws2 = (hc % 2), 2 + (hc % 2)
                    wt1 = p.op(wbk[ws], lambda g, e=e, hc=hc, ws=ws: g.dma_start(
                        out=wb[ws][:], in_=wgu[e].rearrange("(dc pp) o -> pp dc o", pp=128)
                        [:, :, hc * 128:(hc + 1) * 128]), waits=last([wfree[ws]]))
                    wt2 = p.op(wbk[ws2], lambda g, e=e, hc=hc, ws2=ws2: g.dma_start(
                        out=wb[ws2][:], in_=wgu[e].rearrange("(dc pp) o -> pp dc o", pp=128)
                        [:, :, 4096 + hc * 128:4096 + (hc + 1) * 128]),
                        waits=last([wfree[ws2]]))
                else:
                    ws = hc % 2
                    wt1 = p.op(wbk[ws], lambda g, e=e, hc=hc, ws=ws: g.dma_start(
                        out=wb[ws][:], in_=w1T[e].rearrange("(dc pp) o -> pp dc o", pp=128)
                        [:, :, hc * 128:(hc + 1) * 128]), waits=last([wfree[ws]]))
                hrot = hc % 2
                for tt in range(ntt):
                    sl = slice(tt * 512, (tt + 1) * 512)
                    pa, pb = tt % 2, 2 + tt % 2
                    mg_ = None
                    for dc in range(NCH):
                        mg_ = p.op("t", lambda t, dc=dc, sl=sl, ws=ws, pa=pa:
                                   t.matmul(c["ps"][pa][:], wb[ws][:, dc, :],
                                            xb_tiles[dc][:, sl], start=(dc == 0),
                                            stop=(dc == NCH - 1)),
                                   waits=last([wt1], [xb_toks[dc]],
                                              [c["ps_free"].get(pa)]),
                                   inc=(dc == NCH - 1))
                    if mode == "moe0":
                        mu_ = None
                        for dc in range(NCH):
                            mu_ = p.op("t", lambda t, dc=dc, sl=sl, ws2=ws2, pb=pb:
                                       t.matmul(c["ps"][pb][:], wb[ws2][:, dc, :],
                                                xb_tiles[dc][:, sl], start=(dc == 0),
                                                stop=(dc == NCH - 1)),
                                       waits=last([wt2], [xb_toks[dc]],
                                                  [c["ps_free"].get(pb)]),
                                       inc=(dc == NCH - 1))
                        srot = tt % 2
                        tsg = p.op("s", lambda s, srot=srot, pa=pa: s.activation(
                            out=sg[srot][:], in_=c["ps"][pa][:], func=AF.Silu),
                            waits=last([mg_], [sgfree[srot]]))
                        c["ps_free"][pa] = tsg
                        th = p.op("v", lambda v, srot=srot, pb=pb, sl=sl, hrot=hrot:
                                  v.tensor_tensor(out=hstrip[hrot][:, sl],
                                                  in0=sg[srot][:], in1=c["ps"][pb][:],
                                                  op=AL.mult),
                                  waits=last([tsg], [mu_], [hfree[hrot]]))
                        c["ps_free"][pb] = th
                        sgfree[srot] = th
                    else:
                        srot = tt % 2
                        tm = p.op("v", lambda v, srot=srot, pa=pa, sl=sl:
                                  v.scalar_tensor_tensor(out=sg[srot][:], in0=cbc[:, sl],
                                                         scalar=1.0, in1=c["ps"][pa][:],
                                                         op0=AL.mult, op1=AL.mult),
                                  waits=last([mg_], cb_toks, [sgfree[srot]]))
                        c["ps_free"][pa] = tm
                        th = p.op("s", lambda s, srot=srot, sl=sl, hrot=hrot, e=e, hc=hc:
                                  s.activation(out=hstrip[hrot][:, sl], in_=sg[srot][:],
                                               func=AF.Relu,
                                               bias=b1_sb[:, e, hc:hc + 1]),
                                  waits=last([tm], [b1_tok], [hfree[hrot]]))
                        sgfree[srot] = th
                    if tt == ntt - 1:
                        wfree[ws] = mg_
                        if mode == "moe0":
                            wfree[ws2] = mu_
                        hd = p.op(hsk[hrot], lambda h, e=e, hc=hc, hrot=hrot: h.dma_start(
                            out=hD[e][hc * 128:(hc + 1) * 128, :], in_=hstrip[hrot][:]),
                            waits=[th])
                        hfree[hrot] = hd
            cbc_free = ("t", p.cnt["t"]) if mode == "moe1" else None
        p.op("S", lambda h: h.nop(), waits=all_dma_bar(p), inc=False)
        p.finalize(es)
    return p


def build_moe_down(mode, tok=TOK):
    p = Prog()
    nc = p.nc
    es = ExitStack()
    c = Ctx()
    NE = 8
    TP = 1024
    xT = nc.dram_tensor("xT", [D, tok], F32, kind="ExternalInput").ap()
    hD = nc.dram_tensor("hD", [NE, 4096, tok], BF16, kind="ExternalInput").ap()
    coefD = nc.dram_tensor("coefD", [tok, NE], F32, kind="ExternalInput").ap()
    wd = nc.dram_tensor("wd", [NE, 4096, D], F32, kind="ExternalInput").ap()
    b2Row = nc.dram_tensor("b2Row", [1, NE * D], F32, kind="ExternalInput").ap()
    identD = nc.dram_tensor("identD", [128, 128], F32, kind="ExternalInput").ap()
    xoT = nc.dram_tensor("xoT", [D, tok], F32, kind="ExternalOutput").ap()

    with es:
        setup_common(p, es, c)
        identf, _ = load_tile(p, es, "sb_idf", [128, 128], F32, identD)
        coef_sb, _ = load_tile(p, es, "sb_coef", [128, tok // 128, NE], F32,
                               coefD.rearrange("(tt pp) e -> pp tt e", pp=128))
        b2_sb, _ = load_tile(p, es, "sb_b2r", [1, NE * D], BF16, b2Row, queue="G")
        sbar = setup_bar(p)
        identf_tok = coef_tok = b2_tok = sbar
        acc = es.enter_context(nc.sbuf_tensor("acc", [128, TP // 128, D], F32))
        hbuf = [es.enter_context(nc.sbuf_tensor(f"hbuf{i}", [128, 16, TP], BF16))
                for i in range(2)]
        wdb = es.enter_context(nc.sbuf_tensor("wdb", [128, 32, 512], BF16))
        xs_tiles = [es.enter_context(nc.sbuf_tensor(f"xs{i}", [128, TP], F32))
                    for i in range(NCH)]
        lnsc = ln_scratch(p, es, "lnsc", TP)
        pstr = es.enter_context(nc.psum_tensor("pstr", [128, 128], F32))

        hfree = [None, None]
        wd_free = [None, None]
        acc_free = {}
        accb_free = {}
        pstr_free = None
        NTS = TP // 128

        for tp in range(tok // TP):
            tpsl = slice(tp * TP, (tp + 1) * TP)
            z_toks = {}
            tbar0 = ("t", p.cnt["t"])
            for ts in range(NTS):
                z_toks[ts] = p.op("g", lambda g, ts=ts: g.memset(acc[:, ts, :], 0.0),
                                  waits=[tbar0])
            for e in range(NE):
                ht = []
                hbk = [p.key("G", f"hbk{i}") for i in range(2)]
                for half in range(2):
                    ht.append(p.op(hbk[half], lambda g, e=e, half=half, tpsl=tpsl:
                                   g.dma_start(out=hbuf[half][:],
                                               in_=hD[e].rearrange("(hc pp) t -> pp hc t", pp=128)
                                               [:, half * 16:(half + 1) * 16, tpsl]),
                                   waits=last([hfree[half]])))
                for dc2 in range(2):
                    dsl = slice(dc2 * 512, (dc2 + 1) * 512)
                    wt = []
                    wdk = [p.key("G", f"wdk{i}") for i in range(2)]
                    for half in range(2):
                        wt.append(p.op(wdk[half], lambda g, e=e, half=half, dsl=dsl:
                                       g.dma_start(out=wdb[:, half * 16:(half + 1) * 16, :],
                                                   in_=wd[e].rearrange("(hc pp) o -> pp hc o", pp=128)
                                                   [:, half * 16:(half + 1) * 16, dsl]),
                                       waits=last([wd_free[half]])))
                    last_mm = None
                    for ts in range(NTS):
                        psl = ts % 4
                        m = None
                        for hc in range(32):
                            m = p.op("t", lambda t, hc=hc, ts=ts, psl=psl:
                                     t.matmul(c["ps"][psl][:],
                                              hbuf[hc // 16][:, hc % 16, ts * 128:(ts + 1) * 128],
                                              wdb[:, hc, :], start=(hc == 0), stop=False),
                                     waits=last([ht[hc // 16]], [wt[hc // 16]],
                                                [c["ps_free"].get(psl)]), inc=False)
                        m = p.op("t", lambda t, e=e, dc2=dc2, psl=psl: t.matmul(
                            c["ps"][psl][:], c["ones_row"][:],
                            b2_sb[:, e * D + dc2 * 512:e * D + (dc2 + 1) * 512],
                            start=False, stop=True),
                            waits=last([b2_tok], c["ones_ready"]))
                        ev = p.op("v", lambda v, e=e, ts=ts, tp=tp, dsl=dsl, psl=psl:
                                  v.scalar_tensor_tensor(
                                      out=acc[:, ts, dsl], in0=c["ps"][psl][:],
                                      scalar=coef_sb[:, tp * NTS + ts, e:e + 1],
                                      in1=acc[:, ts, dsl], op0=AL.mult, op1=AL.add),
                                  waits=last([m], [coef_tok], [z_toks[ts]],
                                             [acc_free.get((tp, ts, dsl.start))]))
                        acc_free[(tp, ts, dsl.start)] = ev
                        c["ps_free"][psl] = ev
                        last_mm = m
                    wd_free = [last_mm, last_mm]
                hfree = [last_mm, last_mm]
            xdk = p.key("S", "xdk")
            xok2 = p.key("S", "xok2")
            x_last = None
            for cc in range(NCH):
                x_last = p.op(xdk, lambda h, cc=cc, tpsl=tpsl: h.dma_start(
                    out=xs_tiles[cc][:], in_=xT[cc * 128:(cc + 1) * 128, tpsl]),
                    waits=last([("xok2", p.cnt["xok2"])] if tp > 0 else []))
            x_toks = [x_last] * NCH
            addt = {}
            for cc in range(NCH):
                for ts in range(NTS):
                    tr = p.op("t", lambda t, cc=cc, ts=ts: t.transpose(
                        pstr[:], acc[:, ts, cc * 128:(cc + 1) * 128], identf[:]),
                        waits=last([acc_free.get((tp, ts, 0))],
                                   [acc_free.get((tp, ts, 512))],
                                   [identf_tok], [pstr_free]))
                    ad = p.op("v", lambda v, cc=cc, ts=ts: v.scalar_tensor_tensor(
                        out=xs_tiles[cc][:, ts * 128:(ts + 1) * 128], in0=pstr[:],
                        scalar=1.0, in1=xs_tiles[cc][:, ts * 128:(ts + 1) * 128],
                        op0=AL.mult, op1=AL.add),
                        waits=last([tr], [x_toks[cc]]))
                    pstr_free = ad
                    addt[cc] = ad
            out_toks = emit_ln(p, es, c, f"pln{tp}", xs_tiles,
                               [addt[cc] for cc in range(NCH)], xs_tiles,
                               tok=TP, sc=lnsc)
            for cc in range(NCH):
                p.op(xok2, lambda h, cc=cc, tpsl=tpsl: h.dma_start(
                    out=xoT[cc * 128:(cc + 1) * 128, tpsl], in_=xs_tiles[cc][:]),
                    waits=[out_toks[cc]])
        p.op("S", lambda h: h.nop(), waits=all_dma_bar(p), inc=False)
        p.finalize(es)
    return p



def _tile8(v, n):
    return np.ascontiguousarray(v.reshape(n, 128).T).astype(np.float32)


def _lnp(w, b):
    return np.concatenate([_tile8(w, 8), _tile8(b, 8)], axis=1)


_PROG_CACHE = {}


def _prog(key, builder, *args, **kw):
    if key not in _PROG_CACHE:
        _PROG_CACHE[key] = builder(*args, **kw)
    return _PROG_CACHE[key]


def _run(p, in_maps):
    res = run_bass_kernel_spmd(p.nc, in_maps, core_ids=list(range(NCORES)))
    return res.results


def _attn_assemble(qTs, kTs, vTs, qpos_rows, ab_rows):
    QA = np.zeros((NCORES, 8, 65, T), dtype=BF)
    KA = np.zeros((NCORES, 8, 65, T), dtype=BF)
    VA = np.zeros((NCORES, 8, T, 65), dtype=BF)
    ABc = np.zeros((NCORES, 8, 128, 32), dtype=np.float32)
    for cc in range(NCORES):
      for j in range(8):
        b, h = cc % 4, 2 * j + cc // 4
        q = np.concatenate([np.asarray(qTs[2 * b])[64 * h:64 * h + 64],
                            np.asarray(qTs[2 * b + 1])[64 * h:64 * h + 64]], axis=1)
        k = np.concatenate([np.asarray(kTs[2 * b])[64 * h:64 * h + 64],
                            np.asarray(kTs[2 * b + 1])[64 * h:64 * h + 64]], axis=1)
        v = np.concatenate([np.asarray(vTs[2 * b])[:, 64 * h:64 * h + 64],
                            np.asarray(vTs[2 * b + 1])[:, 64 * h:64 * h + 64]], axis=0)
        QA[cc, j, 0:64] = q
        QA[cc, j, 64] = qpos_rows[h]
        KA[cc, j, 0:64] = k
        KA[cc, j, 64] = 1.0
        VA[cc, j, :, 0:64] = v
        VA[cc, j, :, 64] = 1.0
        ABc[cc, j] = ab_rows[h]
    return QA, KA, VA, ABc


def _attn_out_slices(OTs):
    full = np.zeros((B, D, T), dtype=BF)
    for cc in range(NCORES):
      for j in range(8):
        b, h = cc % 4, 2 * j + cc // 4
        full[b, 64 * h:64 * h + 64] = np.asarray(OTs[cc])[j]
    out = []
    for cc in range(NCORES):
        b, half = cc // 2, cc % 2
        out.append(np.ascontiguousarray(full[b][:, half * TOK:(half + 1) * TOK]))
    return out


_DBG_HOOK = None


def _dbg(name, xs):
    if _DBG_HOOK is not None:
        _DBG_HOOK(name, xs)


def kernel(**inp):
    f32 = np.float32
    x0 = np.asarray(inp["L0_attn_x"], dtype=f32)
    xT = np.ascontiguousarray(x0.reshape(B * T, D).T)
    xs = [np.ascontiguousarray(xT[:, c * TOK:(c + 1) * TOK]) for c in range(NCORES)]
    x_in = [a.copy() for a in xs]

    slopes = alibi_slopes(H)
    pos = np.arange(T, dtype=np.float64)
    qpos_rows = [(-slopes[h] * (pos + 64.0)).astype(BF) for h in range(H)]
    p_idx = np.arange(128, dtype=np.float64)[:, None]
    kb_idx = np.arange(32, dtype=np.float64)[None, :]
    ab_rows = [(slopes[h] * (128 * kb_idx + p_idx)).astype(f32) for h in range(H)]
    ab_zero = [np.zeros((128, 32), f32)] * H
    qpos_zero = [np.zeros(T, BF)] * H
    ident = np.eye(128, dtype=f32)
    invf = 10000.0 ** (-np.arange(0, 64, 2) / 64.0)
    cosf = np.cos(pos[None, :] * invf[:, None]).astype(f32)
    sinf = np.sin(pos[None, :] * invf[:, None]).astype(f32)
    cosT = np.zeros((128, T), f32)
    sinT = np.zeros((128, T), f32)
    cosT[0:32] = cosf; cosT[64:96] = cosf
    sinT[0:32] = sinf; sinT[64:96] = sinf

    def seqpos_slice(arr2d, cc):
        half = cc % 2
        return np.ascontiguousarray(arr2d[:, half * TOK:(half + 1) * TOK])

    ipw = np.asarray(inp["L0_attn_in_proj_w"], f32)
    ipb = np.asarray(inp["L0_attn_in_proj_b"], f32)
    opw = np.asarray(inp["L0_attn_out_proj_w"], f32)
    opb = np.asarray(inp["L0_attn_out_proj_b"], f32)
    f1w = np.asarray(inp["L0_attn_ffn1_w"], f32)
    f1b = np.asarray(inp["L0_attn_ffn1_b"], f32)
    f2w = np.asarray(inp["L0_attn_ffn2_w"], f32)
    f2b = np.asarray(inp["L0_attn_ffn2_b"], f32)
    ln1w = np.asarray(inp["L0_attn_ln1_w"], f32)
    ln1b = np.asarray(inp["L0_attn_ln1_b"], f32)
    ln2w = np.asarray(inp["L0_attn_ln2_w"], f32)
    ln2b = np.asarray(inp["L0_attn_ln2_b"], f32)

    def l0_qkv_inputs(i):
        wiT = np.ascontiguousarray(ipw[i].T)
        bq = ipb[i]
        return {"wiT": wiT,
                "bqk": np.concatenate([_tile8(bq[:D], 8), _tile8(bq[D:2 * D], 8)], 1),
                "bv_row": np.ascontiguousarray(bq[2 * D:].reshape(1, D)),
                "lnp": _lnp(ln1w[i], ln1b[i])}

    p_qkv = _prog("qkv_ln", build_qkv, True, False)
    slot_slopes0 = tuple(float(slopes[min(2 * j + 1, 15)]) for j in range(8))
    p_attn = _prog("attn_l0", build_attn, T, 8, slot_slopes0)
    p_inner = _prog("tok_inner", build_tok, "inner")
    p_final = _prog("tok_final", build_tok, "final")

    l0q = l0_qkv_inputs(0)
    r = _run(p_qkv, [dict(xT=xs[c], **l0q) for c in range(NCORES)])
    qTs = [r[c]["qT"] for c in range(NCORES)]
    kTs = [r[c]["kT"] for c in range(NCORES)]
    vTs = [r[c]["vTok"] for c in range(NCORES)]

    for i in range(4):
        QA, KA, VA, ABc = _attn_assemble(qTs, kTs, vTs, qpos_rows, ab_rows)
        r = _run(p_attn, [{"QA": QA[c], "KA": KA[c], "VA": VA[c], "AB": ABc[c]}
                          for c in range(NCORES)])
        at = _attn_out_slices([r[c]["OT"] for c in range(NCORES)])
        common = {"woT": np.ascontiguousarray(opw[i].T), "bo": _tile8(opb[i], 8),
                  "ln2p": _lnp(ln2w[i], ln2b[i]),
                  "f1T": np.ascontiguousarray(f1w[i].T), "b1": _tile8(f1b[i], 32),
                  "f2T": np.ascontiguousarray(f2w[i].T), "b2": _tile8(f2b[i], 8)}
        if i < 3:
            nq = l0_qkv_inputs(i + 1)
            r = _run(p_inner, [dict(xT=xs[c], attnT=at[c], **common, ln1p=nq["lnp"],
                                    wiT=nq["wiT"], bqk=nq["bqk"], bv_row=nq["bv_row"])
                               for c in range(NCORES)])
            xs = [np.asarray(r[c]["xoT"], f32) for c in range(NCORES)]
            _dbg(f"l0_{i}", xs)
            qTs = [r[c]["qT"] for c in range(NCORES)]
            kTs = [r[c]["kT"] for c in range(NCORES)]
            vTs = [r[c]["vTok"] for c in range(NCORES)]
        else:
            fin = {"lnfp": _lnp(np.asarray(inp["L0_attn_final_w"], f32),
                                np.asarray(inp["L0_attn_final_b"], f32))}
            r = _run(p_final, [dict(xT=xs[c], attnT=at[c], **common, **fin,
                                    xinT=x_in[c]) for c in range(NCORES)])
            xs = [np.asarray(r[c]["xoT"], f32) for c in range(NCORES)]
            _dbg("xb0", xs)

    rw0 = np.asarray(inp["L0_ffn_router_w"], f32)
    rwT0 = np.concatenate([rw0.T, np.zeros((D, 1), f32)], 1)
    wg8 = np.concatenate([np.asarray(inp["L0_ffn_w_gate"], f32),
                          np.asarray(inp["L0_ffn_sw_gate"], f32)], 0)
    wu8 = np.concatenate([np.asarray(inp["L0_ffn_w_up"], f32),
                          np.asarray(inp["L0_ffn_sw_up"], f32)], 0)
    wgu = np.ascontiguousarray(np.concatenate([wg8, wu8], axis=2))
    wd8 = np.ascontiguousarray(np.concatenate([np.asarray(inp["L0_ffn_w_down"], f32),
                                               np.asarray(inp["L0_ffn_sw_down"], f32)], 0))
    gb0 = np.zeros((1, 8), f32)
    b2z = np.zeros((1, 8 * D), f32)
    p_h0 = _prog("moe0_h", build_moe_h, "moe0")
    p_d0 = _prog("moe0_d", build_moe_down, "moe0")
    r = _run(p_h0, [{"xT": xs[c], "rwT": rwT0, "gbRow": gb0, "wgu": wgu}
                    for c in range(NCORES)])
    r = _run(p_d0, [{"xT": xs[c], "hD": np.asarray(r[c]["hD"]),
                     "coefD": np.asarray(r[c]["coefD"], f32), "wd": wd8,
                     "b2Row": b2z, "identD": ident} for c in range(NCORES)])
    xs = [np.asarray(r[c]["xoT"], f32) for c in range(NCORES)]
    _dbg("xb1", xs)

    wiT1 = np.ascontiguousarray(np.concatenate(
        [np.asarray(inp["L1_attn_Wq"], f32), np.asarray(inp["L1_attn_Wk"], f32),
         np.asarray(inp["L1_attn_Wv"], f32)], axis=1))
    bq1 = np.concatenate([np.asarray(inp["L1_attn_bq"], f32),
                          np.asarray(inp["L1_attn_bk"], f32)])
    bv1 = np.asarray(inp["L1_attn_bv"], f32)
    p_qkv1 = _prog("qkv_rope", build_qkv, False, True)
    r = _run(p_qkv1, [{"xT": xs[c], "wiT": wiT1,
                       "bqk": np.concatenate([_tile8(bq1[:D], 8), _tile8(bq1[D:], 8)], 1),
                       "bv_row": bv1.reshape(1, D),
                       "cosT": seqpos_slice(cosT, c), "sinT": seqpos_slice(sinT, c)}
                      for c in range(NCORES)])
    QA, KA, VA, ABc = _attn_assemble([r[c]["qT"] for c in range(NCORES)],
                                     [r[c]["kT"] for c in range(NCORES)],
                                     [r[c]["vTok"] for c in range(NCORES)],
                                     qpos_zero, ab_zero)
    p_attn1 = _prog("attn_l1", build_attn)
    r = _run(p_attn1, [{"QA": QA[c], "KA": KA[c], "VA": VA[c], "AB": ABc[c]}
                       for c in range(NCORES)])
    at = _attn_out_slices([r[c]["OT"] for c in range(NCORES)])
    p_wd = _prog("tok_wd", build_tok, "wd")
    r = _run(p_wd, [{"xT": xs[c], "attnT": at[c],
                     "woT": np.asarray(inp["L1_attn_Wd"], f32),
                     "bo": _tile8(np.asarray(inp["L1_attn_bd"], f32), 8)}
                    for c in range(NCORES)])
    xs = [np.asarray(r[c]["xoT"], f32) for c in range(NCORES)]
    _dbg("xb2", xs)

    rwT1 = np.ascontiguousarray(np.asarray(inp["L1_ffn_gate_w"], f32).T)
    gb1 = np.asarray(inp["L1_ffn_gate_b"], f32).reshape(1, 8)
    w1T = np.ascontiguousarray(np.asarray(inp["L1_ffn_w1"], f32).transpose(0, 2, 1))
    b1m = np.asarray(inp["L1_ffn_b1"], f32)
    b1t = np.ascontiguousarray(b1m.reshape(8, 32, 128).transpose(2, 0, 1))
    w2T = np.ascontiguousarray(np.asarray(inp["L1_ffn_w2"], f32).transpose(0, 2, 1))
    b2r = np.ascontiguousarray(np.asarray(inp["L1_ffn_b2"], f32).reshape(1, 8 * D))
    p_h1 = _prog("moe1_h", build_moe_h, "moe1")
    p_d1 = _prog("moe1_d", build_moe_down, "moe1")
    r = _run(p_h1, [{"xT": xs[c], "rwT": rwT1, "gbRow": gb1, "w1T": w1T,
                     "b1t": b1t, "identD": ident} for c in range(NCORES)])
    r = _run(p_d1, [{"xT": xs[c], "hD": np.asarray(r[c]["hD"]),
                     "coefD": np.asarray(r[c]["coefD"], f32), "wd": w2T,
                     "b2Row": b2r, "identD": ident} for c in range(NCORES)])
    xs = [np.asarray(r[c]["xoT"], f32) for c in range(NCORES)]

    out = np.concatenate(xs, axis=1)
    return np.ascontiguousarray(out.T).reshape(B, T, D).astype(np.float32)
